# version 03
# baseline (speedup 1.0000x reference)
import math
import numpy as np
import ml_dtypes
from contextlib import ExitStack

import concourse.bass as bass
import concourse.mybir as mybir
from concourse.bass_utils import run_bass_kernel_spmd

F32 = mybir.dt.float32
BF16 = mybir.dt.bfloat16
I32 = mybir.dt.int32
ALU = mybir.AluOpType
ACT = mybir.ActivationFunctionType
AX = mybir.AxisListType.X

NCORE = 8
B, SEQ, D = 16, 2048, 1024
TOK = 2 * SEQ
NCK = D // 128
ST = 512
NST = TOK // ST
H0 = 4 * D
NH0 = H0 // 128
EPS = 1e-5
INV_D = 1.0 / D

KSTEP = 128
MC = KSTEP * B
NCHUNK = SEQ // KSTEP
SUBS = 8
SBR = SUBS * B
NSUB = KSTEP // SUBS

PH_L0FA = True
PH_SCAN = True
PH_L0FB = True
PH_L1FA = True
PH_ATTN = True
PH_L1FB = True

ENGS = ("te", "ve", "se", "ge", "sy")
ENG_ATTR = {"te": "tensor", "ve": "vector", "se": "scalar", "ge": "gpsimd", "sy": "sync"}


class Bld:

    def __init__(self, nc):
        self.nc = nc
        self.ops = {e: [] for e in ENGS}
        self.count = {e: 0 for e in ENGS}
        self.credits = {e: {} for e in ENGS}
        self.tickets = {}
        self.readers = {}
        self.sem_names = ["E_" + e for e in ENGS] + ["E_cc"]
        self.dma = {}
        self.cc_count = 0

    def add_dma_stream(self, name, eng, ring):
        sems = [f"D_{name}_{i}" for i in range(ring)]
        self.sem_names += sems
        self.dma[name] = dict(eng=eng, ring=sems, idx=0, cnt=[0] * ring)

    def _wait(self, eng, sem, val):
        if self.credits[eng].get(sem, 0) >= val:
            return
        self.credits[eng][sem] = val
        self.ops[eng].append(("w", sem, val))

    def _dep(self, eng, reads, writes):
        for k in reads:
            t = self.tickets.get(k)
            if t is not None:
                self._wait(eng, t[0], t[1])
        for k in writes:
            t = self.tickets.get(k)
            if t is not None:
                self._wait(eng, t[0], t[1])
            for sn, v in self.readers.get(k, {}).items():
                self._wait(eng, sn, v)

    def op(self, eng, emit, reads=(), writes=()):
        self._dep(eng, reads, writes)
        self.count[eng] += 1
        n = self.count[eng]
        sem = "E_" + eng
        self.ops[eng].append(("o", emit, sem))
        for k in reads:
            self.readers.setdefault(k, {})[sem] = n
        for k in writes:
            self.tickets[k] = (sem, n)
            self.readers[k] = {}
        return n

    def dmaop(self, stream, emit, reads=(), writes=()):
        st = self.dma[stream]
        eng = st["eng"]
        slot = st["idx"] % len(st["ring"])
        st["idx"] += 1
        semn = st["ring"][slot]
        st["cnt"][slot] += 16
        target = st["cnt"][slot]
        self._dep(eng, reads, writes)
        if target > 16:
            self._wait(eng, semn, target - 16)
        self.ops[eng].append(("d", emit, semn))
        for k in reads:
            self.readers.setdefault(k, {})[semn] = target
        for k in writes:
            self.tickets[k] = (semn, target)
            self.readers[k] = {}
        return semn, target

    def ccop(self, emit, reads=(), writes=()):
        self._dep("ge", reads, writes)
        self.cc_count += 1
        n = self.cc_count
        self.ops["ge"].append(("o", emit, "E_cc"))
        for k in reads:
            self.readers.setdefault(k, {})["E_cc"] = n
        for k in writes:
            self.tickets[k] = ("E_cc", n)
            self.readers[k] = {}
        return n

    def global_barrier(self):
        targets = [("E_" + e, self.count[e]) for e in ENGS if self.count[e] > 0]
        targets.append(("E_cc", self.cc_count))
        for st in self.dma.values():
            for i, sn in enumerate(st["ring"]):
                if st["cnt"][i] > 0:
                    targets.append((sn, st["cnt"][i]))
        for e in ENGS:
            for sn, v in targets:
                if v > 0:
                    self._wait(e, sn, v)

    def raw_wait(self, eng, key):
        t = self.tickets[key]
        self._wait(eng, t[0], t[1])

    def emit_all(self, block, sems):
        for e in ENGS:
            oplist = self.ops[e]

            def make_body(oplist=oplist):
                def body(E):
                    for item in oplist:
                        if item[0] == "w":
                            E.wait_ge(sems[item[1]], item[2])
                        else:
                            item[1](E, sems[item[2]])
                return body

            getattr(block, ENG_ATTR[e])(make_body())


def build_program():
    nc = bass.Bass("TRN2", target_bir_lowering=False, debug=False)

    def dram_in(name, shape, dt=F32):
        return nc.dram_tensor(name, list(shape), dt, kind="ExternalInput").ap()

    xT = dram_in("xT", [D, TOK])
    attn_wT = dram_in("attn_wT", [D, D])
    next_wT = dram_in("next_wT", [D, D])
    w1e = dram_in("w1e", [D, H0])
    w2e = dram_in("w2e", [H0, D])
    ln_s = dram_in("ln_s", [128, NCK])
    ln_b = dram_in("ln_b", [128, NCK])
    attn_b = dram_in("attn_b", [128, NCK])
    next_b = dram_in("next_b", [128, NCK])
    b1e = dram_in("b1e", [128, NH0])
    b2e = dram_in("b2e", [128, NCK])
    wkv_vT = dram_in("wkv_vT", [D, D])
    bkv_row = dram_in("bkv_row", [1, D])
    z0 = dram_in("z0", [D, 128])
    eta_in = dram_in("eta_in", [1, 1])
    qkv_k = dram_in("qkv_k", [D, 3 * D])
    qkv_b = dram_in("qkv_b", [128, 24])
    W1T = dram_in("W1T", [D, H0])
    W2T = dram_in("W2T", [H0, D])
    l1b1 = dram_in("l1b1", [128, NH0])
    l1b2 = dram_in("l1b2", [128, NCK])
    maskT = dram_in("maskT", [D, TOK], BF16)
    ones_col = dram_in("ones_col", [128, 1])
    ones_colb = dram_in("ones_colb", [128, 1], BF16)
    ones_row = dram_in("ones_row", [1, 128])
    ones_rowb = dram_in("ones_rowb", [1, 128], BF16)
    identb = dram_in("identb", [128, 128], BF16)
    msl = dram_in("msl", [128, 128], BF16)
    msu = dram_in("msu", [128, 128], BF16)
    trimask = dram_in("trimask", [128, 128], BF16)

    out_T = nc.dram_tensor("outT", [D, TOK], F32, kind="ExternalOutput").ap()

    d_x1in = nc.dram_tensor("d_x1in", [NCORE * D, 512], BF16).ap()
    d_x1out = nc.dram_tensor("d_x1out", [NCORE * D, 512], BF16).ap()
    PUB1 = D + MC
    d_pub1 = nc.dram_tensor("d_pub1", [2 * PUB1, MC], BF16).ap()
    d_agp1 = nc.dram_tensor("d_agp1", [NCORE * 2 * PUB1, MC], BF16).ap()
    d_p2in = nc.dram_tensor("d_p2in", [NCORE * 2 * 2 * MC, 128], BF16).ap()
    d_p2out = nc.dram_tensor("d_p2out", [NCORE * 2 * 2 * MC, 128], BF16).ap()
    d_y = nc.dram_tensor("d_y", [128, B * SEQ], BF16).ap()
    d_y3in = nc.dram_tensor("d_y3in", [NCORE * 128, TOK], BF16).ap()
    d_y3out = nc.dram_tensor("d_y3out", [NCORE * 128, TOK], BF16).ap()
    d_rsp = nc.dram_tensor("d_rsp", [D, TOK], F32).ap()
    d_xq = nc.dram_tensor("d_xq", [D, TOK], BF16).ap()
    d_w1q = nc.dram_tensor("d_w1q", [D, H0], BF16).ap()
    d_w2q = nc.dram_tensor("d_w2q", [H0, D], BF16).ap()

    b = Bld(nc)
    b.add_dma_stream("io", "sy", 4)
    b.add_dma_stream("wt", "se", 4)
    b.add_dma_stream("gw", "ge", 6)
    b.add_dma_stream("tp", "sy", 4)

    with ExitStack() as ctx:
        sems = {sn: ctx.enter_context(nc.semaphore(sn)) for sn in b.sem_names}
        sbuf = lambda name, shape, dt: ctx.enter_context(nc.sbuf_tensor(name, shape, dt))
        pst = lambda name, shape: ctx.enter_context(nc.psum_tensor(name, shape, F32))

        c_ones = sbuf("c_ones", [128, 1], F32)
        c_onesb = sbuf("c_onesb", [128, 1], BF16)
        c_ones_row = sbuf("c_ones_row", [1, 128], F32)
        c_ones_rowb = sbuf("c_ones_rowb", [1, 128], BF16)
        c_ident = sbuf("c_ident", [128, 128], BF16)
        c_msl = sbuf("c_msl", [128, 128], BF16)
        c_msu = sbuf("c_msu", [128, 128], BF16)
        c_tri = sbuf("c_tri", [128, 128], BF16)
        s_lns = sbuf("s_lns", [128, NCK], F32)
        s_lnb = sbuf("s_lnb", [128, NCK], F32)
        s_attnb = sbuf("s_attnb", [128, NCK], F32)
        s_nextb = sbuf("s_nextb", [128, NCK], F32)
        s_b1 = sbuf("s_b1", [128, NH0], F32)
        s_b2 = sbuf("s_b2", [128, NCK], F32)
        s_qkvb = sbuf("s_qkvb", [128, 24], F32)
        s_l1b1 = sbuf("s_l1b1", [128, NH0], F32)
        s_l1b2 = sbuf("s_l1b2", [128, NCK], F32)
        s_eta = sbuf("s_eta", [1, 1], F32)
        c_negc = sbuf("c_negc", [128, 1], F32)
        s_l1b1q = sbuf("s_l1b1q", [128, NH0], F32)
        s_l1b2q = sbuf("s_l1b2q", [128, NCK], F32)

        r_mu = sbuf("r_mu", [1, ST], F32)
        r_var = sbuf("r_var", [1, ST], F32)
        r_rstd = sbuf("r_rstd", [1, ST], F32)
        r_nmur = sbuf("r_nmur", [1, ST], F32)

        ps_m = [pst(f"ps_m{i}", [128, 512]) for i in range(2)]
        ps_h = [pst(f"ps_h{i}", [128, 512]) for i in range(2)]
        ps_sum = pst("ps_sum", [1, ST])
        ps_sq = pst("ps_sq", [1, ST])
        ps_bc = [pst(f"ps_bc{i}", [128, ST]) for i in range(2)]

        def ld(stream, dst, src, key):
            b.dmaop(stream, lambda E, s, d=dst, so=src: E.dma_start(out=d, in_=so).then_inc(s, 16),
                    reads=(), writes=(key,))

        ld("io", c_ones[:], ones_col, "ones")
        ld("io", c_onesb[:], ones_colb, "onesb")
        ld("io", c_ones_row[:], ones_row, "ones_row")
        ld("io", c_ones_rowb[:], ones_rowb, "ones_rowb")
        ld("io", c_ident[:], identb, "ident")
        ld("io", c_msl[:], msl, "msl")
        ld("io", c_msu[:], msu, "msu")
        ld("io", c_tri[:], trimask, "tri")
        ld("io", s_lns[:], ln_s, "lns")
        ld("io", s_lnb[:], ln_b, "lnb")
        ld("io", s_attnb[:], attn_b, "attnb")
        ld("io", s_nextb[:], next_b, "nextb")
        ld("io", s_b1[:], b1e, "b1")
        ld("io", s_b2[:], b2e, "b2")
        ld("io", s_qkvb[:], qkv_b, "qkvb")
        ld("io", s_l1b1[:], l1b1, "l1b1")
        ld("io", s_l1b2[:], l1b2, "l1b2")
        ld("io", s_eta[:], eta_in, "eta")

        b.op("te", lambda E, s: E.matmul(ps_bc[0][:, 0:1], c_ones_row[:], s_eta[:],
                                         start=True, stop=True).then_inc(s, 1),
             reads=("ones_row", "eta"), writes=("bc_r",))
        b.op("ve", lambda E, s: E.tensor_scalar(c_negc[:], ps_bc[0][:, 0:1], -2.0 / (B * D), None,
                                                ALU.mult).then_inc(s, 1),
             reads=("bc_r",), writes=("negc",))

        sq_slots = [sbuf(f"sqs{i}", [128, ST], F32) for i in range(1)]

        def emit_ln(tag, w, src_key, src_ap, src_bf16):
            ones_ap = c_onesb[:] if src_bf16 else c_ones[:]
            ones_key = "onesb" if src_bf16 else "ones"
            for c in range(NCK):
                b.op("te", lambda E, s, c=c: E.matmul(
                        ps_sum[0:1, 0:w], ones_ap, src_ap(c),
                        start=(c == 0), stop=(c == NCK - 1)).then_inc(s, 1),
                     reads=(src_key(c), ones_key), writes=("ps_sum",))
                sl = sq_slots[0]
                b.op("se", lambda E, s, c=c, sl=sl: E.activation(
                        out=sl[:, 0:w], in_=src_ap(c), func=ACT.Square).then_inc(s, 1),
                     reads=(src_key(c),), writes=(("sqslot", 0),))
                b.op("te", lambda E, s, c=c, sl=sl: E.matmul(
                        ps_sq[0:1, 0:w], c_ones[:], sl[:, 0:w],
                        start=(c == 0), stop=(c == NCK - 1)).then_inc(s, 1),
                     reads=(("sqslot", 0), "ones"), writes=("ps_sq",))
            b.op("ve", lambda E, s: E.tensor_scalar(r_mu[0:1, 0:w], ps_sum[0:1, 0:w], INV_D, None,
                                                    ALU.mult).then_inc(s, 1),
                 reads=("ps_sum",), writes=("r_mu",))
            b.op("ve", lambda E, s: E.tensor_scalar(r_var[0:1, 0:w], ps_sq[0:1, 0:w], INV_D, EPS,
                                                    ALU.mult, ALU.add).then_inc(s, 1),
                 reads=("ps_sq",), writes=("r_var",))
            b.op("ve", lambda E, s: E.scalar_tensor_tensor(r_rstd[0:1, 0:w], r_mu[0:1, 0:w], 1.0,
                                                           r_mu[0:1, 0:w], ALU.mult, ALU.mult).then_inc(s, 1),
                 reads=("r_mu",), writes=("r_rstd",))
            b.op("ve", lambda E, s: E.tensor_tensor(r_var[0:1, 0:w], r_var[0:1, 0:w], r_rstd[0:1, 0:w],
                                                    ALU.subtract).then_inc(s, 1),
                 reads=("r_var", "r_rstd"), writes=("r_var",))
            b.op("se", lambda E, s: E.activation(out=r_rstd[0:1, 0:w], in_=r_var[0:1, 0:w],
                                                 func=ACT.Sqrt).then_inc(s, 1),
                 reads=("r_var",), writes=("r_rstd",))
            b.op("ve", lambda E, s: E.reciprocal(r_rstd[0:1, 0:w], r_rstd[0:1, 0:w]).then_inc(s, 1),
                 reads=("r_rstd",), writes=("r_rstd",))
            b.op("ve", lambda E, s: E.scalar_tensor_tensor(r_nmur[0:1, 0:w], r_mu[0:1, 0:w], -1.0,
                                                           r_rstd[0:1, 0:w], ALU.mult, ALU.mult).then_inc(s, 1),
                 reads=("r_mu", "r_rstd"), writes=("r_nmur",))
            b.op("te", lambda E, s: E.matmul(ps_bc[0][:, 0:w], c_ones_row[:], r_rstd[0:1, 0:w],
                                             start=True, stop=True).then_inc(s, 1),
                 reads=("r_rstd", "ones_row"), writes=("bc_r",))
            b.op("te", lambda E, s: E.matmul(ps_bc[1][:, 0:w], c_ones_row[:], r_nmur[0:1, 0:w],
                                             start=True, stop=True).then_inc(s, 1),
                 reads=("r_nmur", "ones_row"), writes=("bc_n",))
            return "bc_r", "bc_n"

        def emit_quant(pool, src_key, src_ap, dst_key, dst_ap, M, EMINB, signed, W):
            pf, pi = pool["f"], pool["i"]
            ebits, sbits, vbits, par = pi[0], pi[1], pi[2], pi[3]
            ef, t, cf, frac, g1, g2 = pf[0], pf[1], pf[2], pf[3], pf[4], pf[5]
            xb = src_ap.bitcast(I32)
            K = lambda nm: ("q", nm)

            def o(emit, reads, writes):
                b.op("ve", emit, reads=reads, writes=writes)

            o(lambda E, s: E.tensor_scalar(ebits[:, :W], xb, 23, None, ALU.logical_shift_right).then_inc(s, 1),
              [src_key], [K("eb")])
            o(lambda E, s: E.tensor_scalar(ebits[:, :W], ebits[:, :W], 255, None, ALU.bitwise_and).then_inc(s, 1),
              [K("eb")], [K("eb")])
            o(lambda E, s: E.tensor_copy(ef[:, :W], ebits[:, :W]).then_inc(s, 1), [K("eb")], [K("ef")])
            o(lambda E, s: E.tensor_scalar(ef[:, :W], ef[:, :W], float(EMINB), -(127.0 + M),
                                           ALU.max, ALU.add).then_inc(s, 1), [K("ef")], [K("ef")])
            o(lambda E, s: E.tensor_scalar(ef[:, :W], ef[:, :W], -126.0, None,
                                           ALU.max).then_inc(s, 1), [K("ef")], [K("ef")])
            o(lambda E, s: E.tensor_copy(ebits[:, :W], ef[:, :W]).then_inc(s, 1), [K("ef")], [K("eb")])
            o(lambda E, s: E.tensor_scalar(sbits[:, :W], ebits[:, :W], 127, None, ALU.add).then_inc(s, 1),
              [K("eb")], [K("sb")])
            o(lambda E, s: E.tensor_scalar(sbits[:, :W], sbits[:, :W], 23, None,
                                           ALU.logical_shift_left).then_inc(s, 1), [K("sb")], [K("sb")])
            o(lambda E, s: E.tensor_scalar(vbits[:, :W], ebits[:, :W], -1, 127, ALU.mult, ALU.add).then_inc(s, 1),
              [K("eb")], [K("vb")])
            o(lambda E, s: E.tensor_scalar(vbits[:, :W], vbits[:, :W], 23, None,
                                           ALU.logical_shift_left).then_inc(s, 1), [K("vb")], [K("vb")])
            if signed:
                o(lambda E, s: E.tensor_scalar(ebits[:, :W], xb, 0x7FFFFFFF, None,
                                               ALU.bitwise_and).then_inc(s, 1), [src_key], [K("ab")])
                asrc, akey = ebits[:, :W].bitcast(F32), K("ab")
            else:
                asrc, akey = src_ap, src_key
            o(lambda E, s, asrc=asrc: E.tensor_tensor(t[:, :W], asrc, vbits[:, :W].bitcast(F32),
                                                      ALU.mult).then_inc(s, 1),
              [akey, K("vb")], [K("t")])
            o(lambda E, s: E.tensor_copy(par[:, :W], t[:, :W]).then_inc(s, 1), [K("t")], [K("ci")])
            o(lambda E, s: E.tensor_copy(cf[:, :W], par[:, :W]).then_inc(s, 1), [K("ci")], [K("cf")])
            o(lambda E, s: E.tensor_tensor(frac[:, :W], t[:, :W], cf[:, :W], ALU.subtract).then_inc(s, 1),
              [K("t"), K("cf")], [K("fr")])
            o(lambda E, s: E.tensor_scalar(g1[:, :W], frac[:, :W], 0.5, None, ALU.is_gt).then_inc(s, 1),
              [K("fr")], [K("g1")])
            o(lambda E, s: E.tensor_scalar(g2[:, :W], frac[:, :W], 0.5, None, ALU.is_equal).then_inc(s, 1),
              [K("fr")], [K("g2")])
            o(lambda E, s: E.tensor_scalar(frac[:, :W], frac[:, :W], -0.5, None, ALU.is_equal).then_inc(s, 1),
              [K("fr")], [K("fr")])
            o(lambda E, s: E.tensor_tensor(g2[:, :W], g2[:, :W], frac[:, :W], ALU.subtract).then_inc(s, 1),
              [K("g2"), K("fr")], [K("g2")])
            o(lambda E, s: E.tensor_scalar(par[:, :W], par[:, :W], 1, None, ALU.bitwise_and).then_inc(s, 1),
              [K("ci")], [K("pi")])
            o(lambda E, s: E.tensor_copy(frac[:, :W], par[:, :W]).then_inc(s, 1), [K("pi")], [K("pf")])
            o(lambda E, s: E.tensor_tensor(g2[:, :W], g2[:, :W], frac[:, :W], ALU.mult).then_inc(s, 1),
              [K("g2"), K("pf")], [K("g2")])
            o(lambda E, s: E.tensor_tensor(g1[:, :W], g1[:, :W], g2[:, :W], ALU.add).then_inc(s, 1),
              [K("g1"), K("g2")], [K("g1")])
            o(lambda E, s: E.tensor_tensor(cf[:, :W], cf[:, :W], g1[:, :W], ALU.add).then_inc(s, 1),
              [K("cf"), K("g1")], [K("cf")])
            if signed:
                o(lambda E, s: E.tensor_scalar(vbits[:, :W], xb, -0x80000000, None,
                                               ALU.bitwise_and).then_inc(s, 1), [src_key], [K("sg")])
                o(lambda E, s: E.tensor_scalar(vbits[:, :W], vbits[:, :W], 0x3F800000, None,
                                               ALU.bitwise_or).then_inc(s, 1), [K("sg")], [K("sg")])
                o(lambda E, s: E.tensor_tensor(cf[:, :W], cf[:, :W], vbits[:, :W].bitcast(F32),
                                               ALU.mult).then_inc(s, 1), [K("cf"), K("sg")], [K("cf")])
            o(lambda E, s: E.tensor_tensor(dst_ap, cf[:, :W], sbits[:, :W].bitcast(F32),
                                           ALU.mult).then_inc(s, 1), [K("cf"), K("sb")], [dst_key])

        def emit_wquant(pool, src_dram, dst_dram, rows, cols, tag):
            W = min(2048, cols)
            tin = pool["f"][6]
            tout = pool["bf"]
            for r0 in range(0, rows, 128):
                for c0 in range(0, cols, W):
                    b.dmaop("io", lambda E, s, r0=r0, c0=c0, W=W: E.dma_start(
                                out=tin[:, 0:W], in_=src_dram[r0:r0 + 128, c0:c0 + W]).then_inc(s, 16),
                            reads=(), writes=(("wq_in", tag),))
                    emit_quant(pool, ("wq_in", tag), tin[:, 0:W], ("wq_out", tag), tout[:, 0:W],
                               2, 113.0, True, W)
                    b.dmaop("io", lambda E, s, r0=r0, c0=c0, W=W: E.dma_start(
                                out=dst_dram[r0:r0 + 128, c0:c0 + W], in_=tout[:, 0:W]).then_inc(s, 16),
                            reads=(("wq_out", tag),), writes=(("wq_st", tag),))

        l0buf = {}
        l0ctr = [0]

        def l0ffn_alloc(px):
            sfx = str(l0ctr[0])
            l0ctr[0] += 1
            l0buf["t_xin"] = px.enter_context(nc.sbuf_tensor("t_xin" + sfx, [128, NCK * ST], BF16))
            l0buf["t_f1"] = px.enter_context(nc.sbuf_tensor("t_f1" + sfx, [128, ST], F32))
            l0buf["t_xn"] = px.enter_context(nc.sbuf_tensor("t_xn" + sfx, [128, NCK * ST], BF16))
            l0buf["t_xa"] = px.enter_context(nc.sbuf_tensor("t_xa" + sfx, [128, NCK * ST], BF16))
            l0buf["t_h"] = px.enter_context(nc.sbuf_tensor("t_h" + sfx, [128, 8 * ST], BF16))
            l0buf["t_acc"] = px.enter_context(nc.sbuf_tensor("t_acc" + sfx, [128, NCK * ST], F32))
            l0buf["w_sq"] = [px.enter_context(nc.sbuf_tensor(f"w_sq{i}" + sfx, [128, NCK * 128], BF16))
                             for i in range(2)]
            l0buf["w_1"] = [px.enter_context(nc.sbuf_tensor(f"w_1{i}" + sfx, [128, NCK * 128], BF16))
                            for i in range(2)]
            l0buf["w2s"] = [px.enter_context(nc.sbuf_tensor(f"w2s{i}" + sfx, [128, D], BF16))
                            for i in range(4)]

        def l0ffn_supertile(m0, R):
            t_xin = l0buf["t_xin"]; t_f1 = l0buf["t_f1"]; t_xn = l0buf["t_xn"]
            t_xa = l0buf["t_xa"]; t_h = l0buf["t_h"]; t_acc = l0buf["t_acc"]
            t_eo = l0buf["t_xn"]; t_g = l0buf["t_f1"]
            w_sq = l0buf["w_sq"]; w_1 = l0buf["w_1"]; w2_slots = l0buf["w2s"]

            bc_r, bc_n = emit_ln(
                "lnA", ST,
                lambda c: ("R", c),
                lambda c, m0=m0, R=R: R[:, c * TOK + m0: c * TOK + m0 + ST], False)
            for c in range(NCK):
                b.op("ve", lambda E, s, c=c, m0=m0, R=R: E.tensor_tensor(
                        t_f1[:, 0:ST], R[:, c * TOK + m0: c * TOK + m0 + ST],
                        ps_bc[0][:, 0:ST], ALU.mult).then_inc(s, 1),
                     reads=(("R", c), bc_r), writes=("t_f1",))
                b.op("ve", lambda E, s, c=c: E.tensor_tensor(
                        t_xin[:, c * ST:(c + 1) * ST], t_f1[:, 0:ST],
                        ps_bc[1][:, 0:ST], ALU.add).then_inc(s, 1),
                     reads=("t_f1", bc_n), writes=(("xin", c),))
            bc_r2, bc_n2 = emit_ln(
                "lnB", ST,
                lambda c: ("xin", c),
                lambda c: t_xin[:, c * ST:(c + 1) * ST], True)
            for c in range(NCK):
                b.op("ve", lambda E, s, c=c: E.tensor_tensor(
                        t_f1[:, 0:ST], t_xin[:, c * ST:(c + 1) * ST],
                        ps_bc[0][:, 0:ST], ALU.mult).then_inc(s, 1),
                     reads=(("xin", c), bc_r2), writes=("t_f1",))
                b.op("ve", lambda E, s, c=c: E.tensor_tensor(
                        t_f1[:, 0:ST], t_f1[:, 0:ST],
                        ps_bc[1][:, 0:ST], ALU.add).then_inc(s, 1),
                     reads=("t_f1", bc_n2), writes=("t_f1",))
                b.op("se", lambda E, s, c=c: E.activation(
                        out=t_xn[:, c * ST:(c + 1) * ST], in_=t_f1[:, 0:ST], func=ACT.Identity,
                        bias=s_lnb[:, c:c + 1], scale=s_lns[:, c:c + 1]).then_inc(s, 1),
                     reads=("t_f1", "lns", "lnb"), writes=(("xn", c),))

            for ec in range(NCK):
                w = w_sq[ec % 2]
                b.dmaop("gw", lambda E, s, ec=ec, w=w: E.dma_start(
                            out=w[:].rearrange("p (c q) -> p c q", c=NCK),
                            in_=attn_wT[:, ec * 128:(ec + 1) * 128]
                                .rearrange("(c p) q -> p c q", p=128)).then_inc(s, 16),
                        reads=(), writes=(("w_sq", ec % 2),))
                b.op("te", lambda E, s, ec=ec, w=w: _group_mm(
                        E, s, ps_m[ec % 2][:, 0:ST],
                        [(w[:, c * 128:(c + 1) * 128], t_xn[:, c * ST:(c + 1) * ST])
                         for c in range(NCK)]),
                     reads=tuple(("xn", c) for c in range(NCK)) + (("w_sq", ec % 2),),
                     writes=(("ps_m", ec % 2),))
                b.op("se", lambda E, s, ec=ec: E.activation(
                        out=t_g[:, 0:ST], in_=ps_m[ec % 2][:, 0:ST], func=ACT.Identity,
                        bias=s_attnb[:, ec:ec + 1], scale=1.0).then_inc(s, 1),
                     reads=(("ps_m", ec % 2), "attnb"), writes=("t_f1",))
                b.op("ve", lambda E, s, ec=ec: E.tensor_tensor(
                        t_xa[:, ec * ST:(ec + 1) * ST], t_g[:, 0:ST],
                        t_xin[:, ec * ST:(ec + 1) * ST], ALU.add).then_inc(s, 1),
                     reads=("t_f1", ("xin", ec)), writes=(("xa", ec),))

            for qq in range(4):
                for half in range(2):
                    for hx in range(4):
                        hh = half * 4 + hx
                        hc = qq * 8 + hh
                        w = w_1[hx % 2]
                        b.dmaop("gw", lambda E, s, hc=hc, w=w: E.dma_start(
                                    out=w[:].rearrange("p (c q) -> p c q", c=NCK),
                                    in_=w1e[:, hc * 128:(hc + 1) * 128]
                                        .rearrange("(c p) q -> p c q", p=128)).then_inc(s, 16),
                                reads=(), writes=(("w_1", hx % 2),))
                        b.op("te", lambda E, s, hx=hx, w=w: _group_mm(
                                E, s, ps_h[hx % 2][:, 0:ST],
                                [(w[:, c * 128:(c + 1) * 128], t_xa[:, c * ST:(c + 1) * ST])
                                 for c in range(NCK)]),
                             reads=tuple(("xa", c) for c in range(NCK)) + (("w_1", hx % 2),),
                             writes=(("ps_h", hx % 2),))
                        b.op("se", lambda E, s, hc=hc, hx=hx: E.activation(
                                out=t_h[:, hx * ST:(hx + 1) * ST], in_=ps_h[hx % 2][:, 0:ST],
                                func=ACT.Gelu, bias=s_b1[:, hc:hc + 1], scale=1.0).then_inc(s, 1),
                             reads=(("ps_h", hx % 2), "b1"), writes=(("h", hx),))
                        b.dmaop("gw", lambda E, s, hc=hc, hx=hx: E.dma_start(
                                    out=w2_slots[hx][:],
                                    in_=w2e[hc * 128:(hc + 1) * 128, :]).then_inc(s, 16),
                                reads=(), writes=(("w2s", hx),))
                    for dc in range(NCK):
                        b.op("te", lambda E, s, dc=dc: _group_mm(
                                E, s, ps_m[dc % 2][:, 0:ST],
                                [(w2_slots[hx][:, dc * 128:(dc + 1) * 128],
                                  t_h[:, hx * ST:(hx + 1) * ST]) for hx in range(4)]),
                             reads=tuple(("h", hx) for hx in range(4)) +
                                   tuple(("w2s", hx) for hx in range(4)),
                             writes=(("ps_m", dc % 2),))
                        if qq == 0 and half == 0:
                            b.op("ve", lambda E, s, dc=dc: E.tensor_copy(
                                    t_acc[:, dc * ST:(dc + 1) * ST],
                                    ps_m[dc % 2][:, 0:ST]).then_inc(s, 1),
                                 reads=(("ps_m", dc % 2),), writes=(("acc", dc),))
                        else:
                            b.op("ve", lambda E, s, dc=dc: E.tensor_tensor(
                                    t_acc[:, dc * ST:(dc + 1) * ST],
                                    t_acc[:, dc * ST:(dc + 1) * ST],
                                    ps_m[dc % 2][:, 0:ST], ALU.add).then_inc(s, 1),
                                 reads=(("ps_m", dc % 2), ("acc", dc)), writes=(("acc", dc),))

            for dc in range(NCK):
                b.op("se", lambda E, s, dc=dc: E.activation(
                        out=t_eo[:, dc * ST:(dc + 1) * ST],
                        in_=t_acc[:, dc * ST:(dc + 1) * ST], func=ACT.Identity,
                        bias=s_b2[:, dc:dc + 1], scale=1.0).then_inc(s, 1),
                     reads=(("acc", dc), "b2"), writes=(("xn", dc),))
            for ec in range(NCK):
                w = w_sq[ec % 2]
                b.dmaop("gw", lambda E, s, ec=ec, w=w: E.dma_start(
                            out=w[:].rearrange("p (c q) -> p c q", c=NCK),
                            in_=next_wT[:, ec * 128:(ec + 1) * 128]
                                .rearrange("(c p) q -> p c q", p=128)).then_inc(s, 16),
                        reads=(), writes=(("w_sq", ec % 2),))
                b.op("te", lambda E, s, ec=ec, w=w: _group_mm(
                        E, s, ps_h[ec % 2][:, 0:ST],
                        [(w[:, c * 128:(c + 1) * 128], t_eo[:, c * ST:(c + 1) * ST])
                         for c in range(NCK)]),
                     reads=tuple(("xn", c) for c in range(NCK)) + (("w_sq", ec % 2),),
                     writes=(("ps_h", ec % 2),))
                b.op("se", lambda E, s, ec=ec: E.activation(
                        out=t_g[:, 0:ST], in_=ps_h[ec % 2][:, 0:ST], func=ACT.Gelu,
                        bias=s_nextb[:, ec:ec + 1], scale=1.0).then_inc(s, 1),
                     reads=(("ps_h", ec % 2), "nextb"), writes=("t_f1",))
                b.op("ve", lambda E, s, ec=ec, m0=m0, R=R: E.scalar_tensor_tensor(
                        R[:, ec * TOK + m0: ec * TOK + m0 + ST], t_g[:, 0:ST], 0.5,
                        R[:, ec * TOK + m0: ec * TOK + m0 + ST],
                        ALU.mult, ALU.add).then_inc(s, 1),
                     reads=("t_f1", ("R", ec)), writes=(("R", ec),))

        def l0ffn_phase(R):
            with ExitStack() as px:
                l0ffn_alloc(px)
                for st_i in range(NST):
                    l0ffn_supertile(st_i * ST, R)

        l1ctr = [0]

        def l1ffn_phase():
            call = l1ctr[0]
            l1ctr[0] += 1
            with ExitStack() as px:
                sfx = f"l1_{call}"
                pb = lambda nm, shape, dt: px.enter_context(nc.sbuf_tensor(nm + sfx, shape, dt))
                pool = {
                    "f": [pb(f"qf{i}", [128, 2048], F32) for i in range(8)],
                    "i": [pb(f"qi{i}", [128, 2048], I32) for i in range(4)],
                    "bf": pb("qbf", [128, 2048], BF16),
                }
                t_rt = pb("t_rt", [128, NCK * ST], F32)
                xq = pb("xq", [128, NCK * ST], BF16)
                hq = pb("hq", [128, 2048], BF16)
                t_acc = pb("t_acc", [128, NCK * ST], F32)
                w1s = [pb(f"w1s{i}", [128, NCK * 128], BF16) for i in range(2)]
                w2s = [pb(f"w2s{i}", [128, D], BF16) for i in range(4)]
                t_mask = pb("t_mask", [128, NCK * ST], BF16)

                if call == 0:
                    emit_wquant(pool, W1T, d_w1q, D, H0, "w1")
                    b.op("ve", lambda E, s: E.engine_nop().then_inc(s, 1),
                         reads=(("wq_st", "w1"),), writes=("wq1done",))
                    emit_wquant(pool, W2T, d_w2q, H0, D, "w2")
                    b.op("ve", lambda E, s: E.engine_nop().then_inc(s, 1),
                         reads=(("wq_st", "w2"),), writes=("wq2done",))
                    emit_quant(pool, "l1b1", s_l1b1[:], ("l1b1q",), s_l1b1q[:], 2, 113.0, True, NH0)
                    emit_quant(pool, "l1b2", s_l1b2[:], ("l1b2q",), s_l1b2q[:], 2, 113.0, True, NCK)

                zst = pool["f"][6]
                xst = pool["f"][7]
                for st_i in range(NST):
                    m0 = st_i * ST
                    b.dmaop("io", lambda E, s, m0=m0: E.dma_start(
                                out=t_rt[:].rearrange("p (c m) -> p c m", c=NCK),
                                in_=d_rsp.rearrange("(c p) m -> p c m", p=128)
                                    [:, :, m0:m0 + ST]).then_inc(s, 16),
                            reads=("rspv",), writes=(("rt",),))
                    b.dmaop("io", lambda E, s, m0=m0: E.dma_start(
                                out=t_mask[:].rearrange("p (c m) -> p c m", c=NCK),
                                in_=maskT.rearrange("(c p) m -> p c m", p=128)
                                    [:, :, m0:m0 + ST]).then_inc(s, 16),
                            reads=(), writes=(("mask",),))
                    bc_r, bc_n = emit_ln("l1ln", ST, lambda c: ("rt",),
                                         lambda c: t_rt[:, c * ST:(c + 1) * ST], False)
                    for half in range(2):
                        for cc in range(4):
                            c = half * 4 + cc
                            b.op("ve", lambda E, s, c=c, cc=cc: E.tensor_tensor(
                                    xst[:, cc * ST:(cc + 1) * ST], t_rt[:, c * ST:(c + 1) * ST],
                                    ps_bc[0][:, 0:ST], ALU.mult).then_inc(s, 1),
                                 reads=(("rt",), bc_r), writes=("xst",))
                            b.op("ve", lambda E, s, c=c, cc=cc: E.tensor_tensor(
                                    xst[:, cc * ST:(cc + 1) * ST], xst[:, cc * ST:(cc + 1) * ST],
                                    ps_bc[1][:, 0:ST], ALU.add).then_inc(s, 1),
                                 reads=("xst", bc_n), writes=("xst",))
                        emit_quant(pool, "xst", xst[:, 0:2048], "xq",
                                   xq[:, half * 2048:(half + 1) * 2048], 2, 113.0, True, 2048)
                    xqk = ("xq",)
                    for g in range(8):
                        for hx in range(4):
                            hc = g * 4 + hx
                            w = w1s[hx % 2]
                            b.dmaop("wt", lambda E, s, hc=hc, w=w: E.dma_start(
                                        out=w[:].rearrange("p (c q) -> p c q", c=NCK),
                                        in_=d_w1q[:, hc * 128:(hc + 1) * 128]
                                            .rearrange("(c p) q -> p c q", p=128)).then_inc(s, 16),
                                    reads=("wq1done",), writes=(("w1s", hx % 2),))
                            b.op("te", lambda E, s, hx=hx, w=w: _group_mm(
                                    E, s, ps_h[hx % 2][:, 0:ST],
                                    [(w[:, c * 128:(c + 1) * 128], xq[:, c * ST:(c + 1) * ST])
                                     for c in range(NCK)]),
                                 reads=xqk + (("w1s", hx % 2),), writes=(("ps_h", hx % 2),))
                            b.op("se", lambda E, s, hc=hc, hx=hx: E.activation(
                                    out=zst[:, hx * ST:(hx + 1) * ST], in_=ps_h[hx % 2][:, 0:ST],
                                    func=ACT.Relu, bias=s_l1b1q[:, hc:hc + 1], scale=1.0).then_inc(s, 1),
                                 reads=(("ps_h", hx % 2), ("l1b1q",)), writes=("zst",))
                            b.dmaop("wt", lambda E, s, hc=hc, hx=hx: E.dma_start(
                                        out=w2s[hx][:],
                                        in_=d_w2q[hc * 128:(hc + 1) * 128, :]).then_inc(s, 16),
                                    reads=("wq2done",), writes=(("w2s", hx),))
                        emit_quant(pool, "zst", zst[:, 0:2048], "xst",
                                   xst[:, 0:2048], 7, 1.0, False, 2048)
                        emit_quant(pool, "xst", xst[:, 0:2048], "hq",
                                   hq[:, 0:2048], 2, 113.0, False, 2048)
                        for dc in range(NCK):
                            b.op("te", lambda E, s, dc=dc: _group_mm(
                                    E, s, ps_m[dc % 2][:, 0:ST],
                                    [(w2s[hx][:, dc * 128:(dc + 1) * 128],
                                      hq[:, hx * ST:(hx + 1) * ST]) for hx in range(4)]),
                                 reads=("hq",) + tuple(("w2s", hx) for hx in range(4)),
                                 writes=(("ps_m", dc % 2),))
                            if g == 0:
                                b.op("ve", lambda E, s, dc=dc: E.tensor_copy(
                                        t_acc[:, dc * ST:(dc + 1) * ST],
                                        ps_m[dc % 2][:, 0:ST]).then_inc(s, 1),
                                     reads=(("ps_m", dc % 2),), writes=(("acc", dc),))
                            else:
                                b.op("ve", lambda E, s, dc=dc: E.tensor_tensor(
                                        t_acc[:, dc * ST:(dc + 1) * ST],
                                        t_acc[:, dc * ST:(dc + 1) * ST],
                                        ps_m[dc % 2][:, 0:ST], ALU.add).then_inc(s, 1),
                                     reads=(("ps_m", dc % 2), ("acc", dc)), writes=(("acc", dc),))
                    for half in range(2):
                        for cc in range(4):
                            dc = half * 4 + cc
                            b.op("se", lambda E, s, dc=dc, cc=cc: E.activation(
                                    out=zst[:, cc * ST:(cc + 1) * ST],
                                    in_=t_acc[:, dc * ST:(dc + 1) * ST], func=ACT.Identity,
                                    bias=s_l1b2q[:, dc:dc + 1], scale=1.0).then_inc(s, 1),
                                 reads=(("acc", dc), ("l1b2q",)), writes=("zst",))
                        emit_quant(pool, "zst", zst[:, 0:2048], "xst",
                                   xst[:, 0:2048], 7, 1.0, True, 2048)
                        b.op("ve", lambda E, s, half=half: E.tensor_tensor(
                                xst[:, 0:2048], xst[:, 0:2048],
                                t_mask[:, half * 2048:(half + 1) * 2048], ALU.mult).then_inc(s, 1),
                             reads=("xst", ("mask",)), writes=("xst",))
                        b.op("ve", lambda E, s, half=half: E.scalar_tensor_tensor(
                                t_rt[:, half * 2048:(half + 1) * 2048], xst[:, 0:2048], 0.5,
                                t_rt[:, half * 2048:(half + 1) * 2048],
                                ALU.mult, ALU.add).then_inc(s, 1),
                             reads=("xst", ("rt",)), writes=(("rt",),))
                    b.dmaop("io", lambda E, s, m0=m0: E.dma_start(
                                out=d_rsp.rearrange("(c p) m -> p c m", p=128)
                                    [:, :, m0:m0 + ST],
                                in_=t_rt[:].rearrange("p (c m) -> p c m", c=NCK)).then_inc(s, 16),
                            reads=(("rt",),), writes=(("rspv",),))

        GOFF = [0] * 17
        for q in range(16):
            GOFF[q + 1] = GOFF[q] + (16 - q) * 128

        def scan_pre(R):
            with ExitStack() as px:
                stg = px.enter_context(nc.sbuf_tensor("sc_stg", [128, NCK * ST], BF16))
                for st_i in range(NST):
                    m0 = st_i * ST
                    bc_r, bc_n = emit_ln(
                        "lnS", ST, lambda c: ("R", c),
                        lambda c, m0=m0, R=R: R[:, c * TOK + m0: c * TOK + m0 + ST], False)
                    for c in range(NCK):
                        b.op("ve", lambda E, s, c=c, m0=m0, R=R: E.tensor_tensor(
                                sq_slots[0][:, 0:ST], R[:, c * TOK + m0: c * TOK + m0 + ST],
                                ps_bc[0][:, 0:ST], ALU.mult).then_inc(s, 1),
                             reads=(("R", c), bc_r), writes=(("sqslot", 0),))
                        b.op("ve", lambda E, s, c=c: E.tensor_tensor(
                                stg[:, c * ST:(c + 1) * ST], sq_slots[0][:, 0:ST],
                                ps_bc[1][:, 0:ST], ALU.add).then_inc(s, 1),
                             reads=(("sqslot", 0), bc_n), writes=(("stg", c),))
                    p_ = m0 // SEQ
                    s0 = m0 % SEQ
                    for half in range(2):
                        jd = s0 // 256 + half
                        b.dmaop("io", lambda E, s, p_=p_, jd=jd, half=half: E.dma_start(
                                    out=d_x1in[jd * D:(jd + 1) * D, p_ * 256:(p_ + 1) * 256]
                                        .rearrange("(c p) t -> p c t", p=128),
                                    in_=stg[:].rearrange("p (c t) -> p c t", c=NCK)
                                        [:, :, half * 256:(half + 1) * 256]).then_inc(s, 16),
                                reads=tuple(("stg", c) for c in range(NCK)),
                                writes=(("x1in", st_i, half),))
                for c in range(NCK):
                    b.dmaop("io", lambda E, s, c=c, R=R: E.dma_start(
                                out=d_rsp[c * 128:(c + 1) * 128, :],
                                in_=R[:, c * TOK:(c + 1) * TOK]).then_inc(s, 16),
                            reads=(("R", c),), writes=(("rsp", c),))

        def _a2a(E, s, ins_ap, outs_ap):
            from concourse.aot_env import allow_ici_all_to_all
            with allow_ici_all_to_all():
                E.collective_compute(
                    "AllToAll", ALU.bypass, replica_groups=[list(range(NCORE))],
                    ins=[ins_ap], outs=[outs_ap]).then_inc(s, 1)

        def scan_phase_a():
            b.ccop(lambda E, s: _a2a(E, s, d_x1in.opt(), d_x1out.opt()),
                   reads=tuple(("x1in", i, h) for i in range(NST) for h in range(2)),
                   writes=("x1out",))
            pubkeys = []
            with ExitStack() as px:
                Xt = px.enter_context(nc.sbuf_tensor("sc_Xt", [128, NCK * MC], BF16))
                UV = px.enter_context(nc.sbuf_tensor("sc_UV", [128, 16 * 2048], BF16))
                Gb = px.enter_context(nc.sbuf_tensor("sc_G", [128, GOFF[16]], BF16))
                wkvs = px.enter_context(nc.sbuf_tensor("sc_wkv", [128, NCK * D], BF16))
                bkvs = px.enter_context(nc.sbuf_tensor("sc_bkv", [1, D], BF16))
                utst = px.enter_context(nc.sbuf_tensor("sc_utst", [128, NCK * 128], BF16))
                tls = {nm: px.enter_context(nc.sbuf_tensor("sc_" + nm, [128, 128], BF16))
                       for nm in ("tT", "tL", "t2", "t2t", "t4", "a1", "b2", "b4", "m1", "inv")}
                b.dmaop("gw", lambda E, s: E.dma_start(
                            out=wkvs[:].rearrange("p (c q) -> p c q", c=NCK),
                            in_=wkv_vT.rearrange("(c p) q -> p c q", p=128)).then_inc(s, 16),
                        writes=("wkvs",))
                b.dmaop("gw", lambda E, s: E.dma_start(out=bkvs[:], in_=bkv_row).then_inc(s, 16),
                        writes=("bkvs",))

                for cidx in range(2):
                    b.global_barrier()
                    base1 = cidx * PUB1
                    for o in range(NCORE):
                        for p_ in range(2):
                            blk = o * 2 + p_
                            b.dmaop("io", lambda E, s, o=o, p_=p_, blk=blk, cidx=cidx: E.dma_start(
                                        out=UV[:].rearrange("p (c r) -> p c r", c=16)
                                            [:, 0:NCK, blk * 128:(blk + 1) * 128],
                                        in_=d_x1out[o * D:(o + 1) * D,
                                                    p_ * 256 + cidx * 128: p_ * 256 + (cidx + 1) * 128]
                                            .rearrange("(c p) t -> p c t", p=128)).then_inc(s, 16),
                                    reads=("x1out",), writes=(("Xc", blk),))
                    for o in range(NCORE):
                        for p_ in range(2):
                            blk = o * 2 + p_
                            bglob = o + 8 * p_
                            b.op("ve", lambda E, s, blk=blk, bglob=bglob: E.tensor_copy(
                                    Xt[:].rearrange("p (c t bb) -> p c t bb", c=NCK, bb=16)
                                        [:, :, :, bglob],
                                    UV[:].rearrange("p (c r) -> p c r", c=16)
                                        [:, 0:NCK, blk * 128:(blk + 1) * 128]).then_inc(s, 1),
                                 reads=(("Xc", blk),), writes=(("Xt", o, p_),))
                    xtk = tuple(("Xt", o, p_) for o in range(NCORE) for p_ in range(2))
                    for j in range(16):
                        for cc in range(NCK):
                            b.dmaop("tp", lambda E, s, j=j, cc=cc: E.dma_start(
                                        out=UV[:, j * 2048 + cc * 128: j * 2048 + (cc + 1) * 128],
                                        in_=Xt[:, cc * MC + j * 128: cc * MC + (j + 1) * 128],
                                        transpose=True).then_inc(s, 16),
                                    reads=xtk, writes=(("UVx", j),))
                        b.dmaop("gw", lambda E, s, j=j, base1=base1: E.dma_start(
                                    out=d_pub1[base1 + D + j * 128: base1 + D + (j + 1) * 128, 0:1024],
                                    in_=UV[:, j * 2048: j * 2048 + 1024]).then_inc(s, 16),
                                reads=(("UVx", j),), writes=(("pubx", cidx, j),))
                        pubkeys.append(("pubx", cidx, j))
                    for j in range(16):
                        for hf in range(2):
                            def vmm(E, s, j=j, hf=hf):
                                for c in range(NCK):
                                    E.matmul(ps_m[hf][:, 0:512],
                                             Xt[:, c * MC + j * 128: c * MC + (j + 1) * 128],
                                             wkvs[:, c * D + hf * 512: c * D + (hf + 1) * 512],
                                             start=(c == 0), stop=False)
                                E.matmul(ps_m[hf][:, 0:512], c_ones_rowb[:],
                                         bkvs[0:1, hf * 512:(hf + 1) * 512],
                                         start=False, stop=True).then_inc(s, 1)
                            b.op("te", vmm,
                                 reads=xtk + ("wkvs", "bkvs", "ones_rowb"),
                                 writes=(("ps_m", hf),))
                            b.op("ve", lambda E, s, j=j, hf=hf: E.tensor_copy(
                                    UV[:, j * 2048 + 1024 + hf * 512: j * 2048 + 1024 + (hf + 1) * 512],
                                    ps_m[hf][:, 0:512]).then_inc(s, 1),
                                 reads=(("ps_m", hf),), writes=(("UVv", j, hf),))
                    uvvk = tuple(("UVv", j, hf) for j in range(16) for hf in range(2))
                    for dst_i in range(NCORE):
                        rbase = ((dst_i * 2 + cidx) * 2 + 0) * MC
                        b.dmaop("gw", lambda E, s, dst_i=dst_i, rbase=rbase: E.dma_start(
                                    out=d_p2in[rbase: rbase + MC, :]
                                        .rearrange("(j p) q -> p j q", p=128),
                                    in_=UV[:].rearrange("p (j q) -> p j q", j=16)
                                        [:, :, 1024 + dst_i * 128: 1024 + (dst_i + 1) * 128]
                                    ).then_inc(s, 16),
                                reads=uvvk, writes=(("pubv", cidx, dst_i),))
                        pubkeys.append(("pubv", cidx, dst_i))
                    for q in range(16):
                        g0 = q
                        while g0 < 16:
                            wblk = min(4, 16 - g0)
                            def gmm(E, s, q=q, g0=g0, wblk=wblk):
                                for c in range(NCK):
                                    E.matmul(ps_h[0][:, 0:wblk * 128],
                                             Xt[:, c * MC + q * 128: c * MC + (q + 1) * 128],
                                             Xt[:, c * MC + g0 * 128: c * MC + (g0 + wblk) * 128],
                                             start=(c == 0), stop=(c == NCK - 1)).then_inc(s, 1) \
                                        if c == NCK - 1 else E.matmul(
                                             ps_h[0][:, 0:wblk * 128],
                                             Xt[:, c * MC + q * 128: c * MC + (q + 1) * 128],
                                             Xt[:, c * MC + g0 * 128: c * MC + (g0 + wblk) * 128],
                                             start=(c == 0), stop=False)
                            b.op("te", gmm, reads=xtk, writes=(("ps_h", 0),))
                            b.op("ve", lambda E, s, q=q, g0=g0, wblk=wblk: E.tensor_copy(
                                    Gb[:, GOFF[q] + (g0 - q) * 128: GOFF[q] + (g0 - q + wblk) * 128],
                                    ps_h[0][:, 0:wblk * 128]).then_inc(s, 1),
                                 reads=(("ps_h", 0),), writes=(("G", q, g0),))
                            g0 += wblk
                    gk = tuple(("G", q, g0) for q in range(16) for g0 in range(q, 16, 4))
                    for p_ in range(16):
                        if p_ > 0:
                            for g in range(4):
                                def rmm2(E, s, p_=p_, g=g):
                                    ins = None
                                    for qi in range(p_):
                                        ins = E.matmul(ps_m[g % 2][:, 0:512],
                                                 Gb[:, GOFF[qi] + (p_ - qi) * 128: GOFF[qi] + (p_ - qi + 1) * 128],
                                                 UV[:, qi * 2048 + g * 512: qi * 2048 + (g + 1) * 512],
                                                 start=(qi == 0), stop=(qi == p_ - 1))
                                    ins.then_inc(s, 1)
                                b.op("te", rmm2,
                                     reads=gk + tuple(("UVs", qi) for qi in range(p_)) + uvvk +
                                           tuple(("UVx", j) for j in range(16)),
                                     writes=(("ps_m", g % 2),))
                                b.op("ve", lambda E, s, p_=p_, g=g: E.scalar_tensor_tensor(
                                        UV[:, p_ * 2048 + g * 512: p_ * 2048 + (g + 1) * 512],
                                        ps_m[g % 2][:, 0:512], c_negc[:, 0:1],
                                        UV[:, p_ * 2048 + g * 512: p_ * 2048 + (g + 1) * 512],
                                        ALU.mult, ALU.add).then_inc(s, 1),
                                     reads=(("ps_m", g % 2), "negc", ("UVx", p_), ("UVv", p_, g % 2)),
                                     writes=(("UVr", p_, g), ("UVx", p_), ("UVv", p_, g % 2)))
                        b.op("ve", lambda E, s, p_=p_: E.scalar_tensor_tensor(
                                tls["tT"][:], Gb[:, GOFF[p_]:GOFF[p_] + 128], c_negc[:, 0:1],
                                c_msu[:], ALU.mult, ALU.mult).then_inc(s, 1),
                             reads=gk + ("negc", "msu"), writes=(("tT",),))
                        b.op("ve", lambda E, s, p_=p_: E.scalar_tensor_tensor(
                                tls["tL"][:], Gb[:, GOFF[p_]:GOFF[p_] + 128], c_negc[:, 0:1],
                                c_msl[:], ALU.mult, ALU.mult).then_inc(s, 1),
                             reads=gk + ("negc", "msl"), writes=(("tL",),))
                        def mm1(dst, lh, rh, rk, wk):
                            b.op("te", lambda E, s, lh=lh, rh=rh: E.matmul(
                                    ps_h[1][:, 0:128], lh[:], rh[:],
                                    start=True, stop=True).then_inc(s, 1),
                                 reads=rk, writes=(("ps_h", 1),))
                            b.op("ve", lambda E, s, dst=dst: E.tensor_copy(
                                    dst[:], ps_h[1][:, 0:128]).then_inc(s, 1),
                                 reads=(("ps_h", 1),), writes=(wk,))
                        mm1(tls["t2"], tls["tT"], tls["tL"], (("tT",), ("tL",)), ("t2",))
                        mm1(tls["t2t"], tls["tL"], tls["tT"], (("tT",), ("tL",)), ("t2t",))
                        mm1(tls["t4"], tls["t2t"], tls["t2"], (("t2",), ("t2t",)), ("t4",))
                        b.op("ve", lambda E, s: E.tensor_tensor(
                                tls["a1"][:], tls["tT"][:], c_ident[:], ALU.add).then_inc(s, 1),
                             reads=(("tT",), "ident"), writes=(("a1",),))
                        b.op("ve", lambda E, s: E.tensor_tensor(
                                tls["b2"][:], tls["t2"][:], c_ident[:], ALU.add).then_inc(s, 1),
                             reads=(("t2",), "ident"), writes=(("b2",),))
                        b.op("ve", lambda E, s: E.tensor_tensor(
                                tls["b4"][:], tls["t4"][:], c_ident[:], ALU.add).then_inc(s, 1),
                             reads=(("t4",), "ident"), writes=(("b4",),))
                        mm1(tls["m1"], tls["b2"], tls["a1"], (("b2",), ("a1",)), ("m1",))
                        mm1(tls["inv"], tls["b4"], tls["m1"], (("b4",), ("m1",)), ("inv",))
                        for g in range(4):
                            b.op("te", lambda E, s, p_=p_, g=g: E.matmul(
                                    ps_m[g % 2][:, 0:512], tls["inv"][:],
                                    UV[:, p_ * 2048 + g * 512: p_ * 2048 + (g + 1) * 512],
                                    start=True, stop=True).then_inc(s, 1),
                                 reads=(("inv",),) + ((("UVr", p_, g),) if p_ > 0 else
                                                      (("UVx", p_), ("UVv", p_, g % 2))),
                                 writes=(("ps_m", g % 2),))
                            b.op("ve", lambda E, s, p_=p_, g=g: E.tensor_copy(
                                    UV[:, p_ * 2048 + g * 512: p_ * 2048 + (g + 1) * 512],
                                    ps_m[g % 2][:, 0:512]).then_inc(s, 1),
                                 reads=(("ps_m", g % 2),), writes=(("UVs", p_),))
                    uvsk = tuple(("UVs", j) for j in range(16))
                    for j in range(16):
                        b.dmaop("tp", lambda E, s, j=j: E.dma_start(
                                    out=utst[:].rearrange("p (c q) -> p c q", c=NCK),
                                    in_=UV[:, j * 2048: j * 2048 + 1024]
                                        .rearrange("p (c q) -> p c q", c=NCK),
                                    transpose=True).then_inc(s, 16),
                                reads=uvsk, writes=(("utst",),))
                        b.dmaop("gw", lambda E, s, j=j, base1=base1: E.dma_start(
                                    out=d_pub1[base1: base1 + 1024, j * 128:(j + 1) * 128]
                                        .rearrange("(c p) q -> p c q", p=128),
                                    in_=utst[:].rearrange("p (c q) -> p c q", c=NCK)
                                    ).then_inc(s, 16),
                                reads=(("utst",),), writes=(("pubu", cidx, j),))
                        pubkeys.append(("pubu", cidx, j))
                    for dst_i in range(NCORE):
                        rbase = ((dst_i * 2 + cidx) * 2 + 1) * MC
                        b.dmaop("gw", lambda E, s, dst_i=dst_i, rbase=rbase: E.dma_start(
                                    out=d_p2in[rbase: rbase + MC, :]
                                        .rearrange("(j p) q -> p j q", p=128),
                                    in_=UV[:].rearrange("p (j q) -> p j q", j=16)
                                        [:, :, 1024 + dst_i * 128: 1024 + (dst_i + 1) * 128]
                                    ).then_inc(s, 16),
                                reads=uvsk, writes=(("pubf", cidx, dst_i),))
                        pubkeys.append(("pubf", cidx, dst_i))
            return pubkeys

        def scan_phase_b(pubkeys):
            b.ccop(lambda E, s: E.collective_compute(
                        "AllGather", ALU.bypass, replica_groups=[list(range(NCORE))],
                        ins=[d_pub1.opt()], outs=[d_agp1.opt()],
                    ).then_inc(s, 1),
                   reads=tuple(pubkeys), writes=("agp1",))
            b.ccop(lambda E, s: _a2a(E, s, d_p2in.opt(), d_p2out.opt()),
                   reads=tuple(pubkeys), writes=("p2out",))
            b.global_barrier()
            with ExitStack() as px:
                Ut = [px.enter_context(nc.sbuf_tensor(f"sb_Ut{i}", [128, NCK * MC], BF16)) for i in range(2)]
                Xr = [px.enter_context(nc.sbuf_tensor(f"sb_Xr{i}", [128, 16 * 1024], BF16)) for i in range(2)]
                Vb = [px.enter_context(nc.sbuf_tensor(f"sb_Vb{i}", [128, MC], BF16)) for i in range(2)]
                Fb = [px.enter_context(nc.sbuf_tensor(f"sb_Fb{i}", [128, MC], BF16)) for i in range(2)]
                UZt = px.enter_context(nc.sbuf_tensor("sb_UZt", [128, MC], BF16))
                UZr = px.enter_context(nc.sbuf_tensor("sb_UZr", [128, MC], BF16))
                Erow = px.enter_context(nc.sbuf_tensor("sb_Erow", [128, MC], BF16))
                Yrow = px.enter_context(nc.sbuf_tensor("sb_Yrow", [128, MC], BF16))
                Yt = px.enter_context(nc.sbuf_tensor("sb_Yt", [128, MC], BF16))
                Yb = px.enter_context(nc.sbuf_tensor("sb_Yb", [128, MC], BF16))
                Z = px.enter_context(nc.sbuf_tensor("sb_Z", [128, NCK * 128], F32))
                Zb = px.enter_context(nc.sbuf_tensor("sb_Zb", [128, NCK * 128], BF16))
                b.dmaop("io", lambda E, s: E.dma_start(
                            out=Z[:].rearrange("p (c q) -> p c q", c=NCK),
                            in_=z0.rearrange("(c p) q -> p c q", p=128)).then_inc(s, 16),
                        writes=(("Z",),))
                b.op("ve", lambda E, s: E.tensor_copy(Zb[:], Z[:]).then_inc(s, 1),
                     reads=(("Z",),), writes=(("Zb",),))
                for j in range(NCHUNK):
                    sl = j % 2
                    b.dmaop("io", lambda E, s, j=j, sl=sl: E.dma_start(
                                out=Ut[sl][:].rearrange("p (c m) -> p c m", c=NCK),
                                in_=d_agp1[j * PUB1: j * PUB1 + 1024, :]
                                    .rearrange("(c p) m -> p c m", p=128)).then_inc(s, 16),
                            reads=("agp1",), writes=(("Ut", sl),))
                    b.dmaop("io", lambda E, s, j=j, sl=sl: E.dma_start(
                                out=Xr[sl][:].rearrange("p (r q) -> p r q", r=16),
                                in_=d_agp1[j * PUB1 + 1024: j * PUB1 + 1024 + MC, 0:1024]
                                    .rearrange("(r p) q -> p r q", p=128)).then_inc(s, 16),
                            reads=("agp1",), writes=(("Xr", sl),))
                    b.dmaop("io", lambda E, s, j=j, sl=sl: E.dma_start(
                                out=Vb[sl][:].rearrange("p (r q) -> p r q", r=16),
                                in_=d_p2out[(j * 2 + 0) * MC: (j * 2 + 1) * MC, :]
                                    .rearrange("(r p) q -> p r q", p=128)).then_inc(s, 16),
                            reads=("p2out",), writes=(("Vb", sl),))
                    b.dmaop("io", lambda E, s, j=j, sl=sl: E.dma_start(
                                out=Fb[sl][:].rearrange("p (r q) -> p r q", r=16),
                                in_=d_p2out[(j * 2 + 1) * MC: (j * 2 + 2) * MC, :]
                                    .rearrange("(r p) q -> p r q", p=128)).then_inc(s, 16),
                            reads=("p2out",), writes=(("Fb", sl),))
                    for g in range(4):
                        def emm(E, s, g=g, sl=sl):
                            ins = None
                            for c in range(NCK):
                                ins = E.matmul(ps_m[g % 2][:, 0:512],
                                               Zb[:, c * 128:(c + 1) * 128],
                                               Ut[sl][:, c * MC + g * 512: c * MC + (g + 1) * 512],
                                               start=(c == 0), stop=(c == NCK - 1))
                            ins.then_inc(s, 1)
                        b.op("te", emm, reads=(("Zb",), ("Ut", sl)), writes=(("ps_m", g % 2),))
                        b.op("ve", lambda E, s, g=g: E.tensor_copy(
                                UZt[:, g * 512:(g + 1) * 512], ps_m[g % 2][:, 0:512]).then_inc(s, 1),
                             reads=(("ps_m", g % 2),), writes=(("UZt", g),))
                    for g in range(4):
                        b.dmaop("tp", lambda E, s, g=g: E.dma_start(
                                    out=UZr[:, g * 512:(g + 1) * 512]
                                        .rearrange("p (t q) -> p t q", t=4),
                                    in_=UZt[:, g * 512:(g + 1) * 512]
                                        .rearrange("p (t q) -> p t q", t=4),
                                    transpose=True).then_inc(s, 16),
                                reads=(("UZt", g),), writes=(("UZr", g),))
                    uzk = tuple(("UZr", g) for g in range(4))
                    b.op("ve", lambda E, s, sl=sl: E.tensor_tensor(
                            Erow[:], UZr[:], Fb[sl][:], ALU.subtract).then_inc(s, 1),
                         reads=uzk + (("Fb", sl),), writes=(("Erow",),))
                    b.op("ve", lambda E, s, sl=sl: E.tensor_tensor(
                            Yrow[:], Erow[:], Vb[sl][:], ALU.add).then_inc(s, 1),
                         reads=(("Erow",), ("Vb", sl)), writes=(("Yrow",),))
                    for g in range(4):
                        b.dmaop("tp", lambda E, s, g=g: E.dma_start(
                                    out=Yt[:, g * 512:(g + 1) * 512]
                                        .rearrange("p (t q) -> p t q", t=4),
                                    in_=Yrow[:, g * 512:(g + 1) * 512]
                                        .rearrange("p (t q) -> p t q", t=4),
                                    transpose=True).then_inc(s, 16),
                                reads=(("Yrow",),), writes=(("Yt", g),))
                    b.op("ve", lambda E, s: E.tensor_copy(
                            Yb[:].rearrange("p (bb t) -> p bb t", bb=B),
                            Yt[:].rearrange("p (t bb) -> p bb t", bb=B)).then_inc(s, 1),
                         reads=tuple(("Yt", g) for g in range(4)), writes=(("Yb",),))
                    b.dmaop("io", lambda E, s, j=j: E.dma_start(
                                out=d_y.rearrange("p (bb t2) -> p bb t2", bb=B)
                                    [:, :, j * 128:(j + 1) * 128],
                                in_=Yb[:].rearrange("p (bb t) -> p bb t", bb=B)).then_inc(s, 16),
                            reads=(("Yb",),), writes=(("d_y", j),))
                    for bank in range(2):
                        for q4 in range(4):
                            dc = bank * 4 + q4
                            def zmm(E, s, dc=dc, bank=bank, q4=q4, sl=sl):
                                ins = None
                                for r in range(16):
                                    ins = E.matmul(ps_h[bank][:, q4 * 128:(q4 + 1) * 128],
                                                   Xr[sl][:, r * 1024 + dc * 128: r * 1024 + (dc + 1) * 128],
                                                   Erow[:, r * 128:(r + 1) * 128],
                                                   start=(r == 0), stop=(r == 15))
                                ins.then_inc(s, 1)
                            b.op("te", zmm, reads=(("Xr", sl), ("Erow",)),
                                 writes=(("ps_hz", bank, q4),))
                            b.op("ve", lambda E, s, dc=dc, bank=bank, q4=q4: E.scalar_tensor_tensor(
                                    Z[:, dc * 128:(dc + 1) * 128],
                                    ps_h[bank][:, q4 * 128:(q4 + 1) * 128], c_negc[:, 0:1],
                                    Z[:, dc * 128:(dc + 1) * 128],
                                    ALU.mult, ALU.add).then_inc(s, 1),
                                 reads=(("ps_hz", bank, q4), "negc", ("Z",)), writes=(("Z",),))
                    b.op("ve", lambda E, s: E.tensor_copy(Zb[:], Z[:]).then_inc(s, 1),
                         reads=(("Z",),), writes=(("Zb",),))
                for dst_i in range(NCORE):
                    for p_ in range(2):
                        bglob = dst_i + 8 * p_
                        b.dmaop("io", lambda E, s, dst_i=dst_i, p_=p_, bglob=bglob: E.dma_start(
                                    out=d_y3in[dst_i * 128:(dst_i + 1) * 128,
                                               p_ * SEQ:(p_ + 1) * SEQ],
                                    in_=d_y[:, bglob * SEQ:(bglob + 1) * SEQ]
                                    ).then_inc(s, 16),
                                reads=tuple(("d_y", j) for j in range(NCHUNK)),
                                writes=(("y3in", dst_i, p_),))
            b.ccop(lambda E, s: _a2a(E, s, d_y3in.opt(), d_y3out.opt()),
                   reads=tuple(("y3in", i, p_) for i in range(NCORE) for p_ in range(2)),
                   writes=("y3out",))

        def scan_post(R):
            with ExitStack() as px:
                yst = [px.enter_context(nc.sbuf_tensor(f"sc_yst{i}", [128, ST], BF16)) for i in range(2)]
                for c in range(NCK):
                    b.dmaop("io", lambda E, s, c=c, R=R: E.dma_start(
                                out=R[:, c * TOK:(c + 1) * TOK],
                                in_=d_rsp[c * 128:(c + 1) * 128, :]).then_inc(s, 16),
                            reads=(("rsp", c), "y3out"), writes=(("R", c),))
                k = 0
                for c in range(NCK):
                    for st_i in range(NST):
                        m0 = st_i * ST
                        sl = k % 2
                        k += 1
                        b.dmaop("io", lambda E, s, c=c, m0=m0, sl=sl: E.dma_start(
                                    out=yst[sl][:],
                                    in_=d_y3out[c * 128:(c + 1) * 128, m0:m0 + ST]).then_inc(s, 16),
                                reads=("y3out",), writes=(("yst", sl),))
                        b.op("ve", lambda E, s, c=c, m0=m0, sl=sl, R=R: E.tensor_tensor(
                                R[:, c * TOK + m0: c * TOK + m0 + ST],
                                R[:, c * TOK + m0: c * TOK + m0 + ST],
                                yst[sl][:], ALU.add).then_inc(s, 1),
                             reads=(("yst", sl), ("R", c)), writes=(("R", c),))

        def attn_phase():
            SCL = 1.0 / math.sqrt(2.0 * D)
            with ExitStack() as px:
                pb = lambda nm, shape, dt: px.enter_context(nc.sbuf_tensor(nm + "_at", shape, dt))
                t_rt = pb("t_rt", [128, NCK * ST], F32)
                t_f1a = pb("t_f1a", [128, ST], F32)
                xnq = pb("xnq", [128, NCK * ST], BF16)
                qkvT = pb("qkvT", [128, 24 * ST], BF16)
                wq = [pb(f"wq{i}", [128, NCK * 128], BF16) for i in range(2)]
                Qh = pb("Qh", [128, SEQ], BF16)
                Kh = pb("Kh", [128, SEQ], BF16)
                Vh = pb("Vh", [128, SEQ], BF16)
                Vrow = pb("Vrow", [128, 16 * 64], BF16)
                Vaug = pb("Vaug", [128, 16 * 65], BF16)
                Pbuf = pb("Pbuf", [128, 16 * 512], BF16)
                oh = pb("oh", [128, SEQ], BF16)
                ohg = pb("ohg", [128, SEQ], BF16)
                gst = pb("gst", [128, SEQ], BF16)
                ost = pb("ost", [128, 512], F32)
                rsum = pb("rsum", [1, 512], F32)
                aacc = pb("aacc", [128, NCK * ST], BF16)
                c_m4 = pb("c_m4", [128, 1], F32)
                b.op("ve", lambda E, s: E.memset(c_m4[:], -4.0).then_inc(s, 1), writes=("m4",))

                for st_i in range(NST):
                    m0 = st_i * ST
                    hh0 = (m0 % SEQ) // 128
                    b.dmaop("io", lambda E, s, m0=m0: E.dma_start(
                                out=t_rt[:].rearrange("p (c m) -> p c m", c=NCK),
                                in_=d_rsp.rearrange("(c p) m -> p c m", p=128)
                                    [:, :, m0:m0 + ST]).then_inc(s, 16),
                            reads=("rspv",), writes=(("rt",),))
                    bc_r, bc_n = emit_ln("atln", ST, lambda c: ("rt",),
                                         lambda c: t_rt[:, c * ST:(c + 1) * ST], False)
                    for c in range(NCK):
                        b.op("ve", lambda E, s, c=c: E.tensor_tensor(
                                t_f1a[:, 0:ST], t_rt[:, c * ST:(c + 1) * ST],
                                ps_bc[0][:, 0:ST], ALU.mult).then_inc(s, 1),
                             reads=(("rt",), bc_r), writes=("t_f1a",))
                        b.op("ve", lambda E, s, c=c: E.tensor_tensor(
                                xnq[:, c * ST:(c + 1) * ST], t_f1a[:, 0:ST],
                                ps_bc[1][:, 0:ST], ALU.add).then_inc(s, 1),
                             reads=("t_f1a", bc_n), writes=("xnq",))
                    for ec in range(24):
                        w = wq[ec % 2]
                        b.dmaop("gw", lambda E, s, ec=ec, w=w: E.dma_start(
                                    out=w[:].rearrange("p (c q) -> p c q", c=NCK),
                                    in_=qkv_k[:, ec * 128:(ec + 1) * 128]
                                        .rearrange("(c p) q -> p c q", p=128)).then_inc(s, 16),
                                reads=(), writes=(("wq", ec % 2),))
                        b.op("te", lambda E, s, ec=ec, w=w: _group_mm(
                                E, s, ps_m[ec % 2][:, 0:ST],
                                [(w[:, c * 128:(c + 1) * 128], xnq[:, c * ST:(c + 1) * ST])
                                 for c in range(NCK)]),
                             reads=("xnq", ("wq", ec % 2)), writes=(("ps_m", ec % 2),))
                        b.op("se", lambda E, s, ec=ec: E.activation(
                                out=qkvT[:, ec * ST:(ec + 1) * ST], in_=ps_m[ec % 2][:, 0:ST],
                                func=ACT.Identity, bias=s_qkvb[:, ec:ec + 1], scale=1.0).then_inc(s, 1),
                             reads=(("ps_m", ec % 2), "qkvb"), writes=("qkvT",))
                    for hi in range(4):
                        for mat, off, dst in (("q", 0, Qh), ("k", 64, Kh), ("v", 128, Vh)):
                            for par in range(2):
                                f0 = 192 * par + off
                                ec0, pp0 = f0 // 128, f0 % 128
                                b.dmaop("tp", lambda E, s, hi=hi, par=par, ec0=ec0, pp0=pp0:
                                        E.dma_start(
                                            out=gst[0:64, par * 1024:(par + 1) * 1024]
                                                .rearrange("p (gg t) -> p gg t", gg=8),
                                            in_=qkvT[pp0:pp0 + 64, :]
                                                .rearrange("p (e m) -> p e m", e=24)
                                                [:, ec0::3, hi * 128:(hi + 1) * 128]).then_inc(s, 16),
                                        reads=("qkvT",), writes=(("gst", par),))
                            b.op("ve", lambda E, s, dst=dst: E.tensor_copy(
                                    dst[0:64, :].rearrange("p (t gg par) -> p gg par t", gg=8, par=2),
                                    gst[0:64, :].rearrange("p (par gg t) -> p gg par t", par=2, gg=8)
                                    ).then_inc(s, 1),
                                 reads=(("gst", 0), ("gst", 1)), writes=((mat,),))
                        b.dmaop("tp", lambda E, s: E.dma_start(
                                    out=Vrow[:].rearrange("p (t q) -> p t q", t=16),
                                    in_=Vh[0:64, :].rearrange("p (t q) -> p t q", t=16),
                                    transpose=True).then_inc(s, 16),
                                reads=(("v",),), writes=("Vrow",))
                        b.op("ve", lambda E, s: E.tensor_copy(
                                Vaug[:].rearrange("p (t q) -> p t q", q=65)[:, :, 0:64],
                                Vrow[:].rearrange("p (t q) -> p t q", q=64)).then_inc(s, 1),
                             reads=("Vrow",), writes=("Vaug",))
                        b.op("ve", lambda E, s: E.memset(
                                Vaug[:].rearrange("p (t q) -> p t q", q=65)[:, :, 64:65],
                                1.0).then_inc(s, 1),
                             reads=(), writes=("Vaug",))
                        for qt in range(4):
                            nkc = 4 * qt + 4
                            for kc in range(nkc):
                                b.op("te", lambda E, s, kc=kc, qt=qt: E.matmul(
                                        ps_m[kc % 2][:, 0:512],
                                        Kh[0:64, kc * 128:(kc + 1) * 128],
                                        Qh[0:64, qt * 512:(qt + 1) * 512],
                                        start=True, stop=True).then_inc(s, 1),
                                     reads=(("q",), ("k",)), writes=(("ps_m", kc % 2),))
                                b.op("se", lambda E, s, kc=kc, qt=qt: E.activation(
                                        out=Pbuf[:, kc * 512:(kc + 1) * 512],
                                        in_=ps_m[kc % 2][:, 0:512], func=ACT.Exp,
                                        bias=c_m4[:, 0:1], scale=SCL).then_inc(s, 1),
                                     reads=(("ps_m", kc % 2), "m4"), writes=("Pbuf",))
                            for j in range(4):
                                kc = 4 * qt + j
                                b.op("ve", lambda E, s, kc=kc, j=j: E.tensor_tensor(
                                        Pbuf[:, kc * 512 + j * 128: kc * 512 + (j + 1) * 128],
                                        Pbuf[:, kc * 512 + j * 128: kc * 512 + (j + 1) * 128],
                                        c_tri[:], ALU.mult).then_inc(s, 1),
                                     reads=("Pbuf", "tri"), writes=("Pbuf",))
                                if j > 0:
                                    b.op("ve", lambda E, s, kc=kc, j=j: E.memset(
                                            Pbuf[:, kc * 512: kc * 512 + j * 128], 0.0).then_inc(s, 1),
                                         reads=(), writes=("Pbuf",))
                            def pvmm(E, s, qt=qt, nkc=nkc):
                                ins = None
                                for kc in range(nkc):
                                    ins = E.matmul(ps_h[qt % 2][0:65, 0:512],
                                                   Vaug[:, kc * 65:(kc + 1) * 65],
                                                   Pbuf[:, kc * 512:(kc + 1) * 512],
                                                   start=(kc == 0), stop=(kc == nkc - 1))
                                ins.then_inc(s, 1)
                            b.op("te", pvmm, reads=("Vaug", "Pbuf"), writes=(("ps_h", qt % 2),))
                            b.op("ve", lambda E, s, qt=qt: E.reciprocal(
                                    rsum[0:1, 0:512], ps_h[qt % 2][64:65, 0:512]).then_inc(s, 1),
                                 reads=(("ps_h", qt % 2),), writes=("rsum",))
                            b.op("te", lambda E, s: E.matmul(
                                    ps_bc[0][0:64, 0:512], c_ones_row[0:1, 0:64], rsum[0:1, 0:512],
                                    start=True, stop=True).then_inc(s, 1),
                                 reads=("rsum", "ones_row"), writes=("bc_r",))
                            b.op("se", lambda E, s, qt=qt: E.activation(
                                    out=ost[0:64, 0:512], in_=ps_h[qt % 2][0:64, 0:512],
                                    func=ACT.Identity).then_inc(s, 1),
                                 reads=(("ps_h", qt % 2),), writes=("ost",))
                            b.op("ve", lambda E, s, qt=qt: E.tensor_tensor(
                                    oh[0:64, qt * 512:(qt + 1) * 512], ost[0:64, 0:512],
                                    ps_bc[0][0:64, 0:512], ALU.mult).then_inc(s, 1),
                                 reads=("ost", "bc_r"), writes=("oh",))
                        b.op("ve", lambda E, s: E.tensor_copy(
                                ohg[0:64, :].rearrange("p (g t) -> p g t", g=16),
                                oh[0:64, :].rearrange("p (t g) -> p g t", g=16)).then_inc(s, 1),
                             reads=("oh",), writes=("ohg",))
                        for par in range(2):
                            b.dmaop("tp", lambda E, s, hi=hi, par=par: E.dma_start(
                                        out=aacc[par * 64:par * 64 + 64, :]
                                            .rearrange("p (e m) -> p e m", e=NCK)
                                            [:, :, hi * 128:(hi + 1) * 128],
                                        in_=ohg[0:64, :]
                                            .rearrange("p (gh par2 t) -> p gh par2 t", gh=8, par2=2)
                                            [:, :, par, :]).then_inc(s, 16),
                                    reads=("ohg",), writes=("aacc",))
                    for c in range(NCK):
                        b.op("ve", lambda E, s, c=c: E.tensor_tensor(
                                t_rt[:, c * ST:(c + 1) * ST], t_rt[:, c * ST:(c + 1) * ST],
                                aacc[:, c * ST:(c + 1) * ST], ALU.add).then_inc(s, 1),
                             reads=(("rt",), "aacc"), writes=(("rt",),))
                    b.dmaop("io", lambda E, s, m0=m0: E.dma_start(
                                out=d_rsp.rearrange("(c p) m -> p c m", p=128)
                                    [:, :, m0:m0 + ST],
                                in_=t_rt[:].rearrange("p (c m) -> p c m", c=NCK)).then_inc(s, 16),
                            reads=(("rt",),), writes=(("rspv",),))

        def spill_R(R):
            for c in range(NCK):
                b.dmaop("io", lambda E, s, c=c, R=R: E.dma_start(
                            out=d_rsp[c * 128:(c + 1) * 128, :],
                            in_=R[:, c * TOK:(c + 1) * TOK]).then_inc(s, 16),
                        reads=(("R", c),), writes=(("rsp", c),))
            b.op("ve", lambda E, s: E.engine_nop().then_inc(s, 1),
                 reads=tuple(("rsp", c) for c in range(NCK)), writes=("rspv",))

        def load_x(R):
            for c in range(NCK):
                b.dmaop("io", lambda E, s, c=c, R=R: E.dma_start(
                            out=R[:, c * TOK:(c + 1) * TOK],
                            in_=xT[c * 128:(c + 1) * 128, :]).then_inc(s, 16),
                        reads=(), writes=(("R", c),))

        if PH_SCAN:
            with ExitStack() as e1:
                R1 = e1.enter_context(nc.sbuf_tensor("R1", [128, NCK * TOK], F32))
                load_x(R1)
                if PH_L0FA:
                    l0ffn_phase(R1)
                scan_pre(R1)
            b.global_barrier()
            pk = scan_phase_a()
            b.global_barrier()
            scan_phase_b(pk)
            b.global_barrier()
            with ExitStack() as e2:
                R2 = e2.enter_context(nc.sbuf_tensor("R2", [128, NCK * TOK], F32))
                scan_post(R2)
                if PH_L0FB:
                    l0ffn_phase(R2)
                spill_R(R2)
            b.global_barrier()
        else:
            with ExitStack() as e1:
                R1 = e1.enter_context(nc.sbuf_tensor("R1", [128, NCK * TOK], F32))
                load_x(R1)
                if PH_L0FA:
                    l0ffn_phase(R1)
                spill_R(R1)
            b.global_barrier()

        if PH_L1FA:
            l1ffn_phase()
            b.global_barrier()
        if PH_ATTN:
            attn_phase()
            b.global_barrier()
        if PH_L1FB:
            l1ffn_phase()
            b.global_barrier()

        for c in range(NCK):
            b.dmaop("io", lambda E, s, c=c: E.dma_start(
                        out=out_T[c * 128:(c + 1) * 128, :],
                        in_=d_rsp[c * 128:(c + 1) * 128, :]).then_inc(s, 16),
                    reads=("rspv",), writes=(("out", c),))
        for c in range(NCK):
            b.raw_wait("sy", ("out", c))

        block = ctx.enter_context(nc.Block())
        b.emit_all(block, sems)
    return nc


def _group_mm(E, s, outap, pairs):
    n = len(pairs)
    for i, (l, r) in enumerate(pairs):
        ins = E.matmul(outap, l, r, start=(i == 0), stop=(i == n - 1))
    ins.then_inc(s, 1)




def _bernoulli_threefry(seed, p, n):
    rot = [[13, 15, 26, 6], [17, 29, 16, 24]]
    ks0, ks1 = np.uint32(0), np.uint32(seed)
    ks2 = np.uint32(ks0 ^ ks1 ^ np.uint32(0x1BD11BDA))
    i = np.arange(n, dtype=np.uint64)
    x0 = (i >> np.uint64(32)).astype(np.uint32)
    x1 = (i & np.uint64(0xFFFFFFFF)).astype(np.uint32)
    x0 = (x0 + ks0).astype(np.uint32)
    x1 = (x1 + ks1).astype(np.uint32)
    ks = [ks0, ks1, ks2]

    def rotl(x, r):
        r = np.uint32(r)
        return ((x << r) | (x >> np.uint32(32 - r))).astype(np.uint32)

    for d in range(5):
        for r in rot[d % 2]:
            x0 = (x0 + x1).astype(np.uint32)
            x1 = rotl(x1, r)
            x1 = (x1 ^ x0).astype(np.uint32)
        x0 = (x0 + ks[(d + 1) % 3]).astype(np.uint32)
        x1 = (x1 + ks[(d + 2) % 3] + np.uint32(d + 1)).astype(np.uint32)
    bits = x0 ^ x1
    u = ((bits >> np.uint32(9)) | np.uint32(0x3F800000)).view(np.float32) - np.float32(1.0)
    return u < np.float32(p)


def _prep_inputs(kw):
    f32 = np.float32
    x = np.asarray(kw["L0_attn_x"], f32)
    W0 = np.asarray(kw["L0_attn_W0"], f32)
    Wkv = np.asarray(kw["L0_attn_Wkv"], f32)
    bkv = np.asarray(kw["L0_attn_bkv"], f32)
    eta = np.asarray(kw["L0_attn_eta"], f32).reshape(1, 1)

    def pcol(v):
        return np.ascontiguousarray(np.asarray(v, f32).reshape(-1, 128).T)

    C = np.ascontiguousarray
    keep = _bernoulli_threefry(42, 0.8, B * SEQ * D).reshape(B, SEQ, D)
    maskv = (keep.astype(np.float32) * 1.25).astype(ml_dtypes.bfloat16)

    kq = np.arange(128)
    tri = (kq[:, None] <= kq[None, :]).astype(np.float32)
    tb = np.arange(128) // B
    msl_np = (tb[:, None] > tb[None, :]).astype(np.float32)

    common = {
        "attn_wT": C(np.asarray(kw["L0_ffn_attn_w"], f32).T),
        "next_wT": C(np.asarray(kw["L0_ffn_next_w"], f32).T),
        "ln_s": pcol(kw["L0_ffn_ln_scale"]), "ln_b": pcol(kw["L0_ffn_ln_bias"]),
        "attn_b": pcol(kw["L0_ffn_attn_b"]), "next_b": pcol(kw["L0_ffn_next_b"]),
        "wkv_vT": C(Wkv[D:, :].T), "bkv_row": C(bkv[D:].reshape(1, D)),
        "eta_in": eta,
        "qkv_k": C(np.asarray(kw["L1_attn_qkv_kernel"], f32)),
        "qkv_b": pcol(np.asarray(kw["L1_attn_qkv_bias"], f32).reshape(3 * D)),
        "W1T": C(np.asarray(kw["L1_ffn_W1"], f32).T),
        "W2T": C(np.asarray(kw["L1_ffn_W2"], f32).T),
        "l1b1": pcol(kw["L1_ffn_b1"]), "l1b2": pcol(kw["L1_ffn_b2"]),
        "ones_col": np.ones((128, 1), f32),
        "ones_colb": np.ones((128, 1), ml_dtypes.bfloat16),
        "ones_row": np.ones((1, 128), f32),
        "ones_rowb": np.ones((1, 128), ml_dtypes.bfloat16),
        "identb": np.eye(128, dtype=ml_dtypes.bfloat16),
        "msl": msl_np.astype(ml_dtypes.bfloat16),
        "msu": msl_np.T.astype(ml_dtypes.bfloat16),
        "trimask": tri.astype(ml_dtypes.bfloat16),
    }
    w1 = np.asarray(kw["L0_ffn_w1"], f32)
    w2 = np.asarray(kw["L0_ffn_w2"], f32)
    b1 = np.asarray(kw["L0_ffn_b1"], f32)
    b2 = np.asarray(kw["L0_ffn_b2"], f32)
    in_maps = []
    for i in range(NCORE):
        bpair = [i, i + 8]
        m = dict(common)
        m["xT"] = C(x[bpair].reshape(TOK, D).T)
        m["w1e"] = C(w1[i])
        m["w2e"] = C(w2[i])
        m["b1e"] = pcol(b1[i])
        m["b2e"] = pcol(b2[i])
        m["z0"] = C(W0[i * 128:(i + 1) * 128, :].T)
        m["maskT"] = C(maskv[bpair].reshape(TOK, D).T)
        in_maps.append(m)
    return in_maps


def kernel(**inputs):
    in_maps = _prep_inputs(inputs)
    nc = build_program()
    res = run_bass_kernel_spmd(nc, in_maps, core_ids=list(range(NCORE)))
    out = np.empty((B, SEQ, D), np.float32)
    for i in range(NCORE):
        oT = res.results[i]["outT"]
        out[[i, i + 8]] = np.asarray(oT).T.reshape(2, SEQ, D)
    return out


# version 04
# speedup vs baseline: 1.0043x; 1.0043x over previous
import math
import numpy as np
import ml_dtypes
from contextlib import ExitStack

import concourse.bass as bass
import concourse.mybir as mybir
from concourse.bass_utils import run_bass_kernel_spmd

F32 = mybir.dt.float32
BF16 = mybir.dt.bfloat16
I32 = mybir.dt.int32
ALU = mybir.AluOpType
ACT = mybir.ActivationFunctionType
AX = mybir.AxisListType.X

NCORE = 8
B, SEQ, D = 16, 2048, 1024
TOK = 2 * SEQ
NCK = D // 128
ST = 512
NST = TOK // ST
H0 = 4 * D
NH0 = H0 // 128
EPS = 1e-5
INV_D = 1.0 / D

KSTEP = 128
MC = KSTEP * B
NCHUNK = SEQ // KSTEP
SUBS = 8
SBR = SUBS * B
NSUB = KSTEP // SUBS

PH_L0FA = True
PH_SCAN = True
PH_L0FB = True
PH_L1FA = True
PH_ATTN = True
PH_L1FB = True

ENGS = ("te", "ve", "se", "ge", "sy")
ENG_ATTR = {"te": "tensor", "ve": "vector", "se": "scalar", "ge": "gpsimd", "sy": "sync"}


class Bld:

    def __init__(self, nc):
        self.nc = nc
        self.ops = {e: [] for e in ENGS}
        self.count = {e: 0 for e in ENGS}
        self.credits = {e: {} for e in ENGS}
        self.tickets = {}
        self.readers = {}
        self.sem_names = ["E_" + e for e in ENGS] + ["E_cc"]
        self.dma = {}
        self.cc_count = 0

    def add_dma_stream(self, name, eng, ring):
        sems = [f"D_{name}_{i}" for i in range(ring)]
        self.sem_names += sems
        self.dma[name] = dict(eng=eng, ring=sems, idx=0, cnt=[0] * ring)

    def _wait(self, eng, sem, val):
        if self.credits[eng].get(sem, 0) >= val:
            return
        self.credits[eng][sem] = val
        self.ops[eng].append(("w", sem, val))

    def _dep(self, eng, reads, writes):
        for k in reads:
            t = self.tickets.get(k)
            if t is not None:
                self._wait(eng, t[0], t[1])
        for k in writes:
            t = self.tickets.get(k)
            if t is not None:
                self._wait(eng, t[0], t[1])
            for sn, v in self.readers.get(k, {}).items():
                self._wait(eng, sn, v)

    def op(self, eng, emit, reads=(), writes=()):
        self._dep(eng, reads, writes)
        self.count[eng] += 1
        n = self.count[eng]
        sem = "E_" + eng
        self.ops[eng].append(("o", emit, sem))
        for k in reads:
            self.readers.setdefault(k, {})[sem] = n
        for k in writes:
            self.tickets[k] = (sem, n)
            self.readers[k] = {}
        return n

    def dmaop(self, stream, emit, reads=(), writes=()):
        st = self.dma[stream]
        eng = st["eng"]
        slot = st["idx"] % len(st["ring"])
        st["idx"] += 1
        semn = st["ring"][slot]
        st["cnt"][slot] += 16
        target = st["cnt"][slot]
        self._dep(eng, reads, writes)
        if target > 16:
            self._wait(eng, semn, target - 16)
        self.ops[eng].append(("d", emit, semn))
        for k in reads:
            self.readers.setdefault(k, {})[semn] = target
        for k in writes:
            self.tickets[k] = (semn, target)
            self.readers[k] = {}
        return semn, target

    def ccop(self, emit, reads=(), writes=()):
        self._dep("ge", reads, writes)
        self.cc_count += 1
        n = self.cc_count
        self.ops["ge"].append(("o", emit, "E_cc"))
        for k in reads:
            self.readers.setdefault(k, {})["E_cc"] = n
        for k in writes:
            self.tickets[k] = ("E_cc", n)
            self.readers[k] = {}
        return n

    def global_barrier(self):
        targets = [("E_" + e, self.count[e]) for e in ENGS if self.count[e] > 0]
        targets.append(("E_cc", self.cc_count))
        for st in self.dma.values():
            for i, sn in enumerate(st["ring"]):
                if st["cnt"][i] > 0:
                    targets.append((sn, st["cnt"][i]))
        for e in ENGS:
            for sn, v in targets:
                if v > 0:
                    self._wait(e, sn, v)

    def raw_wait(self, eng, key):
        t = self.tickets[key]
        self._wait(eng, t[0], t[1])

    def emit_all(self, block, sems):
        for e in ENGS:
            oplist = self.ops[e]

            def make_body(oplist=oplist):
                def body(E):
                    for item in oplist:
                        if item[0] == "w":
                            E.wait_ge(sems[item[1]], item[2])
                        else:
                            item[1](E, sems[item[2]])
                return body

            getattr(block, ENG_ATTR[e])(make_body())


def build_program():
    nc = bass.Bass("TRN2", target_bir_lowering=False, debug=False)

    def dram_in(name, shape, dt=F32):
        return nc.dram_tensor(name, list(shape), dt, kind="ExternalInput").ap()

    xT = dram_in("xT", [D, TOK])
    attn_wT = dram_in("attn_wT", [D, D])
    next_wT = dram_in("next_wT", [D, D])
    w1e = dram_in("w1e", [D, H0])
    w2e = dram_in("w2e", [H0, D])
    ln_s = dram_in("ln_s", [128, NCK])
    ln_b = dram_in("ln_b", [128, NCK])
    attn_b = dram_in("attn_b", [128, NCK])
    next_b = dram_in("next_b", [128, NCK])
    b1e = dram_in("b1e", [128, NH0])
    b2e = dram_in("b2e", [128, NCK])
    wkv_vT = dram_in("wkv_vT", [D, D])
    bkv_row = dram_in("bkv_row", [1, D])
    z0 = dram_in("z0", [D, 128])
    eta_in = dram_in("eta_in", [1, 1])
    qkv_k = dram_in("qkv_k", [D, 3 * D])
    qkv_b = dram_in("qkv_b", [128, 24])
    W1T = dram_in("W1T", [D, H0])
    W2T = dram_in("W2T", [H0, D])
    l1b1 = dram_in("l1b1", [128, NH0])
    l1b2 = dram_in("l1b2", [128, NCK])
    maskT = dram_in("maskT", [D, TOK], BF16)
    ones_col = dram_in("ones_col", [128, 1])
    ones_colb = dram_in("ones_colb", [128, 1], BF16)
    ones_row = dram_in("ones_row", [1, 128])
    ones_rowb = dram_in("ones_rowb", [1, 128], BF16)
    identb = dram_in("identb", [128, 128], BF16)
    msl = dram_in("msl", [128, 128], BF16)
    msu = dram_in("msu", [128, 128], BF16)
    trimask = dram_in("trimask", [128, 128], BF16)

    out_T = nc.dram_tensor("outT", [D, TOK], F32, kind="ExternalOutput").ap()

    d_x1in = nc.dram_tensor("d_x1in", [NCORE * D, 512], BF16).ap()
    d_x1out = nc.dram_tensor("d_x1out", [NCORE * D, 512], BF16).ap()
    PUB1 = D + MC
    d_pub1 = nc.dram_tensor("d_pub1", [2 * PUB1, MC], BF16).ap()
    d_agp1 = nc.dram_tensor("d_agp1", [NCORE * 2 * PUB1, MC], BF16).ap()
    d_p2in = nc.dram_tensor("d_p2in", [NCORE * 2 * 2 * MC, 128], BF16).ap()
    d_p2out = nc.dram_tensor("d_p2out", [NCORE * 2 * 2 * MC, 128], BF16).ap()
    d_y = nc.dram_tensor("d_y", [128, B * SEQ], BF16).ap()
    d_y3in = nc.dram_tensor("d_y3in", [NCORE * 128, TOK], BF16).ap()
    d_y3out = nc.dram_tensor("d_y3out", [NCORE * 128, TOK], BF16).ap()
    d_rsp = nc.dram_tensor("d_rsp", [D, TOK], F32).ap()
    d_xq = nc.dram_tensor("d_xq", [D, TOK], BF16).ap()
    d_w1q = nc.dram_tensor("d_w1q", [D, H0], BF16).ap()
    d_w2q = nc.dram_tensor("d_w2q", [H0, D], BF16).ap()

    b = Bld(nc)
    b.add_dma_stream("io", "sy", 4)
    b.add_dma_stream("wt", "se", 4)
    b.add_dma_stream("gw", "ge", 6)
    b.add_dma_stream("tp", "sy", 4)

    with ExitStack() as ctx:
        sems = {sn: ctx.enter_context(nc.semaphore(sn)) for sn in b.sem_names}
        sbuf = lambda name, shape, dt: ctx.enter_context(nc.sbuf_tensor(name, shape, dt))
        pst = lambda name, shape: ctx.enter_context(nc.psum_tensor(name, shape, F32))

        c_ones = sbuf("c_ones", [128, 1], F32)
        c_onesb = sbuf("c_onesb", [128, 1], BF16)
        c_ones_row = sbuf("c_ones_row", [1, 128], F32)
        c_ones_rowb = sbuf("c_ones_rowb", [1, 128], BF16)
        c_ident = sbuf("c_ident", [128, 128], BF16)
        c_msl = sbuf("c_msl", [128, 128], BF16)
        c_msu = sbuf("c_msu", [128, 128], BF16)
        c_tri = sbuf("c_tri", [128, 128], BF16)
        s_lns = sbuf("s_lns", [128, NCK], F32)
        s_lnb = sbuf("s_lnb", [128, NCK], F32)
        s_attnb = sbuf("s_attnb", [128, NCK], F32)
        s_nextb = sbuf("s_nextb", [128, NCK], F32)
        s_b1 = sbuf("s_b1", [128, NH0], F32)
        s_b2 = sbuf("s_b2", [128, NCK], F32)
        s_qkvb = sbuf("s_qkvb", [128, 24], F32)
        s_l1b1 = sbuf("s_l1b1", [128, NH0], F32)
        s_l1b2 = sbuf("s_l1b2", [128, NCK], F32)
        s_eta = sbuf("s_eta", [1, 1], F32)
        c_negc = sbuf("c_negc", [128, 1], F32)
        s_l1b1q = sbuf("s_l1b1q", [128, NH0], F32)
        s_l1b2q = sbuf("s_l1b2q", [128, NCK], F32)

        r_mu = sbuf("r_mu", [1, ST], F32)
        r_var = sbuf("r_var", [1, ST], F32)
        r_rstd = sbuf("r_rstd", [1, ST], F32)
        r_nmur = sbuf("r_nmur", [1, ST], F32)

        ps_m = [pst(f"ps_m{i}", [128, 512]) for i in range(2)]
        ps_h = [pst(f"ps_h{i}", [128, 512]) for i in range(2)]
        ps_sum = pst("ps_sum", [1, ST])
        ps_sq = pst("ps_sq", [1, ST])
        ps_bc = [pst(f"ps_bc{i}", [128, ST]) for i in range(2)]

        def ld(stream, dst, src, key):
            b.dmaop(stream, lambda E, s, d=dst, so=src: E.dma_start(out=d, in_=so).then_inc(s, 16),
                    reads=(), writes=(key,))

        ld("io", c_ones[:], ones_col, "ones")
        ld("io", c_onesb[:], ones_colb, "onesb")
        ld("io", c_ones_row[:], ones_row, "ones_row")
        ld("io", c_ones_rowb[:], ones_rowb, "ones_rowb")
        ld("io", c_ident[:], identb, "ident")
        ld("io", c_msl[:], msl, "msl")
        ld("io", c_msu[:], msu, "msu")
        ld("io", c_tri[:], trimask, "tri")
        ld("io", s_lns[:], ln_s, "lns")
        ld("io", s_lnb[:], ln_b, "lnb")
        ld("io", s_attnb[:], attn_b, "attnb")
        ld("io", s_nextb[:], next_b, "nextb")
        ld("io", s_b1[:], b1e, "b1")
        ld("io", s_b2[:], b2e, "b2")
        ld("io", s_qkvb[:], qkv_b, "qkvb")
        ld("io", s_l1b1[:], l1b1, "l1b1")
        ld("io", s_l1b2[:], l1b2, "l1b2")
        ld("io", s_eta[:], eta_in, "eta")

        b.op("te", lambda E, s: E.matmul(ps_bc[0][:, 0:1], c_ones_row[:], s_eta[:],
                                         start=True, stop=True).then_inc(s, 1),
             reads=("ones_row", "eta"), writes=("bc_r",))
        b.op("ve", lambda E, s: E.tensor_scalar(c_negc[:], ps_bc[0][:, 0:1], -2.0 / (B * D), None,
                                                ALU.mult).then_inc(s, 1),
             reads=("bc_r",), writes=("negc",))

        sq_slots = [sbuf(f"sqs{i}", [128, ST], F32) for i in range(1)]

        def emit_ln(tag, w, src_key, src_ap, src_bf16):
            ones_ap = c_onesb[:] if src_bf16 else c_ones[:]
            ones_key = "onesb" if src_bf16 else "ones"
            for c in range(NCK):
                b.op("te", lambda E, s, c=c: E.matmul(
                        ps_sum[0:1, 0:w], ones_ap, src_ap(c),
                        start=(c == 0), stop=(c == NCK - 1)).then_inc(s, 1),
                     reads=(src_key(c), ones_key), writes=("ps_sum",))
                sl = sq_slots[0]
                b.op("se", lambda E, s, c=c, sl=sl: E.activation(
                        out=sl[:, 0:w], in_=src_ap(c), func=ACT.Square).then_inc(s, 1),
                     reads=(src_key(c),), writes=(("sqslot", 0),))
                b.op("te", lambda E, s, c=c, sl=sl: E.matmul(
                        ps_sq[0:1, 0:w], c_ones[:], sl[:, 0:w],
                        start=(c == 0), stop=(c == NCK - 1)).then_inc(s, 1),
                     reads=(("sqslot", 0), "ones"), writes=("ps_sq",))
            b.op("ve", lambda E, s: E.tensor_scalar(r_mu[0:1, 0:w], ps_sum[0:1, 0:w], INV_D, None,
                                                    ALU.mult).then_inc(s, 1),
                 reads=("ps_sum",), writes=("r_mu",))
            b.op("ve", lambda E, s: E.tensor_scalar(r_var[0:1, 0:w], ps_sq[0:1, 0:w], INV_D, EPS,
                                                    ALU.mult, ALU.add).then_inc(s, 1),
                 reads=("ps_sq",), writes=("r_var",))
            b.op("ve", lambda E, s: E.scalar_tensor_tensor(r_rstd[0:1, 0:w], r_mu[0:1, 0:w], 1.0,
                                                           r_mu[0:1, 0:w], ALU.mult, ALU.mult).then_inc(s, 1),
                 reads=("r_mu",), writes=("r_rstd",))
            b.op("ve", lambda E, s: E.tensor_tensor(r_var[0:1, 0:w], r_var[0:1, 0:w], r_rstd[0:1, 0:w],
                                                    ALU.subtract).then_inc(s, 1),
                 reads=("r_var", "r_rstd"), writes=("r_var",))
            b.op("se", lambda E, s: E.activation(out=r_rstd[0:1, 0:w], in_=r_var[0:1, 0:w],
                                                 func=ACT.Sqrt).then_inc(s, 1),
                 reads=("r_var",), writes=("r_rstd",))
            b.op("ve", lambda E, s: E.reciprocal(r_rstd[0:1, 0:w], r_rstd[0:1, 0:w]).then_inc(s, 1),
                 reads=("r_rstd",), writes=("r_rstd",))
            b.op("ve", lambda E, s: E.scalar_tensor_tensor(r_nmur[0:1, 0:w], r_mu[0:1, 0:w], -1.0,
                                                           r_rstd[0:1, 0:w], ALU.mult, ALU.mult).then_inc(s, 1),
                 reads=("r_mu", "r_rstd"), writes=("r_nmur",))
            b.op("te", lambda E, s: E.matmul(ps_bc[0][:, 0:w], c_ones_row[:], r_rstd[0:1, 0:w],
                                             start=True, stop=True).then_inc(s, 1),
                 reads=("r_rstd", "ones_row"), writes=("bc_r",))
            b.op("te", lambda E, s: E.matmul(ps_bc[1][:, 0:w], c_ones_row[:], r_nmur[0:1, 0:w],
                                             start=True, stop=True).then_inc(s, 1),
                 reads=("r_nmur", "ones_row"), writes=("bc_n",))
            return "bc_r", "bc_n"

        def emit_quant(pool, src_key, src_ap, dst_key, dst_ap, M, EMINB, signed, W):
            pf, pi = pool["f"], pool["i"]
            ebits, sbits, vbits, par = pi[0], pi[1], pi[2], pi[3]
            ef, t, cf, frac, g1, g2 = pf[0], pf[1], pf[2], pf[3], pf[4], pf[5]
            xb = src_ap.bitcast(I32)
            K = lambda nm: ("q", nm)

            def o(emit, reads, writes):
                b.op("ve", emit, reads=reads, writes=writes)

            o(lambda E, s: E.tensor_scalar(ebits[:, :W], xb, 23, None, ALU.logical_shift_right).then_inc(s, 1),
              [src_key], [K("eb")])
            o(lambda E, s: E.tensor_scalar(ebits[:, :W], ebits[:, :W], 255, None, ALU.bitwise_and).then_inc(s, 1),
              [K("eb")], [K("eb")])
            o(lambda E, s: E.tensor_copy(ef[:, :W], ebits[:, :W]).then_inc(s, 1), [K("eb")], [K("ef")])
            o(lambda E, s: E.tensor_scalar(ef[:, :W], ef[:, :W], float(EMINB), -(127.0 + M),
                                           ALU.max, ALU.add).then_inc(s, 1), [K("ef")], [K("ef")])
            o(lambda E, s: E.tensor_scalar(ef[:, :W], ef[:, :W], -126.0, None,
                                           ALU.max).then_inc(s, 1), [K("ef")], [K("ef")])
            o(lambda E, s: E.tensor_copy(ebits[:, :W], ef[:, :W]).then_inc(s, 1), [K("ef")], [K("eb")])
            o(lambda E, s: E.tensor_scalar(sbits[:, :W], ebits[:, :W], 127, None, ALU.add).then_inc(s, 1),
              [K("eb")], [K("sb")])
            o(lambda E, s: E.tensor_scalar(sbits[:, :W], sbits[:, :W], 23, None,
                                           ALU.logical_shift_left).then_inc(s, 1), [K("sb")], [K("sb")])
            o(lambda E, s: E.tensor_scalar(vbits[:, :W], ebits[:, :W], -1, 127, ALU.mult, ALU.add).then_inc(s, 1),
              [K("eb")], [K("vb")])
            o(lambda E, s: E.tensor_scalar(vbits[:, :W], vbits[:, :W], 23, None,
                                           ALU.logical_shift_left).then_inc(s, 1), [K("vb")], [K("vb")])
            if signed:
                o(lambda E, s: E.tensor_scalar(ebits[:, :W], xb, 0x7FFFFFFF, None,
                                               ALU.bitwise_and).then_inc(s, 1), [src_key], [K("ab")])
                asrc, akey = ebits[:, :W].bitcast(F32), K("ab")
            else:
                asrc, akey = src_ap, src_key
            o(lambda E, s, asrc=asrc: E.tensor_tensor(t[:, :W], asrc, vbits[:, :W].bitcast(F32),
                                                      ALU.mult).then_inc(s, 1),
              [akey, K("vb")], [K("t")])
            o(lambda E, s: E.tensor_copy(par[:, :W], t[:, :W]).then_inc(s, 1), [K("t")], [K("ci")])
            o(lambda E, s: E.tensor_copy(cf[:, :W], par[:, :W]).then_inc(s, 1), [K("ci")], [K("cf")])
            o(lambda E, s: E.tensor_tensor(frac[:, :W], t[:, :W], cf[:, :W], ALU.subtract).then_inc(s, 1),
              [K("t"), K("cf")], [K("fr")])
            o(lambda E, s: E.tensor_scalar(g1[:, :W], frac[:, :W], 0.5, None, ALU.is_gt).then_inc(s, 1),
              [K("fr")], [K("g1")])
            o(lambda E, s: E.tensor_scalar(g2[:, :W], frac[:, :W], 0.5, None, ALU.is_equal).then_inc(s, 1),
              [K("fr")], [K("g2")])
            o(lambda E, s: E.tensor_scalar(frac[:, :W], frac[:, :W], -0.5, None, ALU.is_equal).then_inc(s, 1),
              [K("fr")], [K("fr")])
            o(lambda E, s: E.tensor_tensor(g2[:, :W], g2[:, :W], frac[:, :W], ALU.subtract).then_inc(s, 1),
              [K("g2"), K("fr")], [K("g2")])
            o(lambda E, s: E.tensor_scalar(par[:, :W], par[:, :W], 1, None, ALU.bitwise_and).then_inc(s, 1),
              [K("ci")], [K("pi")])
            o(lambda E, s: E.tensor_copy(frac[:, :W], par[:, :W]).then_inc(s, 1), [K("pi")], [K("pf")])
            o(lambda E, s: E.tensor_tensor(g2[:, :W], g2[:, :W], frac[:, :W], ALU.mult).then_inc(s, 1),
              [K("g2"), K("pf")], [K("g2")])
            o(lambda E, s: E.tensor_tensor(g1[:, :W], g1[:, :W], g2[:, :W], ALU.add).then_inc(s, 1),
              [K("g1"), K("g2")], [K("g1")])
            o(lambda E, s: E.tensor_tensor(cf[:, :W], cf[:, :W], g1[:, :W], ALU.add).then_inc(s, 1),
              [K("cf"), K("g1")], [K("cf")])
            if signed:
                o(lambda E, s: E.tensor_scalar(vbits[:, :W], xb, -0x80000000, None,
                                               ALU.bitwise_and).then_inc(s, 1), [src_key], [K("sg")])
                o(lambda E, s: E.tensor_scalar(vbits[:, :W], vbits[:, :W], 0x3F800000, None,
                                               ALU.bitwise_or).then_inc(s, 1), [K("sg")], [K("sg")])
                o(lambda E, s: E.tensor_tensor(cf[:, :W], cf[:, :W], vbits[:, :W].bitcast(F32),
                                               ALU.mult).then_inc(s, 1), [K("cf"), K("sg")], [K("cf")])
            o(lambda E, s: E.tensor_tensor(dst_ap, cf[:, :W], sbits[:, :W].bitcast(F32),
                                           ALU.mult).then_inc(s, 1), [K("cf"), K("sb")], [dst_key])

        def emit_wquant(pool, src_dram, dst_dram, rows, cols, tag):
            W = min(2048, cols)
            tin = pool["f"][6]
            tout = pool["bf"]
            for r0 in range(0, rows, 128):
                for c0 in range(0, cols, W):
                    b.dmaop("io", lambda E, s, r0=r0, c0=c0, W=W: E.dma_start(
                                out=tin[:, 0:W], in_=src_dram[r0:r0 + 128, c0:c0 + W]).then_inc(s, 16),
                            reads=(), writes=(("wq_in", tag),))
                    emit_quant(pool, ("wq_in", tag), tin[:, 0:W], ("wq_out", tag), tout[:, 0:W],
                               2, 113.0, True, W)
                    b.dmaop("io", lambda E, s, r0=r0, c0=c0, W=W: E.dma_start(
                                out=dst_dram[r0:r0 + 128, c0:c0 + W], in_=tout[:, 0:W]).then_inc(s, 16),
                            reads=(("wq_out", tag),), writes=(("wq_st", tag),))

        l0buf = {}
        l0ctr = [0]

        def l0ffn_alloc(px):
            sfx = str(l0ctr[0])
            l0ctr[0] += 1
            l0buf["t_xin"] = px.enter_context(nc.sbuf_tensor("t_xin" + sfx, [128, NCK * ST], BF16))
            l0buf["t_f1"] = px.enter_context(nc.sbuf_tensor("t_f1" + sfx, [128, ST], F32))
            l0buf["t_xn"] = px.enter_context(nc.sbuf_tensor("t_xn" + sfx, [128, NCK * ST], BF16))
            l0buf["t_xa"] = px.enter_context(nc.sbuf_tensor("t_xa" + sfx, [128, NCK * ST], BF16))
            l0buf["t_h"] = px.enter_context(nc.sbuf_tensor("t_h" + sfx, [128, 8 * ST], BF16))
            l0buf["t_acc"] = px.enter_context(nc.sbuf_tensor("t_acc" + sfx, [128, NCK * ST], F32))
            l0buf["w_sq"] = [px.enter_context(nc.sbuf_tensor(f"w_sq{i}" + sfx, [128, NCK * 128], BF16))
                             for i in range(2)]
            l0buf["w_1"] = [px.enter_context(nc.sbuf_tensor(f"w_1{i}" + sfx, [128, NCK * 128], BF16))
                            for i in range(2)]
            l0buf["w2s"] = [px.enter_context(nc.sbuf_tensor(f"w2s{i}" + sfx, [128, D], BF16))
                            for i in range(4)]

        def l0ffn_supertile(m0, R):
            t_xin = l0buf["t_xin"]; t_f1 = l0buf["t_f1"]; t_xn = l0buf["t_xn"]
            t_xa = l0buf["t_xa"]; t_h = l0buf["t_h"]; t_acc = l0buf["t_acc"]
            t_eo = l0buf["t_xn"]; t_g = l0buf["t_f1"]
            w_sq = l0buf["w_sq"]; w_1 = l0buf["w_1"]; w2_slots = l0buf["w2s"]

            bc_r, bc_n = emit_ln(
                "lnA", ST,
                lambda c: ("R", c),
                lambda c, m0=m0, R=R: R[:, c * TOK + m0: c * TOK + m0 + ST], False)
            for c in range(NCK):
                b.op("ve", lambda E, s, c=c, m0=m0, R=R: E.tensor_tensor(
                        t_f1[:, 0:ST], R[:, c * TOK + m0: c * TOK + m0 + ST],
                        ps_bc[0][:, 0:ST], ALU.mult).then_inc(s, 1),
                     reads=(("R", c), bc_r), writes=("t_f1",))
                b.op("ve", lambda E, s, c=c: E.tensor_tensor(
                        t_xin[:, c * ST:(c + 1) * ST], t_f1[:, 0:ST],
                        ps_bc[1][:, 0:ST], ALU.add).then_inc(s, 1),
                     reads=("t_f1", bc_n), writes=(("xin", c),))
            bc_r2, bc_n2 = emit_ln(
                "lnB", ST,
                lambda c: ("xin", c),
                lambda c: t_xin[:, c * ST:(c + 1) * ST], True)
            for c in range(NCK):
                b.op("ve", lambda E, s, c=c: E.tensor_tensor(
                        t_f1[:, 0:ST], t_xin[:, c * ST:(c + 1) * ST],
                        ps_bc[0][:, 0:ST], ALU.mult).then_inc(s, 1),
                     reads=(("xin", c), bc_r2), writes=("t_f1",))
                b.op("ve", lambda E, s, c=c: E.tensor_tensor(
                        t_f1[:, 0:ST], t_f1[:, 0:ST],
                        ps_bc[1][:, 0:ST], ALU.add).then_inc(s, 1),
                     reads=("t_f1", bc_n2), writes=("t_f1",))
                b.op("se", lambda E, s, c=c: E.activation(
                        out=t_xn[:, c * ST:(c + 1) * ST], in_=t_f1[:, 0:ST], func=ACT.Identity,
                        bias=s_lnb[:, c:c + 1], scale=s_lns[:, c:c + 1]).then_inc(s, 1),
                     reads=("t_f1", "lns", "lnb"), writes=(("xn", c),))

            for ec in range(NCK):
                w = w_sq[ec % 2]
                b.dmaop("gw", lambda E, s, ec=ec, w=w: E.dma_start(
                            out=w[:].rearrange("p (c q) -> p c q", c=NCK),
                            in_=attn_wT[:, ec * 128:(ec + 1) * 128]
                                .rearrange("(c p) q -> p c q", p=128)).then_inc(s, 16),
                        reads=(), writes=(("w_sq", ec % 2),))
                b.op("te", lambda E, s, ec=ec, w=w: _group_mm(
                        E, s, ps_m[ec % 2][:, 0:ST],
                        [(w[:, c * 128:(c + 1) * 128], t_xn[:, c * ST:(c + 1) * ST])
                         for c in range(NCK)]),
                     reads=tuple(("xn", c) for c in range(NCK)) + (("w_sq", ec % 2),),
                     writes=(("ps_m", ec % 2),))
                b.op("se", lambda E, s, ec=ec: E.activation(
                        out=t_g[:, 0:ST], in_=ps_m[ec % 2][:, 0:ST], func=ACT.Identity,
                        bias=s_attnb[:, ec:ec + 1], scale=1.0).then_inc(s, 1),
                     reads=(("ps_m", ec % 2), "attnb"), writes=("t_f1",))
                b.op("ve", lambda E, s, ec=ec: E.tensor_tensor(
                        t_xa[:, ec * ST:(ec + 1) * ST], t_g[:, 0:ST],
                        t_xin[:, ec * ST:(ec + 1) * ST], ALU.add).then_inc(s, 1),
                     reads=("t_f1", ("xin", ec)), writes=(("xa", ec),))

            for qq in range(4):
                for half in range(2):
                    for hx in range(4):
                        hh = half * 4 + hx
                        hc = qq * 8 + hh
                        w = w_1[hx % 2]
                        b.dmaop("gw", lambda E, s, hc=hc, w=w: E.dma_start(
                                    out=w[:].rearrange("p (c q) -> p c q", c=NCK),
                                    in_=w1e[:, hc * 128:(hc + 1) * 128]
                                        .rearrange("(c p) q -> p c q", p=128)).then_inc(s, 16),
                                reads=(), writes=(("w_1", hx % 2),))
                        b.op("te", lambda E, s, hx=hx, w=w: _group_mm(
                                E, s, ps_h[hx % 2][:, 0:ST],
                                [(w[:, c * 128:(c + 1) * 128], t_xa[:, c * ST:(c + 1) * ST])
                                 for c in range(NCK)]),
                             reads=tuple(("xa", c) for c in range(NCK)) + (("w_1", hx % 2),),
                             writes=(("ps_h", hx % 2),))
                        b.op("se", lambda E, s, hc=hc, hx=hx: E.activation(
                                out=t_h[:, hx * ST:(hx + 1) * ST], in_=ps_h[hx % 2][:, 0:ST],
                                func=ACT.Gelu, bias=s_b1[:, hc:hc + 1], scale=1.0).then_inc(s, 1),
                             reads=(("ps_h", hx % 2), "b1"), writes=(("h", hx),))
                        b.dmaop("gw", lambda E, s, hc=hc, hx=hx: E.dma_start(
                                    out=w2_slots[hx][:],
                                    in_=w2e[hc * 128:(hc + 1) * 128, :]).then_inc(s, 16),
                                reads=(), writes=(("w2s", hx),))
                    for dc in range(NCK):
                        b.op("te", lambda E, s, dc=dc: _group_mm(
                                E, s, ps_m[dc % 2][:, 0:ST],
                                [(w2_slots[hx][:, dc * 128:(dc + 1) * 128],
                                  t_h[:, hx * ST:(hx + 1) * ST]) for hx in range(4)]),
                             reads=tuple(("h", hx) for hx in range(4)) +
                                   tuple(("w2s", hx) for hx in range(4)),
                             writes=(("ps_m", dc % 2),))
                        if qq == 0 and half == 0:
                            b.op("se", lambda E, s, dc=dc: E.activation(
                                    out=t_acc[:, dc * ST:(dc + 1) * ST],
                                    in_=ps_m[dc % 2][:, 0:ST], func=ACT.Identity).then_inc(s, 1),
                                 reads=(("ps_m", dc % 2),), writes=(("acc", dc),))
                        else:
                            b.op("ve", lambda E, s, dc=dc: E.tensor_tensor(
                                    t_acc[:, dc * ST:(dc + 1) * ST],
                                    t_acc[:, dc * ST:(dc + 1) * ST],
                                    ps_m[dc % 2][:, 0:ST], ALU.add).then_inc(s, 1),
                                 reads=(("ps_m", dc % 2), ("acc", dc)), writes=(("acc", dc),))

            for dc in range(NCK):
                b.op("se", lambda E, s, dc=dc: E.activation(
                        out=t_eo[:, dc * ST:(dc + 1) * ST],
                        in_=t_acc[:, dc * ST:(dc + 1) * ST], func=ACT.Identity,
                        bias=s_b2[:, dc:dc + 1], scale=1.0).then_inc(s, 1),
                     reads=(("acc", dc), "b2"), writes=(("xn", dc),))
            for ec in range(NCK):
                w = w_sq[ec % 2]
                b.dmaop("gw", lambda E, s, ec=ec, w=w: E.dma_start(
                            out=w[:].rearrange("p (c q) -> p c q", c=NCK),
                            in_=next_wT[:, ec * 128:(ec + 1) * 128]
                                .rearrange("(c p) q -> p c q", p=128)).then_inc(s, 16),
                        reads=(), writes=(("w_sq", ec % 2),))
                b.op("te", lambda E, s, ec=ec, w=w: _group_mm(
                        E, s, ps_h[ec % 2][:, 0:ST],
                        [(w[:, c * 128:(c + 1) * 128], t_eo[:, c * ST:(c + 1) * ST])
                         for c in range(NCK)]),
                     reads=tuple(("xn", c) for c in range(NCK)) + (("w_sq", ec % 2),),
                     writes=(("ps_h", ec % 2),))
                b.op("se", lambda E, s, ec=ec: E.activation(
                        out=t_g[:, 0:ST], in_=ps_h[ec % 2][:, 0:ST], func=ACT.Gelu,
                        bias=s_nextb[:, ec:ec + 1], scale=1.0).then_inc(s, 1),
                     reads=(("ps_h", ec % 2), "nextb"), writes=("t_f1",))
                b.op("ve", lambda E, s, ec=ec, m0=m0, R=R: E.scalar_tensor_tensor(
                        R[:, ec * TOK + m0: ec * TOK + m0 + ST], t_g[:, 0:ST], 0.5,
                        R[:, ec * TOK + m0: ec * TOK + m0 + ST],
                        ALU.mult, ALU.add).then_inc(s, 1),
                     reads=("t_f1", ("R", ec)), writes=(("R", ec),))

        def l0ffn_phase(R):
            with ExitStack() as px:
                l0ffn_alloc(px)
                for st_i in range(NST):
                    l0ffn_supertile(st_i * ST, R)

        l1ctr = [0]

        def l1ffn_phase():
            call = l1ctr[0]
            l1ctr[0] += 1
            with ExitStack() as px:
                sfx = f"l1_{call}"
                pb = lambda nm, shape, dt: px.enter_context(nc.sbuf_tensor(nm + sfx, shape, dt))
                pool = {
                    "f": [pb(f"qf{i}", [128, 2048], F32) for i in range(8)],
                    "i": [pb(f"qi{i}", [128, 2048], I32) for i in range(4)],
                    "bf": pb("qbf", [128, 2048], BF16),
                }
                t_rt = pb("t_rt", [128, NCK * ST], F32)
                xq = pb("xq", [128, NCK * ST], BF16)
                hq = pb("hq", [128, 2048], BF16)
                t_acc = pb("t_acc", [128, NCK * ST], F32)
                w1s = [pb(f"w1s{i}", [128, NCK * 128], BF16) for i in range(2)]
                w2s = [pb(f"w2s{i}", [128, D], BF16) for i in range(4)]
                t_mask = pb("t_mask", [128, NCK * ST], BF16)

                if call == 0:
                    emit_wquant(pool, W1T, d_w1q, D, H0, "w1")
                    b.op("ve", lambda E, s: E.engine_nop().then_inc(s, 1),
                         reads=(("wq_st", "w1"),), writes=("wq1done",))
                    emit_wquant(pool, W2T, d_w2q, H0, D, "w2")
                    b.op("ve", lambda E, s: E.engine_nop().then_inc(s, 1),
                         reads=(("wq_st", "w2"),), writes=("wq2done",))
                    emit_quant(pool, "l1b1", s_l1b1[:], ("l1b1q",), s_l1b1q[:], 2, 113.0, True, NH0)
                    emit_quant(pool, "l1b2", s_l1b2[:], ("l1b2q",), s_l1b2q[:], 2, 113.0, True, NCK)

                zst = pool["f"][6]
                xst = pool["f"][7]
                for st_i in range(NST):
                    m0 = st_i * ST
                    b.dmaop("io", lambda E, s, m0=m0: E.dma_start(
                                out=t_rt[:].rearrange("p (c m) -> p c m", c=NCK),
                                in_=d_rsp.rearrange("(c p) m -> p c m", p=128)
                                    [:, :, m0:m0 + ST]).then_inc(s, 16),
                            reads=("rspv",), writes=(("rt",),))
                    b.dmaop("io", lambda E, s, m0=m0: E.dma_start(
                                out=t_mask[:].rearrange("p (c m) -> p c m", c=NCK),
                                in_=maskT.rearrange("(c p) m -> p c m", p=128)
                                    [:, :, m0:m0 + ST]).then_inc(s, 16),
                            reads=(), writes=(("mask",),))
                    bc_r, bc_n = emit_ln("l1ln", ST, lambda c: ("rt",),
                                         lambda c: t_rt[:, c * ST:(c + 1) * ST], False)
                    for half in range(2):
                        for cc in range(4):
                            c = half * 4 + cc
                            b.op("ve", lambda E, s, c=c, cc=cc: E.tensor_tensor(
                                    xst[:, cc * ST:(cc + 1) * ST], t_rt[:, c * ST:(c + 1) * ST],
                                    ps_bc[0][:, 0:ST], ALU.mult).then_inc(s, 1),
                                 reads=(("rt",), bc_r), writes=("xst",))
                            b.op("ve", lambda E, s, c=c, cc=cc: E.tensor_tensor(
                                    xst[:, cc * ST:(cc + 1) * ST], xst[:, cc * ST:(cc + 1) * ST],
                                    ps_bc[1][:, 0:ST], ALU.add).then_inc(s, 1),
                                 reads=("xst", bc_n), writes=("xst",))
                        emit_quant(pool, "xst", xst[:, 0:2048], "xq",
                                   xq[:, half * 2048:(half + 1) * 2048], 2, 113.0, True, 2048)
                    xqk = ("xq",)
                    for g in range(8):
                        for hx in range(4):
                            hc = g * 4 + hx
                            w = w1s[hx % 2]
                            b.dmaop("wt", lambda E, s, hc=hc, w=w: E.dma_start(
                                        out=w[:].rearrange("p (c q) -> p c q", c=NCK),
                                        in_=d_w1q[:, hc * 128:(hc + 1) * 128]
                                            .rearrange("(c p) q -> p c q", p=128)).then_inc(s, 16),
                                    reads=("wq1done",), writes=(("w1s", hx % 2),))
                            b.op("te", lambda E, s, hx=hx, w=w: _group_mm(
                                    E, s, ps_h[hx % 2][:, 0:ST],
                                    [(w[:, c * 128:(c + 1) * 128], xq[:, c * ST:(c + 1) * ST])
                                     for c in range(NCK)]),
                                 reads=xqk + (("w1s", hx % 2),), writes=(("ps_h", hx % 2),))
                            b.op("se", lambda E, s, hc=hc, hx=hx: E.activation(
                                    out=zst[:, hx * ST:(hx + 1) * ST], in_=ps_h[hx % 2][:, 0:ST],
                                    func=ACT.Relu, bias=s_l1b1q[:, hc:hc + 1], scale=1.0).then_inc(s, 1),
                                 reads=(("ps_h", hx % 2), ("l1b1q",)), writes=("zst",))
                            b.dmaop("wt", lambda E, s, hc=hc, hx=hx: E.dma_start(
                                        out=w2s[hx][:],
                                        in_=d_w2q[hc * 128:(hc + 1) * 128, :]).then_inc(s, 16),
                                    reads=("wq2done",), writes=(("w2s", hx),))
                        emit_quant(pool, "zst", zst[:, 0:2048], "xst",
                                   xst[:, 0:2048], 7, 1.0, False, 2048)
                        emit_quant(pool, "xst", xst[:, 0:2048], "hq",
                                   hq[:, 0:2048], 2, 113.0, False, 2048)
                        for dc in range(NCK):
                            b.op("te", lambda E, s, dc=dc: _group_mm(
                                    E, s, ps_m[dc % 2][:, 0:ST],
                                    [(w2s[hx][:, dc * 128:(dc + 1) * 128],
                                      hq[:, hx * ST:(hx + 1) * ST]) for hx in range(4)]),
                                 reads=("hq",) + tuple(("w2s", hx) for hx in range(4)),
                                 writes=(("ps_m", dc % 2),))
                            if g == 0:
                                b.op("se", lambda E, s, dc=dc: E.activation(
                                        out=t_acc[:, dc * ST:(dc + 1) * ST],
                                        in_=ps_m[dc % 2][:, 0:ST], func=ACT.Identity).then_inc(s, 1),
                                     reads=(("ps_m", dc % 2),), writes=(("acc", dc),))
                            else:
                                b.op("ve", lambda E, s, dc=dc: E.tensor_tensor(
                                        t_acc[:, dc * ST:(dc + 1) * ST],
                                        t_acc[:, dc * ST:(dc + 1) * ST],
                                        ps_m[dc % 2][:, 0:ST], ALU.add).then_inc(s, 1),
                                     reads=(("ps_m", dc % 2), ("acc", dc)), writes=(("acc", dc),))
                    for half in range(2):
                        for cc in range(4):
                            dc = half * 4 + cc
                            b.op("se", lambda E, s, dc=dc, cc=cc: E.activation(
                                    out=zst[:, cc * ST:(cc + 1) * ST],
                                    in_=t_acc[:, dc * ST:(dc + 1) * ST], func=ACT.Identity,
                                    bias=s_l1b2q[:, dc:dc + 1], scale=1.0).then_inc(s, 1),
                                 reads=(("acc", dc), ("l1b2q",)), writes=("zst",))
                        emit_quant(pool, "zst", zst[:, 0:2048], "xst",
                                   xst[:, 0:2048], 7, 1.0, True, 2048)
                        b.op("ve", lambda E, s, half=half: E.tensor_tensor(
                                xst[:, 0:2048], xst[:, 0:2048],
                                t_mask[:, half * 2048:(half + 1) * 2048], ALU.mult).then_inc(s, 1),
                             reads=("xst", ("mask",)), writes=("xst",))
                        b.op("ve", lambda E, s, half=half: E.scalar_tensor_tensor(
                                t_rt[:, half * 2048:(half + 1) * 2048], xst[:, 0:2048], 0.5,
                                t_rt[:, half * 2048:(half + 1) * 2048],
                                ALU.mult, ALU.add).then_inc(s, 1),
                             reads=("xst", ("rt",)), writes=(("rt",),))
                    b.dmaop("io", lambda E, s, m0=m0: E.dma_start(
                                out=d_rsp.rearrange("(c p) m -> p c m", p=128)
                                    [:, :, m0:m0 + ST],
                                in_=t_rt[:].rearrange("p (c m) -> p c m", c=NCK)).then_inc(s, 16),
                            reads=(("rt",),), writes=(("rspv",),))

        GOFF = [0] * 17
        for q in range(16):
            GOFF[q + 1] = GOFF[q] + (16 - q) * 128

        def scan_pre(R):
            with ExitStack() as px:
                stg = px.enter_context(nc.sbuf_tensor("sc_stg", [128, NCK * ST], BF16))
                for st_i in range(NST):
                    m0 = st_i * ST
                    bc_r, bc_n = emit_ln(
                        "lnS", ST, lambda c: ("R", c),
                        lambda c, m0=m0, R=R: R[:, c * TOK + m0: c * TOK + m0 + ST], False)
                    for c in range(NCK):
                        b.op("ve", lambda E, s, c=c, m0=m0, R=R: E.tensor_tensor(
                                sq_slots[0][:, 0:ST], R[:, c * TOK + m0: c * TOK + m0 + ST],
                                ps_bc[0][:, 0:ST], ALU.mult).then_inc(s, 1),
                             reads=(("R", c), bc_r), writes=(("sqslot", 0),))
                        b.op("ve", lambda E, s, c=c: E.tensor_tensor(
                                stg[:, c * ST:(c + 1) * ST], sq_slots[0][:, 0:ST],
                                ps_bc[1][:, 0:ST], ALU.add).then_inc(s, 1),
                             reads=(("sqslot", 0), bc_n), writes=(("stg", c),))
                    p_ = m0 // SEQ
                    s0 = m0 % SEQ
                    for half in range(2):
                        jd = s0 // 256 + half
                        b.dmaop("io", lambda E, s, p_=p_, jd=jd, half=half: E.dma_start(
                                    out=d_x1in[jd * D:(jd + 1) * D, p_ * 256:(p_ + 1) * 256]
                                        .rearrange("(c p) t -> p c t", p=128),
                                    in_=stg[:].rearrange("p (c t) -> p c t", c=NCK)
                                        [:, :, half * 256:(half + 1) * 256]).then_inc(s, 16),
                                reads=tuple(("stg", c) for c in range(NCK)),
                                writes=(("x1in", st_i, half),))
                for c in range(NCK):
                    b.dmaop("io", lambda E, s, c=c, R=R: E.dma_start(
                                out=d_rsp[c * 128:(c + 1) * 128, :],
                                in_=R[:, c * TOK:(c + 1) * TOK]).then_inc(s, 16),
                            reads=(("R", c),), writes=(("rsp", c),))

        def _a2a(E, s, ins_ap, outs_ap):
            from concourse.aot_env import allow_ici_all_to_all
            with allow_ici_all_to_all():
                E.collective_compute(
                    "AllToAll", ALU.bypass, replica_groups=[list(range(NCORE))],
                    ins=[ins_ap], outs=[outs_ap]).then_inc(s, 1)

        def scan_phase_a():
            b.ccop(lambda E, s: _a2a(E, s, d_x1in.opt(), d_x1out.opt()),
                   reads=tuple(("x1in", i, h) for i in range(NST) for h in range(2)),
                   writes=("x1out",))
            pubkeys = []
            with ExitStack() as px:
                Xt = px.enter_context(nc.sbuf_tensor("sc_Xt", [128, NCK * MC], BF16))
                UV = px.enter_context(nc.sbuf_tensor("sc_UV", [128, 16 * 2048], BF16))
                Gb = px.enter_context(nc.sbuf_tensor("sc_G", [128, GOFF[16]], BF16))
                wkvs = px.enter_context(nc.sbuf_tensor("sc_wkv", [128, NCK * D], BF16))
                bkvs = px.enter_context(nc.sbuf_tensor("sc_bkv", [1, D], BF16))
                utst = px.enter_context(nc.sbuf_tensor("sc_utst", [128, NCK * 128], BF16))
                tls = {nm: px.enter_context(nc.sbuf_tensor("sc_" + nm, [128, 128], BF16))
                       for nm in ("tT", "tL", "t2", "t2t", "t4", "a1", "b2", "b4", "m1", "inv")}
                b.dmaop("gw", lambda E, s: E.dma_start(
                            out=wkvs[:].rearrange("p (c q) -> p c q", c=NCK),
                            in_=wkv_vT.rearrange("(c p) q -> p c q", p=128)).then_inc(s, 16),
                        writes=("wkvs",))
                b.dmaop("gw", lambda E, s: E.dma_start(out=bkvs[:], in_=bkv_row).then_inc(s, 16),
                        writes=("bkvs",))

                for cidx in range(2):
                    b.global_barrier()
                    base1 = cidx * PUB1
                    for o in range(NCORE):
                        for p_ in range(2):
                            blk = o * 2 + p_
                            b.dmaop("io", lambda E, s, o=o, p_=p_, blk=blk, cidx=cidx: E.dma_start(
                                        out=UV[:].rearrange("p (c r) -> p c r", c=16)
                                            [:, 0:NCK, blk * 128:(blk + 1) * 128],
                                        in_=d_x1out[o * D:(o + 1) * D,
                                                    p_ * 256 + cidx * 128: p_ * 256 + (cidx + 1) * 128]
                                            .rearrange("(c p) t -> p c t", p=128)).then_inc(s, 16),
                                    reads=("x1out",), writes=(("Xc", blk),))
                    for o in range(NCORE):
                        for p_ in range(2):
                            blk = o * 2 + p_
                            bglob = o + 8 * p_
                            b.op("ve", lambda E, s, blk=blk, bglob=bglob: E.tensor_copy(
                                    Xt[:].rearrange("p (c t bb) -> p c t bb", c=NCK, bb=16)
                                        [:, :, :, bglob],
                                    UV[:].rearrange("p (c r) -> p c r", c=16)
                                        [:, 0:NCK, blk * 128:(blk + 1) * 128]).then_inc(s, 1),
                                 reads=(("Xc", blk),), writes=(("Xt", o, p_),))
                    xtk = tuple(("Xt", o, p_) for o in range(NCORE) for p_ in range(2))
                    for j in range(16):
                        for cc in range(NCK):
                            b.dmaop("tp", lambda E, s, j=j, cc=cc: E.dma_start(
                                        out=UV[:, j * 2048 + cc * 128: j * 2048 + (cc + 1) * 128],
                                        in_=Xt[:, cc * MC + j * 128: cc * MC + (j + 1) * 128],
                                        transpose=True).then_inc(s, 16),
                                    reads=xtk, writes=(("UVx", j),))
                        b.dmaop("gw", lambda E, s, j=j, base1=base1: E.dma_start(
                                    out=d_pub1[base1 + D + j * 128: base1 + D + (j + 1) * 128, 0:1024],
                                    in_=UV[:, j * 2048: j * 2048 + 1024]).then_inc(s, 16),
                                reads=(("UVx", j),), writes=(("pubx", cidx, j),))
                        pubkeys.append(("pubx", cidx, j))
                    for j in range(16):
                        for hf in range(2):
                            def vmm(E, s, j=j, hf=hf):
                                for c in range(NCK):
                                    E.matmul(ps_m[hf][:, 0:512],
                                             Xt[:, c * MC + j * 128: c * MC + (j + 1) * 128],
                                             wkvs[:, c * D + hf * 512: c * D + (hf + 1) * 512],
                                             start=(c == 0), stop=False)
                                E.matmul(ps_m[hf][:, 0:512], c_ones_rowb[:],
                                         bkvs[0:1, hf * 512:(hf + 1) * 512],
                                         start=False, stop=True).then_inc(s, 1)
                            b.op("te", vmm,
                                 reads=xtk + ("wkvs", "bkvs", "ones_rowb"),
                                 writes=(("ps_m", hf),))
                            b.op("se", lambda E, s, j=j, hf=hf: E.activation(
                                    out=UV[:, j * 2048 + 1024 + hf * 512: j * 2048 + 1024 + (hf + 1) * 512],
                                    in_=ps_m[hf][:, 0:512], func=ACT.Identity).then_inc(s, 1),
                                 reads=(("ps_m", hf),), writes=(("UVv", j, hf),))
                    uvvk = tuple(("UVv", j, hf) for j in range(16) for hf in range(2))
                    for dst_i in range(NCORE):
                        rbase = ((dst_i * 2 + cidx) * 2 + 0) * MC
                        b.dmaop("gw", lambda E, s, dst_i=dst_i, rbase=rbase: E.dma_start(
                                    out=d_p2in[rbase: rbase + MC, :]
                                        .rearrange("(j p) q -> p j q", p=128),
                                    in_=UV[:].rearrange("p (j q) -> p j q", j=16)
                                        [:, :, 1024 + dst_i * 128: 1024 + (dst_i + 1) * 128]
                                    ).then_inc(s, 16),
                                reads=uvvk, writes=(("pubv", cidx, dst_i),))
                        pubkeys.append(("pubv", cidx, dst_i))
                    for q in range(16):
                        g0 = q
                        while g0 < 16:
                            wblk = min(4, 16 - g0)
                            def gmm(E, s, q=q, g0=g0, wblk=wblk):
                                for c in range(NCK):
                                    E.matmul(ps_h[0][:, 0:wblk * 128],
                                             Xt[:, c * MC + q * 128: c * MC + (q + 1) * 128],
                                             Xt[:, c * MC + g0 * 128: c * MC + (g0 + wblk) * 128],
                                             start=(c == 0), stop=(c == NCK - 1)).then_inc(s, 1) \
                                        if c == NCK - 1 else E.matmul(
                                             ps_h[0][:, 0:wblk * 128],
                                             Xt[:, c * MC + q * 128: c * MC + (q + 1) * 128],
                                             Xt[:, c * MC + g0 * 128: c * MC + (g0 + wblk) * 128],
                                             start=(c == 0), stop=False)
                            b.op("te", gmm, reads=xtk, writes=(("ps_h", 0),))
                            b.op("se", lambda E, s, q=q, g0=g0, wblk=wblk: E.activation(
                                    out=Gb[:, GOFF[q] + (g0 - q) * 128: GOFF[q] + (g0 - q + wblk) * 128],
                                    in_=ps_h[0][:, 0:wblk * 128], func=ACT.Identity).then_inc(s, 1),
                                 reads=(("ps_h", 0),), writes=(("G", q, g0),))
                            g0 += wblk
                    gk = tuple(("G", q, g0) for q in range(16) for g0 in range(q, 16, 4))
                    for p_ in range(16):
                        if p_ > 0:
                            for g in range(4):
                                def rmm2(E, s, p_=p_, g=g):
                                    ins = None
                                    for qi in range(p_):
                                        ins = E.matmul(ps_m[g % 2][:, 0:512],
                                                 Gb[:, GOFF[qi] + (p_ - qi) * 128: GOFF[qi] + (p_ - qi + 1) * 128],
                                                 UV[:, qi * 2048 + g * 512: qi * 2048 + (g + 1) * 512],
                                                 start=(qi == 0), stop=(qi == p_ - 1))
                                    ins.then_inc(s, 1)
                                b.op("te", rmm2,
                                     reads=gk + tuple(("UVs", qi) for qi in range(p_)) + uvvk +
                                           tuple(("UVx", j) for j in range(16)),
                                     writes=(("ps_m", g % 2),))
                                b.op("ve", lambda E, s, p_=p_, g=g: E.scalar_tensor_tensor(
                                        UV[:, p_ * 2048 + g * 512: p_ * 2048 + (g + 1) * 512],
                                        ps_m[g % 2][:, 0:512], c_negc[:, 0:1],
                                        UV[:, p_ * 2048 + g * 512: p_ * 2048 + (g + 1) * 512],
                                        ALU.mult, ALU.add).then_inc(s, 1),
                                     reads=(("ps_m", g % 2), "negc", ("UVx", p_), ("UVv", p_, g % 2)),
                                     writes=(("UVr", p_, g), ("UVx", p_), ("UVv", p_, g % 2)))
                        b.op("ve", lambda E, s, p_=p_: E.scalar_tensor_tensor(
                                tls["tT"][:], Gb[:, GOFF[p_]:GOFF[p_] + 128], c_negc[:, 0:1],
                                c_msu[:], ALU.mult, ALU.mult).then_inc(s, 1),
                             reads=gk + ("negc", "msu"), writes=(("tT",),))
                        b.op("ve", lambda E, s, p_=p_: E.scalar_tensor_tensor(
                                tls["tL"][:], Gb[:, GOFF[p_]:GOFF[p_] + 128], c_negc[:, 0:1],
                                c_msl[:], ALU.mult, ALU.mult).then_inc(s, 1),
                             reads=gk + ("negc", "msl"), writes=(("tL",),))
                        def mm1(dst, lh, rh, rk, wk):
                            b.op("te", lambda E, s, lh=lh, rh=rh: E.matmul(
                                    ps_h[1][:, 0:128], lh[:], rh[:],
                                    start=True, stop=True).then_inc(s, 1),
                                 reads=rk, writes=(("ps_h", 1),))
                            b.op("ve", lambda E, s, dst=dst: E.tensor_copy(
                                    dst[:], ps_h[1][:, 0:128]).then_inc(s, 1),
                                 reads=(("ps_h", 1),), writes=(wk,))
                        mm1(tls["t2"], tls["tT"], tls["tL"], (("tT",), ("tL",)), ("t2",))
                        mm1(tls["t2t"], tls["tL"], tls["tT"], (("tT",), ("tL",)), ("t2t",))
                        mm1(tls["t4"], tls["t2t"], tls["t2"], (("t2",), ("t2t",)), ("t4",))
                        b.op("ve", lambda E, s: E.tensor_tensor(
                                tls["a1"][:], tls["tT"][:], c_ident[:], ALU.add).then_inc(s, 1),
                             reads=(("tT",), "ident"), writes=(("a1",),))
                        b.op("ve", lambda E, s: E.tensor_tensor(
                                tls["b2"][:], tls["t2"][:], c_ident[:], ALU.add).then_inc(s, 1),
                             reads=(("t2",), "ident"), writes=(("b2",),))
                        b.op("ve", lambda E, s: E.tensor_tensor(
                                tls["b4"][:], tls["t4"][:], c_ident[:], ALU.add).then_inc(s, 1),
                             reads=(("t4",), "ident"), writes=(("b4",),))
                        mm1(tls["m1"], tls["b2"], tls["a1"], (("b2",), ("a1",)), ("m1",))
                        mm1(tls["inv"], tls["b4"], tls["m1"], (("b4",), ("m1",)), ("inv",))
                        for g in range(4):
                            b.op("te", lambda E, s, p_=p_, g=g: E.matmul(
                                    ps_m[g % 2][:, 0:512], tls["inv"][:],
                                    UV[:, p_ * 2048 + g * 512: p_ * 2048 + (g + 1) * 512],
                                    start=True, stop=True).then_inc(s, 1),
                                 reads=(("inv",),) + ((("UVr", p_, g),) if p_ > 0 else
                                                      (("UVx", p_), ("UVv", p_, g % 2))),
                                 writes=(("ps_m", g % 2),))
                            b.op("se", lambda E, s, p_=p_, g=g: E.activation(
                                    out=UV[:, p_ * 2048 + g * 512: p_ * 2048 + (g + 1) * 512],
                                    in_=ps_m[g % 2][:, 0:512], func=ACT.Identity).then_inc(s, 1),
                                 reads=(("ps_m", g % 2),), writes=(("UVs", p_),))
                    uvsk = tuple(("UVs", j) for j in range(16))
                    for j in range(16):
                        b.dmaop("tp", lambda E, s, j=j: E.dma_start(
                                    out=utst[:].rearrange("p (c q) -> p c q", c=NCK),
                                    in_=UV[:, j * 2048: j * 2048 + 1024]
                                        .rearrange("p (c q) -> p c q", c=NCK),
                                    transpose=True).then_inc(s, 16),
                                reads=uvsk, writes=(("utst",),))
                        b.dmaop("gw", lambda E, s, j=j, base1=base1: E.dma_start(
                                    out=d_pub1[base1: base1 + 1024, j * 128:(j + 1) * 128]
                                        .rearrange("(c p) q -> p c q", p=128),
                                    in_=utst[:].rearrange("p (c q) -> p c q", c=NCK)
                                    ).then_inc(s, 16),
                                reads=(("utst",),), writes=(("pubu", cidx, j),))
                        pubkeys.append(("pubu", cidx, j))
                    for dst_i in range(NCORE):
                        rbase = ((dst_i * 2 + cidx) * 2 + 1) * MC
                        b.dmaop("gw", lambda E, s, dst_i=dst_i, rbase=rbase: E.dma_start(
                                    out=d_p2in[rbase: rbase + MC, :]
                                        .rearrange("(j p) q -> p j q", p=128),
                                    in_=UV[:].rearrange("p (j q) -> p j q", j=16)
                                        [:, :, 1024 + dst_i * 128: 1024 + (dst_i + 1) * 128]
                                    ).then_inc(s, 16),
                                reads=uvsk, writes=(("pubf", cidx, dst_i),))
                        pubkeys.append(("pubf", cidx, dst_i))
            return pubkeys

        def scan_phase_b(pubkeys):
            b.ccop(lambda E, s: E.collective_compute(
                        "AllGather", ALU.bypass, replica_groups=[list(range(NCORE))],
                        ins=[d_pub1.opt()], outs=[d_agp1.opt()],
                    ).then_inc(s, 1),
                   reads=tuple(pubkeys), writes=("agp1",))
            b.ccop(lambda E, s: _a2a(E, s, d_p2in.opt(), d_p2out.opt()),
                   reads=tuple(pubkeys), writes=("p2out",))
            b.global_barrier()
            with ExitStack() as px:
                Ut = [px.enter_context(nc.sbuf_tensor(f"sb_Ut{i}", [128, NCK * MC], BF16)) for i in range(2)]
                Xr = [px.enter_context(nc.sbuf_tensor(f"sb_Xr{i}", [128, 16 * 1024], BF16)) for i in range(2)]
                Vb = [px.enter_context(nc.sbuf_tensor(f"sb_Vb{i}", [128, MC], BF16)) for i in range(2)]
                Fb = [px.enter_context(nc.sbuf_tensor(f"sb_Fb{i}", [128, MC], BF16)) for i in range(2)]
                UZt = px.enter_context(nc.sbuf_tensor("sb_UZt", [128, MC], BF16))
                UZr = px.enter_context(nc.sbuf_tensor("sb_UZr", [128, MC], BF16))
                Erow = px.enter_context(nc.sbuf_tensor("sb_Erow", [128, MC], BF16))
                Yrow = px.enter_context(nc.sbuf_tensor("sb_Yrow", [128, MC], BF16))
                Yt = px.enter_context(nc.sbuf_tensor("sb_Yt", [128, MC], BF16))
                Yb = px.enter_context(nc.sbuf_tensor("sb_Yb", [128, MC], BF16))
                Z = px.enter_context(nc.sbuf_tensor("sb_Z", [128, NCK * 128], F32))
                Zb = px.enter_context(nc.sbuf_tensor("sb_Zb", [128, NCK * 128], BF16))
                b.dmaop("io", lambda E, s: E.dma_start(
                            out=Z[:].rearrange("p (c q) -> p c q", c=NCK),
                            in_=z0.rearrange("(c p) q -> p c q", p=128)).then_inc(s, 16),
                        writes=(("Z",),))
                b.op("se", lambda E, s: E.copy(out=Zb[:], in_=Z[:]).then_inc(s, 1),
                     reads=(("Z",),), writes=(("Zb",),))
                for j in range(NCHUNK):
                    sl = j % 2
                    b.dmaop("io", lambda E, s, j=j, sl=sl: E.dma_start(
                                out=Ut[sl][:].rearrange("p (c m) -> p c m", c=NCK),
                                in_=d_agp1[j * PUB1: j * PUB1 + 1024, :]
                                    .rearrange("(c p) m -> p c m", p=128)).then_inc(s, 16),
                            reads=("agp1",), writes=(("Ut", sl),))
                    b.dmaop("io", lambda E, s, j=j, sl=sl: E.dma_start(
                                out=Xr[sl][:].rearrange("p (r q) -> p r q", r=16),
                                in_=d_agp1[j * PUB1 + 1024: j * PUB1 + 1024 + MC, 0:1024]
                                    .rearrange("(r p) q -> p r q", p=128)).then_inc(s, 16),
                            reads=("agp1",), writes=(("Xr", sl),))
                    b.dmaop("io", lambda E, s, j=j, sl=sl: E.dma_start(
                                out=Vb[sl][:].rearrange("p (r q) -> p r q", r=16),
                                in_=d_p2out[(j * 2 + 0) * MC: (j * 2 + 1) * MC, :]
                                    .rearrange("(r p) q -> p r q", p=128)).then_inc(s, 16),
                            reads=("p2out",), writes=(("Vb", sl),))
                    b.dmaop("io", lambda E, s, j=j, sl=sl: E.dma_start(
                                out=Fb[sl][:].rearrange("p (r q) -> p r q", r=16),
                                in_=d_p2out[(j * 2 + 1) * MC: (j * 2 + 2) * MC, :]
                                    .rearrange("(r p) q -> p r q", p=128)).then_inc(s, 16),
                            reads=("p2out",), writes=(("Fb", sl),))
                    for g in range(4):
                        def emm(E, s, g=g, sl=sl):
                            ins = None
                            for c in range(NCK):
                                ins = E.matmul(ps_m[g % 2][:, 0:512],
                                               Zb[:, c * 128:(c + 1) * 128],
                                               Ut[sl][:, c * MC + g * 512: c * MC + (g + 1) * 512],
                                               start=(c == 0), stop=(c == NCK - 1))
                            ins.then_inc(s, 1)
                        b.op("te", emm, reads=(("Zb",), ("Ut", sl)), writes=(("ps_m", g % 2),))
                        b.op("se", lambda E, s, g=g: E.activation(
                                out=UZt[:, g * 512:(g + 1) * 512], in_=ps_m[g % 2][:, 0:512],
                                func=ACT.Identity).then_inc(s, 1),
                             reads=(("ps_m", g % 2),), writes=(("UZt", g),))
                    for g in range(4):
                        b.dmaop("tp", lambda E, s, g=g: E.dma_start(
                                    out=UZr[:, g * 512:(g + 1) * 512]
                                        .rearrange("p (t q) -> p t q", t=4),
                                    in_=UZt[:, g * 512:(g + 1) * 512]
                                        .rearrange("p (t q) -> p t q", t=4),
                                    transpose=True).then_inc(s, 16),
                                reads=(("UZt", g),), writes=(("UZr", g),))
                    uzk = tuple(("UZr", g) for g in range(4))
                    b.op("ve", lambda E, s, sl=sl: E.tensor_tensor(
                            Erow[:], UZr[:], Fb[sl][:], ALU.subtract).then_inc(s, 1),
                         reads=uzk + (("Fb", sl),), writes=(("Erow",),))
                    b.op("ve", lambda E, s, sl=sl: E.tensor_tensor(
                            Yrow[:], Erow[:], Vb[sl][:], ALU.add).then_inc(s, 1),
                         reads=(("Erow",), ("Vb", sl)), writes=(("Yrow",),))
                    for g in range(4):
                        b.dmaop("tp", lambda E, s, g=g: E.dma_start(
                                    out=Yt[:, g * 512:(g + 1) * 512]
                                        .rearrange("p (t q) -> p t q", t=4),
                                    in_=Yrow[:, g * 512:(g + 1) * 512]
                                        .rearrange("p (t q) -> p t q", t=4),
                                    transpose=True).then_inc(s, 16),
                                reads=(("Yrow",),), writes=(("Yt", g),))
                    b.op("ve", lambda E, s: E.tensor_copy(
                            Yb[:].rearrange("p (bb t) -> p bb t", bb=B),
                            Yt[:].rearrange("p (t bb) -> p bb t", bb=B)).then_inc(s, 1),
                         reads=tuple(("Yt", g) for g in range(4)), writes=(("Yb",),))
                    b.dmaop("io", lambda E, s, j=j: E.dma_start(
                                out=d_y.rearrange("p (bb t2) -> p bb t2", bb=B)
                                    [:, :, j * 128:(j + 1) * 128],
                                in_=Yb[:].rearrange("p (bb t) -> p bb t", bb=B)).then_inc(s, 16),
                            reads=(("Yb",),), writes=(("d_y", j),))
                    for bank in range(2):
                        for q4 in range(4):
                            dc = bank * 4 + q4
                            def zmm(E, s, dc=dc, bank=bank, q4=q4, sl=sl):
                                ins = None
                                for r in range(16):
                                    ins = E.matmul(ps_h[bank][:, q4 * 128:(q4 + 1) * 128],
                                                   Xr[sl][:, r * 1024 + dc * 128: r * 1024 + (dc + 1) * 128],
                                                   Erow[:, r * 128:(r + 1) * 128],
                                                   start=(r == 0), stop=(r == 15))
                                ins.then_inc(s, 1)
                            b.op("te", zmm, reads=(("Xr", sl), ("Erow",)),
                                 writes=(("ps_hz", bank, q4),))
                            b.op("ve", lambda E, s, dc=dc, bank=bank, q4=q4: E.scalar_tensor_tensor(
                                    Z[:, dc * 128:(dc + 1) * 128],
                                    ps_h[bank][:, q4 * 128:(q4 + 1) * 128], c_negc[:, 0:1],
                                    Z[:, dc * 128:(dc + 1) * 128],
                                    ALU.mult, ALU.add).then_inc(s, 1),
                                 reads=(("ps_hz", bank, q4), "negc", ("Z",)), writes=(("Z",),))
                    b.op("ve", lambda E, s: E.tensor_copy(Zb[:], Z[:]).then_inc(s, 1),
                         reads=(("Z",),), writes=(("Zb",),))
                for dst_i in range(NCORE):
                    for p_ in range(2):
                        bglob = dst_i + 8 * p_
                        b.dmaop("io", lambda E, s, dst_i=dst_i, p_=p_, bglob=bglob: E.dma_start(
                                    out=d_y3in[dst_i * 128:(dst_i + 1) * 128,
                                               p_ * SEQ:(p_ + 1) * SEQ],
                                    in_=d_y[:, bglob * SEQ:(bglob + 1) * SEQ]
                                    ).then_inc(s, 16),
                                reads=tuple(("d_y", j) for j in range(NCHUNK)),
                                writes=(("y3in", dst_i, p_),))
            b.ccop(lambda E, s: _a2a(E, s, d_y3in.opt(), d_y3out.opt()),
                   reads=tuple(("y3in", i, p_) for i in range(NCORE) for p_ in range(2)),
                   writes=("y3out",))

        def scan_post(R):
            with ExitStack() as px:
                yst = [px.enter_context(nc.sbuf_tensor(f"sc_yst{i}", [128, ST], BF16)) for i in range(2)]
                for c in range(NCK):
                    b.dmaop("io", lambda E, s, c=c, R=R: E.dma_start(
                                out=R[:, c * TOK:(c + 1) * TOK],
                                in_=d_rsp[c * 128:(c + 1) * 128, :]).then_inc(s, 16),
                            reads=(("rsp", c), "y3out"), writes=(("R", c),))
                k = 0
                for c in range(NCK):
                    for st_i in range(NST):
                        m0 = st_i * ST
                        sl = k % 2
                        k += 1
                        b.dmaop("io", lambda E, s, c=c, m0=m0, sl=sl: E.dma_start(
                                    out=yst[sl][:],
                                    in_=d_y3out[c * 128:(c + 1) * 128, m0:m0 + ST]).then_inc(s, 16),
                                reads=("y3out",), writes=(("yst", sl),))
                        b.op("ve", lambda E, s, c=c, m0=m0, sl=sl, R=R: E.tensor_tensor(
                                R[:, c * TOK + m0: c * TOK + m0 + ST],
                                R[:, c * TOK + m0: c * TOK + m0 + ST],
                                yst[sl][:], ALU.add).then_inc(s, 1),
                             reads=(("yst", sl), ("R", c)), writes=(("R", c),))

        def attn_phase():
            SCL = 1.0 / math.sqrt(2.0 * D)
            with ExitStack() as px:
                pb = lambda nm, shape, dt: px.enter_context(nc.sbuf_tensor(nm + "_at", shape, dt))
                t_rt = pb("t_rt", [128, NCK * ST], F32)
                t_f1a = pb("t_f1a", [128, ST], F32)
                xnq = pb("xnq", [128, NCK * ST], BF16)
                qkvT = pb("qkvT", [128, 24 * ST], BF16)
                wq = [pb(f"wq{i}", [128, NCK * 128], BF16) for i in range(2)]
                Qh = pb("Qh", [128, SEQ], BF16)
                Kh = pb("Kh", [128, SEQ], BF16)
                Vh = pb("Vh", [128, SEQ], BF16)
                Vrow = pb("Vrow", [128, 16 * 64], BF16)
                Vaug = pb("Vaug", [128, 16 * 65], BF16)
                Pbuf = pb("Pbuf", [128, 16 * 512], BF16)
                oh = pb("oh", [128, SEQ], BF16)
                ohg = pb("ohg", [128, SEQ], BF16)
                gst = pb("gst", [128, SEQ], BF16)
                ost = pb("ost", [128, 512], F32)
                rsum = pb("rsum", [1, 512], F32)
                aacc = pb("aacc", [128, NCK * ST], BF16)
                c_m4 = pb("c_m4", [128, 1], F32)
                b.op("ve", lambda E, s: E.memset(c_m4[:], -4.0).then_inc(s, 1), writes=("m4",))

                for st_i in range(NST):
                    m0 = st_i * ST
                    hh0 = (m0 % SEQ) // 128
                    b.dmaop("io", lambda E, s, m0=m0: E.dma_start(
                                out=t_rt[:].rearrange("p (c m) -> p c m", c=NCK),
                                in_=d_rsp.rearrange("(c p) m -> p c m", p=128)
                                    [:, :, m0:m0 + ST]).then_inc(s, 16),
                            reads=("rspv",), writes=(("rt",),))
                    bc_r, bc_n = emit_ln("atln", ST, lambda c: ("rt",),
                                         lambda c: t_rt[:, c * ST:(c + 1) * ST], False)
                    for c in range(NCK):
                        b.op("ve", lambda E, s, c=c: E.tensor_tensor(
                                t_f1a[:, 0:ST], t_rt[:, c * ST:(c + 1) * ST],
                                ps_bc[0][:, 0:ST], ALU.mult).then_inc(s, 1),
                             reads=(("rt",), bc_r), writes=("t_f1a",))
                        b.op("ve", lambda E, s, c=c: E.tensor_tensor(
                                xnq[:, c * ST:(c + 1) * ST], t_f1a[:, 0:ST],
                                ps_bc[1][:, 0:ST], ALU.add).then_inc(s, 1),
                             reads=("t_f1a", bc_n), writes=("xnq",))
                    for ec in range(24):
                        w = wq[ec % 2]
                        b.dmaop("gw", lambda E, s, ec=ec, w=w: E.dma_start(
                                    out=w[:].rearrange("p (c q) -> p c q", c=NCK),
                                    in_=qkv_k[:, ec * 128:(ec + 1) * 128]
                                        .rearrange("(c p) q -> p c q", p=128)).then_inc(s, 16),
                                reads=(), writes=(("wq", ec % 2),))
                        b.op("te", lambda E, s, ec=ec, w=w: _group_mm(
                                E, s, ps_m[ec % 2][:, 0:ST],
                                [(w[:, c * 128:(c + 1) * 128], xnq[:, c * ST:(c + 1) * ST])
                                 for c in range(NCK)]),
                             reads=("xnq", ("wq", ec % 2)), writes=(("ps_m", ec % 2),))
                        b.op("se", lambda E, s, ec=ec: E.activation(
                                out=qkvT[:, ec * ST:(ec + 1) * ST], in_=ps_m[ec % 2][:, 0:ST],
                                func=ACT.Identity, bias=s_qkvb[:, ec:ec + 1], scale=1.0).then_inc(s, 1),
                             reads=(("ps_m", ec % 2), "qkvb"), writes=("qkvT",))
                    for hi in range(4):
                        for mat, off, dst in (("q", 0, Qh), ("k", 64, Kh), ("v", 128, Vh)):
                            for par in range(2):
                                f0 = 192 * par + off
                                ec0, pp0 = f0 // 128, f0 % 128
                                b.dmaop("tp", lambda E, s, hi=hi, par=par, ec0=ec0, pp0=pp0:
                                        E.dma_start(
                                            out=gst[0:64, par * 1024:(par + 1) * 1024]
                                                .rearrange("p (gg t) -> p gg t", gg=8),
                                            in_=qkvT[pp0:pp0 + 64, :]
                                                .rearrange("p (e m) -> p e m", e=24)
                                                [:, ec0::3, hi * 128:(hi + 1) * 128]).then_inc(s, 16),
                                        reads=("qkvT",), writes=(("gst", par),))
                            b.op("ve", lambda E, s, dst=dst: E.tensor_copy(
                                    dst[0:64, :].rearrange("p (t gg par) -> p gg par t", gg=8, par=2),
                                    gst[0:64, :].rearrange("p (par gg t) -> p gg par t", par=2, gg=8)
                                    ).then_inc(s, 1),
                                 reads=(("gst", 0), ("gst", 1)), writes=((mat,),))
                        b.dmaop("tp", lambda E, s: E.dma_start(
                                    out=Vrow[:].rearrange("p (t q) -> p t q", t=16),
                                    in_=Vh[0:64, :].rearrange("p (t q) -> p t q", t=16),
                                    transpose=True).then_inc(s, 16),
                                reads=(("v",),), writes=("Vrow",))
                        b.op("ve", lambda E, s: E.tensor_copy(
                                Vaug[:].rearrange("p (t q) -> p t q", q=65)[:, :, 0:64],
                                Vrow[:].rearrange("p (t q) -> p t q", q=64)).then_inc(s, 1),
                             reads=("Vrow",), writes=("Vaug",))
                        b.op("ve", lambda E, s: E.memset(
                                Vaug[:].rearrange("p (t q) -> p t q", q=65)[:, :, 64:65],
                                1.0).then_inc(s, 1),
                             reads=(), writes=("Vaug",))
                        for qt in range(4):
                            nkc = 4 * qt + 4
                            for kc in range(nkc):
                                b.op("te", lambda E, s, kc=kc, qt=qt: E.matmul(
                                        ps_m[kc % 2][:, 0:512],
                                        Kh[0:64, kc * 128:(kc + 1) * 128],
                                        Qh[0:64, qt * 512:(qt + 1) * 512],
                                        start=True, stop=True).then_inc(s, 1),
                                     reads=(("q",), ("k",)), writes=(("ps_m", kc % 2),))
                                b.op("se", lambda E, s, kc=kc, qt=qt: E.activation(
                                        out=Pbuf[:, kc * 512:(kc + 1) * 512],
                                        in_=ps_m[kc % 2][:, 0:512], func=ACT.Exp,
                                        bias=c_m4[:, 0:1], scale=SCL).then_inc(s, 1),
                                     reads=(("ps_m", kc % 2), "m4"), writes=("Pbuf",))
                            for j in range(4):
                                kc = 4 * qt + j
                                b.op("ve", lambda E, s, kc=kc, j=j: E.tensor_tensor(
                                        Pbuf[:, kc * 512 + j * 128: kc * 512 + (j + 1) * 128],
                                        Pbuf[:, kc * 512 + j * 128: kc * 512 + (j + 1) * 128],
                                        c_tri[:], ALU.mult).then_inc(s, 1),
                                     reads=("Pbuf", "tri"), writes=("Pbuf",))
                                if j > 0:
                                    b.op("ve", lambda E, s, kc=kc, j=j: E.memset(
                                            Pbuf[:, kc * 512: kc * 512 + j * 128], 0.0).then_inc(s, 1),
                                         reads=(), writes=("Pbuf",))
                            def pvmm(E, s, qt=qt, nkc=nkc):
                                ins = None
                                for kc in range(nkc):
                                    ins = E.matmul(ps_h[qt % 2][0:65, 0:512],
                                                   Vaug[:, kc * 65:(kc + 1) * 65],
                                                   Pbuf[:, kc * 512:(kc + 1) * 512],
                                                   start=(kc == 0), stop=(kc == nkc - 1))
                                ins.then_inc(s, 1)
                            b.op("te", pvmm, reads=("Vaug", "Pbuf"), writes=(("ps_h", qt % 2),))
                            b.op("ve", lambda E, s, qt=qt: E.reciprocal(
                                    rsum[0:1, 0:512], ps_h[qt % 2][64:65, 0:512]).then_inc(s, 1),
                                 reads=(("ps_h", qt % 2),), writes=("rsum",))
                            b.op("te", lambda E, s: E.matmul(
                                    ps_bc[0][0:64, 0:512], c_ones_row[0:1, 0:64], rsum[0:1, 0:512],
                                    start=True, stop=True).then_inc(s, 1),
                                 reads=("rsum", "ones_row"), writes=("bc_r",))
                            b.op("se", lambda E, s, qt=qt: E.activation(
                                    out=ost[0:64, 0:512], in_=ps_h[qt % 2][0:64, 0:512],
                                    func=ACT.Identity).then_inc(s, 1),
                                 reads=(("ps_h", qt % 2),), writes=("ost",))
                            b.op("ve", lambda E, s, qt=qt: E.tensor_tensor(
                                    oh[0:64, qt * 512:(qt + 1) * 512], ost[0:64, 0:512],
                                    ps_bc[0][0:64, 0:512], ALU.mult).then_inc(s, 1),
                                 reads=("ost", "bc_r"), writes=("oh",))
                        b.op("ve", lambda E, s: E.tensor_copy(
                                ohg[0:64, :].rearrange("p (g t) -> p g t", g=16),
                                oh[0:64, :].rearrange("p (t g) -> p g t", g=16)).then_inc(s, 1),
                             reads=("oh",), writes=("ohg",))
                        for par in range(2):
                            b.dmaop("tp", lambda E, s, hi=hi, par=par: E.dma_start(
                                        out=aacc[par * 64:par * 64 + 64, :]
                                            .rearrange("p (e m) -> p e m", e=NCK)
                                            [:, :, hi * 128:(hi + 1) * 128],
                                        in_=ohg[0:64, :]
                                            .rearrange("p (gh par2 t) -> p gh par2 t", gh=8, par2=2)
                                            [:, :, par, :]).then_inc(s, 16),
                                    reads=("ohg",), writes=("aacc",))
                    for c in range(NCK):
                        b.op("ve", lambda E, s, c=c: E.tensor_tensor(
                                t_rt[:, c * ST:(c + 1) * ST], t_rt[:, c * ST:(c + 1) * ST],
                                aacc[:, c * ST:(c + 1) * ST], ALU.add).then_inc(s, 1),
                             reads=(("rt",), "aacc"), writes=(("rt",),))
                    b.dmaop("io", lambda E, s, m0=m0: E.dma_start(
                                out=d_rsp.rearrange("(c p) m -> p c m", p=128)
                                    [:, :, m0:m0 + ST],
                                in_=t_rt[:].rearrange("p (c m) -> p c m", c=NCK)).then_inc(s, 16),
                            reads=(("rt",),), writes=(("rspv",),))

        def spill_R(R):
            for c in range(NCK):
                b.dmaop("io", lambda E, s, c=c, R=R: E.dma_start(
                            out=d_rsp[c * 128:(c + 1) * 128, :],
                            in_=R[:, c * TOK:(c + 1) * TOK]).then_inc(s, 16),
                        reads=(("R", c),), writes=(("rsp", c),))
            b.op("ve", lambda E, s: E.engine_nop().then_inc(s, 1),
                 reads=tuple(("rsp", c) for c in range(NCK)), writes=("rspv",))

        def load_x(R):
            for c in range(NCK):
                b.dmaop("io", lambda E, s, c=c, R=R: E.dma_start(
                            out=R[:, c * TOK:(c + 1) * TOK],
                            in_=xT[c * 128:(c + 1) * 128, :]).then_inc(s, 16),
                        reads=(), writes=(("R", c),))

        if PH_SCAN:
            with ExitStack() as e1:
                R1 = e1.enter_context(nc.sbuf_tensor("R1", [128, NCK * TOK], F32))
                load_x(R1)
                if PH_L0FA:
                    l0ffn_phase(R1)
                scan_pre(R1)
            b.global_barrier()
            pk = scan_phase_a()
            b.global_barrier()
            scan_phase_b(pk)
            b.global_barrier()
            with ExitStack() as e2:
                R2 = e2.enter_context(nc.sbuf_tensor("R2", [128, NCK * TOK], F32))
                scan_post(R2)
                if PH_L0FB:
                    l0ffn_phase(R2)
                spill_R(R2)
            b.global_barrier()
        else:
            with ExitStack() as e1:
                R1 = e1.enter_context(nc.sbuf_tensor("R1", [128, NCK * TOK], F32))
                load_x(R1)
                if PH_L0FA:
                    l0ffn_phase(R1)
                spill_R(R1)
            b.global_barrier()

        if PH_L1FA:
            l1ffn_phase()
            b.global_barrier()
        if PH_ATTN:
            attn_phase()
            b.global_barrier()
        if PH_L1FB:
            l1ffn_phase()
            b.global_barrier()

        for c in range(NCK):
            b.dmaop("io", lambda E, s, c=c: E.dma_start(
                        out=out_T[c * 128:(c + 1) * 128, :],
                        in_=d_rsp[c * 128:(c + 1) * 128, :]).then_inc(s, 16),
                    reads=("rspv",), writes=(("out", c),))
        for c in range(NCK):
            b.raw_wait("sy", ("out", c))

        block = ctx.enter_context(nc.Block())
        b.emit_all(block, sems)
    return nc


def _group_mm(E, s, outap, pairs):
    n = len(pairs)
    for i, (l, r) in enumerate(pairs):
        ins = E.matmul(outap, l, r, start=(i == 0), stop=(i == n - 1))
    ins.then_inc(s, 1)




def _bernoulli_threefry(seed, p, n):
    rot = [[13, 15, 26, 6], [17, 29, 16, 24]]
    ks0, ks1 = np.uint32(0), np.uint32(seed)
    ks2 = np.uint32(ks0 ^ ks1 ^ np.uint32(0x1BD11BDA))
    i = np.arange(n, dtype=np.uint64)
    x0 = (i >> np.uint64(32)).astype(np.uint32)
    x1 = (i & np.uint64(0xFFFFFFFF)).astype(np.uint32)
    x0 = (x0 + ks0).astype(np.uint32)
    x1 = (x1 + ks1).astype(np.uint32)
    ks = [ks0, ks1, ks2]

    def rotl(x, r):
        r = np.uint32(r)
        return ((x << r) | (x >> np.uint32(32 - r))).astype(np.uint32)

    for d in range(5):
        for r in rot[d % 2]:
            x0 = (x0 + x1).astype(np.uint32)
            x1 = rotl(x1, r)
            x1 = (x1 ^ x0).astype(np.uint32)
        x0 = (x0 + ks[(d + 1) % 3]).astype(np.uint32)
        x1 = (x1 + ks[(d + 2) % 3] + np.uint32(d + 1)).astype(np.uint32)
    bits = x0 ^ x1
    u = ((bits >> np.uint32(9)) | np.uint32(0x3F800000)).view(np.float32) - np.float32(1.0)
    return u < np.float32(p)


def _prep_inputs(kw):
    f32 = np.float32
    x = np.asarray(kw["L0_attn_x"], f32)
    W0 = np.asarray(kw["L0_attn_W0"], f32)
    Wkv = np.asarray(kw["L0_attn_Wkv"], f32)
    bkv = np.asarray(kw["L0_attn_bkv"], f32)
    eta = np.asarray(kw["L0_attn_eta"], f32).reshape(1, 1)

    def pcol(v):
        return np.ascontiguousarray(np.asarray(v, f32).reshape(-1, 128).T)

    C = np.ascontiguousarray
    keep = _bernoulli_threefry(42, 0.8, B * SEQ * D).reshape(B, SEQ, D)
    maskv = (keep.astype(np.float32) * 1.25).astype(ml_dtypes.bfloat16)

    kq = np.arange(128)
    tri = (kq[:, None] <= kq[None, :]).astype(np.float32)
    tb = np.arange(128) // B
    msl_np = (tb[:, None] > tb[None, :]).astype(np.float32)

    common = {
        "attn_wT": C(np.asarray(kw["L0_ffn_attn_w"], f32).T),
        "next_wT": C(np.asarray(kw["L0_ffn_next_w"], f32).T),
        "ln_s": pcol(kw["L0_ffn_ln_scale"]), "ln_b": pcol(kw["L0_ffn_ln_bias"]),
        "attn_b": pcol(kw["L0_ffn_attn_b"]), "next_b": pcol(kw["L0_ffn_next_b"]),
        "wkv_vT": C(Wkv[D:, :].T), "bkv_row": C(bkv[D:].reshape(1, D)),
        "eta_in": eta,
        "qkv_k": C(np.asarray(kw["L1_attn_qkv_kernel"], f32)),
        "qkv_b": pcol(np.asarray(kw["L1_attn_qkv_bias"], f32).reshape(3 * D)),
        "W1T": C(np.asarray(kw["L1_ffn_W1"], f32).T),
        "W2T": C(np.asarray(kw["L1_ffn_W2"], f32).T),
        "l1b1": pcol(kw["L1_ffn_b1"]), "l1b2": pcol(kw["L1_ffn_b2"]),
        "ones_col": np.ones((128, 1), f32),
        "ones_colb": np.ones((128, 1), ml_dtypes.bfloat16),
        "ones_row": np.ones((1, 128), f32),
        "ones_rowb": np.ones((1, 128), ml_dtypes.bfloat16),
        "identb": np.eye(128, dtype=ml_dtypes.bfloat16),
        "msl": msl_np.astype(ml_dtypes.bfloat16),
        "msu": msl_np.T.astype(ml_dtypes.bfloat16),
        "trimask": tri.astype(ml_dtypes.bfloat16),
    }
    w1 = np.asarray(kw["L0_ffn_w1"], f32)
    w2 = np.asarray(kw["L0_ffn_w2"], f32)
    b1 = np.asarray(kw["L0_ffn_b1"], f32)
    b2 = np.asarray(kw["L0_ffn_b2"], f32)
    in_maps = []
    for i in range(NCORE):
        bpair = [i, i + 8]
        m = dict(common)
        m["xT"] = C(x[bpair].reshape(TOK, D).T)
        m["w1e"] = C(w1[i])
        m["w2e"] = C(w2[i])
        m["b1e"] = pcol(b1[i])
        m["b2e"] = pcol(b2[i])
        m["z0"] = C(W0[i * 128:(i + 1) * 128, :].T)
        m["maskT"] = C(maskv[bpair].reshape(TOK, D).T)
        in_maps.append(m)
    return in_maps


def kernel(**inputs):
    in_maps = _prep_inputs(inputs)
    nc = build_program()
    res = run_bass_kernel_spmd(nc, in_maps, core_ids=list(range(NCORE)))
    out = np.empty((B, SEQ, D), np.float32)
    for i in range(NCORE):
        oT = res.results[i]["outT"]
        out[[i, i + 8]] = np.asarray(oT).T.reshape(2, SEQ, D)
    return out


# version 05
# speedup vs baseline: 1.3322x; 1.3264x over previous
import math
import numpy as np
import ml_dtypes
from contextlib import ExitStack

import concourse.bass as bass
import concourse.mybir as mybir
from concourse.bass_utils import run_bass_kernel_spmd

F32 = mybir.dt.float32
BF16 = mybir.dt.bfloat16
I32 = mybir.dt.int32
ALU = mybir.AluOpType
ACT = mybir.ActivationFunctionType
AX = mybir.AxisListType.X

NCORE = 8
B, SEQ, D = 16, 2048, 1024
TOK = 2 * SEQ
NCK = D // 128
ST = 512
NST = TOK // ST
H0 = 4 * D
NH0 = H0 // 128
EPS = 1e-5
INV_D = 1.0 / D

KSTEP = 128
MC = KSTEP * B
NCHUNK = SEQ // KSTEP
SUBS = 8
SBR = SUBS * B
NSUB = KSTEP // SUBS

PH_L0FA = True
PH_SCAN = True
PH_L0FB = True
PH_L1FA = True
PH_ATTN = True
PH_L1FB = True

ENGS = ("te", "ve", "se", "ge", "sy")
ENG_ATTR = {"te": "tensor", "ve": "vector", "se": "scalar", "ge": "gpsimd", "sy": "sync"}


class Bld:

    def __init__(self, nc):
        self.nc = nc
        self.ops = {e: [] for e in ENGS}
        self.count = {e: 0 for e in ENGS}
        self.credits = {e: {} for e in ENGS}
        self.tickets = {}
        self.readers = {}
        self.sem_names = ["E_" + e for e in ENGS] + ["E_cc"]
        self.dma = {}
        self.cc_count = 0

    def add_dma_stream(self, name, eng, ring):
        sems = [f"D_{name}_{i}" for i in range(ring)]
        self.sem_names += sems
        self.dma[name] = dict(eng=eng, ring=sems, idx=0, cnt=[0] * ring)

    def _wait(self, eng, sem, val):
        if self.credits[eng].get(sem, 0) >= val:
            return
        self.credits[eng][sem] = val
        self.ops[eng].append(("w", sem, val))

    def _dep(self, eng, reads, writes):
        for k in reads:
            t = self.tickets.get(k)
            if t is not None:
                self._wait(eng, t[0], t[1])
        for k in writes:
            t = self.tickets.get(k)
            if t is not None:
                self._wait(eng, t[0], t[1])
            for sn, v in self.readers.get(k, {}).items():
                self._wait(eng, sn, v)

    def op(self, eng, emit, reads=(), writes=()):
        self._dep(eng, reads, writes)
        self.count[eng] += 1
        n = self.count[eng]
        sem = "E_" + eng
        self.ops[eng].append(("o", emit, sem))
        for k in reads:
            self.readers.setdefault(k, {})[sem] = n
        for k in writes:
            self.tickets[k] = (sem, n)
            self.readers[k] = {}
        return n

    def dmaop(self, stream, emit, reads=(), writes=()):
        st = self.dma[stream]
        eng = st["eng"]
        slot = st["idx"] % len(st["ring"])
        st["idx"] += 1
        semn = st["ring"][slot]
        st["cnt"][slot] += 16
        target = st["cnt"][slot]
        self._dep(eng, reads, writes)
        if target > 16:
            self._wait(eng, semn, target - 16)
        self.ops[eng].append(("d", emit, semn))
        for k in reads:
            self.readers.setdefault(k, {})[semn] = target
        for k in writes:
            self.tickets[k] = (semn, target)
            self.readers[k] = {}
        return semn, target

    def ccop(self, emit, reads=(), writes=()):
        self._dep("ge", reads, writes)
        self.cc_count += 1
        n = self.cc_count
        self.ops["ge"].append(("o", emit, "E_cc"))
        for k in reads:
            self.readers.setdefault(k, {})["E_cc"] = n
        for k in writes:
            self.tickets[k] = ("E_cc", n)
            self.readers[k] = {}
        return n

    def global_barrier(self):
        targets = [("E_" + e, self.count[e]) for e in ENGS if self.count[e] > 0]
        targets.append(("E_cc", self.cc_count))
        for st in self.dma.values():
            for i, sn in enumerate(st["ring"]):
                if st["cnt"][i] > 0:
                    targets.append((sn, st["cnt"][i]))
        for e in ENGS:
            for sn, v in targets:
                if v > 0:
                    self._wait(e, sn, v)

    def raw_wait(self, eng, key):
        t = self.tickets[key]
        self._wait(eng, t[0], t[1])

    def emit_all(self, block, sems):
        for e in ENGS:
            oplist = self.ops[e]

            def make_body(oplist=oplist):
                def body(E):
                    for item in oplist:
                        if item[0] == "w":
                            E.wait_ge(sems[item[1]], item[2])
                        else:
                            item[1](E, sems[item[2]])
                return body

            getattr(block, ENG_ATTR[e])(make_body())


def build_program():
    nc = bass.Bass("TRN2", target_bir_lowering=False, debug=False)

    def dram_in(name, shape, dt=F32):
        return nc.dram_tensor(name, list(shape), dt, kind="ExternalInput").ap()

    xT = dram_in("xT", [D, TOK])
    attn_wT = dram_in("attn_wT", [D, D])
    next_wT = dram_in("next_wT", [D, D])
    w1e = dram_in("w1e", [D, H0])
    w2e = dram_in("w2e", [H0, D])
    ln_s = dram_in("ln_s", [128, NCK])
    ln_b = dram_in("ln_b", [128, NCK])
    attn_b = dram_in("attn_b", [128, NCK])
    next_b = dram_in("next_b", [128, NCK])
    b1e = dram_in("b1e", [128, NH0])
    b2e = dram_in("b2e", [128, NCK])
    wkv_vT = dram_in("wkv_vT", [D, D])
    bkv_row = dram_in("bkv_row", [1, D])
    z0 = dram_in("z0", [D, 128])
    eta_in = dram_in("eta_in", [1, 1])
    qkv_k = dram_in("qkv_k", [D, 3 * D])
    qkv_b = dram_in("qkv_b", [128, 24])
    W1T = dram_in("W1T", [D, H0])
    W2T = dram_in("W2T", [H0, D])
    l1b1 = dram_in("l1b1", [128, NH0])
    l1b2 = dram_in("l1b2", [128, NCK])
    maskT = dram_in("maskT", [D, TOK], BF16)
    ones_col = dram_in("ones_col", [128, 1])
    ones_colb = dram_in("ones_colb", [128, 1], BF16)
    ones_row = dram_in("ones_row", [1, 128])
    ones_rowb = dram_in("ones_rowb", [1, 128], BF16)
    identb = dram_in("identb", [128, 128], BF16)
    msl = dram_in("msl", [128, 128], BF16)
    msu = dram_in("msu", [128, 128], BF16)
    trimask = dram_in("trimask", [128, 128], BF16)

    out_T = nc.dram_tensor("outT", [D, TOK], F32, kind="ExternalOutput").ap()

    d_x1in = nc.dram_tensor("d_x1in", [NCORE * D, 512], BF16).ap()
    d_x1out = nc.dram_tensor("d_x1out", [NCORE * D, 512], BF16).ap()
    PUB1 = D + MC
    d_pub1 = nc.dram_tensor("d_pub1", [2 * PUB1, MC], BF16).ap()
    d_agp1 = nc.dram_tensor("d_agp1", [NCORE * 2 * PUB1, MC], BF16).ap()
    d_p2in = nc.dram_tensor("d_p2in", [NCORE * 2 * 2 * MC, 128], BF16).ap()
    d_p2out = nc.dram_tensor("d_p2out", [NCORE * 2 * 2 * MC, 128], BF16).ap()
    d_y = nc.dram_tensor("d_y", [128, B * SEQ], BF16).ap()
    d_y3in = nc.dram_tensor("d_y3in", [NCORE * 128, TOK], BF16).ap()
    d_y3out = nc.dram_tensor("d_y3out", [NCORE * 128, TOK], BF16).ap()
    d_rsp = nc.dram_tensor("d_rsp", [D, TOK], F32).ap()
    d_xq = nc.dram_tensor("d_xq", [D, TOK], BF16).ap()
    d_w1q = nc.dram_tensor("d_w1q", [D, H0], BF16).ap()
    d_w2q = nc.dram_tensor("d_w2q", [H0, D], BF16).ap()

    b = Bld(nc)
    b.add_dma_stream("io", "sy", 4)
    b.add_dma_stream("wt", "se", 4)
    b.add_dma_stream("gw", "ge", 6)
    b.add_dma_stream("tp", "sy", 4)

    with ExitStack() as ctx:
        sems = {sn: ctx.enter_context(nc.semaphore(sn)) for sn in b.sem_names}
        sbuf = lambda name, shape, dt: ctx.enter_context(nc.sbuf_tensor(name, shape, dt))
        pst = lambda name, shape: ctx.enter_context(nc.psum_tensor(name, shape, F32))

        c_ones = sbuf("c_ones", [128, 1], F32)
        c_onesb = sbuf("c_onesb", [128, 1], BF16)
        c_ones_row = sbuf("c_ones_row", [1, 128], F32)
        c_ones_rowb = sbuf("c_ones_rowb", [1, 128], BF16)
        c_ident = sbuf("c_ident", [128, 128], BF16)
        c_msl = sbuf("c_msl", [128, 128], BF16)
        c_msu = sbuf("c_msu", [128, 128], BF16)
        c_tri = sbuf("c_tri", [128, 128], BF16)
        s_lns = sbuf("s_lns", [128, NCK], F32)
        s_lnb = sbuf("s_lnb", [128, NCK], F32)
        s_attnb = sbuf("s_attnb", [128, NCK], F32)
        s_nextb = sbuf("s_nextb", [128, NCK], F32)
        s_b1 = sbuf("s_b1", [128, NH0], F32)
        s_b2 = sbuf("s_b2", [128, NCK], F32)
        s_qkvb = sbuf("s_qkvb", [128, 24], F32)
        s_l1b1 = sbuf("s_l1b1", [128, NH0], F32)
        s_l1b2 = sbuf("s_l1b2", [128, NCK], F32)
        s_eta = sbuf("s_eta", [1, 1], F32)
        c_negc = sbuf("c_negc", [128, 1], F32)
        s_l1b1q = sbuf("s_l1b1q", [128, NH0], F32)
        s_l1b2q = sbuf("s_l1b2q", [128, NCK], F32)

        r_mu = sbuf("r_mu", [1, ST], F32)
        r_var = sbuf("r_var", [1, ST], F32)
        r_rstd = sbuf("r_rstd", [1, ST], F32)
        r_nmur = sbuf("r_nmur", [1, ST], F32)

        ps_m = [pst(f"ps_m{i}", [128, 512]) for i in range(2)]
        ps_h = [pst(f"ps_h{i}", [128, 512]) for i in range(2)]
        ps_sum = pst("ps_sum", [1, ST])
        ps_sq = pst("ps_sq", [1, ST])
        ps_bc = [pst(f"ps_bc{i}", [128, ST]) for i in range(2)]

        def ld(stream, dst, src, key):
            b.dmaop(stream, lambda E, s, d=dst, so=src: E.dma_start(out=d, in_=so).then_inc(s, 16),
                    reads=(), writes=(key,))

        ld("io", c_ones[:], ones_col, "ones")
        ld("io", c_onesb[:], ones_colb, "onesb")
        ld("io", c_ones_row[:], ones_row, "ones_row")
        ld("io", c_ones_rowb[:], ones_rowb, "ones_rowb")
        ld("io", c_ident[:], identb, "ident")
        ld("io", c_msl[:], msl, "msl")
        ld("io", c_msu[:], msu, "msu")
        ld("io", c_tri[:], trimask, "tri")
        ld("io", s_lns[:], ln_s, "lns")
        ld("io", s_lnb[:], ln_b, "lnb")
        ld("io", s_attnb[:], attn_b, "attnb")
        ld("io", s_nextb[:], next_b, "nextb")
        ld("io", s_b1[:], b1e, "b1")
        ld("io", s_b2[:], b2e, "b2")
        ld("io", s_qkvb[:], qkv_b, "qkvb")
        ld("io", s_l1b1[:], l1b1, "l1b1")
        ld("io", s_l1b2[:], l1b2, "l1b2")
        ld("io", s_eta[:], eta_in, "eta")

        b.op("te", lambda E, s: E.matmul(ps_bc[0][:, 0:1], c_ones_row[:], s_eta[:],
                                         start=True, stop=True).then_inc(s, 1),
             reads=("ones_row", "eta"), writes=("bc_r",))
        b.op("ve", lambda E, s: E.tensor_scalar(c_negc[:], ps_bc[0][:, 0:1], -2.0 / (B * D), None,
                                                ALU.mult).then_inc(s, 1),
             reads=("bc_r",), writes=("negc",))

        sq_slots = [sbuf(f"sqs{i}", [128, ST], F32) for i in range(1)]

        def emit_ln(tag, w, src_key, src_ap, src_bf16):
            ones_ap = c_onesb[:] if src_bf16 else c_ones[:]
            ones_key = "onesb" if src_bf16 else "ones"
            for c in range(NCK):
                b.op("te", lambda E, s, c=c: E.matmul(
                        ps_sum[0:1, 0:w], ones_ap, src_ap(c),
                        start=(c == 0), stop=(c == NCK - 1)).then_inc(s, 1),
                     reads=(src_key(c), ones_key), writes=("ps_sum",))
                sl = sq_slots[0]
                b.op("se", lambda E, s, c=c, sl=sl: E.activation(
                        out=sl[:, 0:w], in_=src_ap(c), func=ACT.Square).then_inc(s, 1),
                     reads=(src_key(c),), writes=(("sqslot", 0),))
                b.op("te", lambda E, s, c=c, sl=sl: E.matmul(
                        ps_sq[0:1, 0:w], c_ones[:], sl[:, 0:w],
                        start=(c == 0), stop=(c == NCK - 1)).then_inc(s, 1),
                     reads=(("sqslot", 0), "ones"), writes=("ps_sq",))
            b.op("ve", lambda E, s: E.tensor_scalar(r_mu[0:1, 0:w], ps_sum[0:1, 0:w], INV_D, None,
                                                    ALU.mult).then_inc(s, 1),
                 reads=("ps_sum",), writes=("r_mu",))
            b.op("ve", lambda E, s: E.tensor_scalar(r_var[0:1, 0:w], ps_sq[0:1, 0:w], INV_D, EPS,
                                                    ALU.mult, ALU.add).then_inc(s, 1),
                 reads=("ps_sq",), writes=("r_var",))
            b.op("ve", lambda E, s: E.scalar_tensor_tensor(r_rstd[0:1, 0:w], r_mu[0:1, 0:w], 1.0,
                                                           r_mu[0:1, 0:w], ALU.mult, ALU.mult).then_inc(s, 1),
                 reads=("r_mu",), writes=("r_rstd",))
            b.op("ve", lambda E, s: E.tensor_tensor(r_var[0:1, 0:w], r_var[0:1, 0:w], r_rstd[0:1, 0:w],
                                                    ALU.subtract).then_inc(s, 1),
                 reads=("r_var", "r_rstd"), writes=("r_var",))
            b.op("se", lambda E, s: E.activation(out=r_rstd[0:1, 0:w], in_=r_var[0:1, 0:w],
                                                 func=ACT.Sqrt).then_inc(s, 1),
                 reads=("r_var",), writes=("r_rstd",))
            b.op("ve", lambda E, s: E.reciprocal(r_rstd[0:1, 0:w], r_rstd[0:1, 0:w]).then_inc(s, 1),
                 reads=("r_rstd",), writes=("r_rstd",))
            b.op("ve", lambda E, s: E.scalar_tensor_tensor(r_nmur[0:1, 0:w], r_mu[0:1, 0:w], -1.0,
                                                           r_rstd[0:1, 0:w], ALU.mult, ALU.mult).then_inc(s, 1),
                 reads=("r_mu", "r_rstd"), writes=("r_nmur",))
            b.op("te", lambda E, s: E.matmul(ps_bc[0][:, 0:w], c_ones_row[:], r_rstd[0:1, 0:w],
                                             start=True, stop=True).then_inc(s, 1),
                 reads=("r_rstd", "ones_row"), writes=("bc_r",))
            b.op("te", lambda E, s: E.matmul(ps_bc[1][:, 0:w], c_ones_row[:], r_nmur[0:1, 0:w],
                                             start=True, stop=True).then_inc(s, 1),
                 reads=("r_nmur", "ones_row"), writes=("bc_n",))
            return "bc_r", "bc_n"

        def emit_quant(pool, src_key, src_ap, dst_key, dst_ap, M, EMINB, signed, W):
            pf, pi = pool["f"], pool["i"]
            a_i, rb_i, v_i = pi[0], pi[1], pi[2]
            m_f, s_f, d_f = pf[0], pf[1], pf[2]
            xb = src_ap.bitcast(I32)
            K = lambda nm: ("q", nm)
            drop = 23 - M
            addc = (1 << (drop - 1)) - 1
            maskc = -(1 << drop)

            def o(emit, reads, writes):
                b.op("ve", emit, reads=reads, writes=writes)

            o(lambda E, s: E.tensor_scalar(rb_i[:, :W], xb, drop, None,
                                           ALU.logical_shift_right).then_inc(s, 1),
              [src_key], [K("rb")])
            o(lambda E, s: E.tensor_scalar(rb_i[:, :W], rb_i[:, :W], 1, None,
                                           ALU.bitwise_and).then_inc(s, 1), [K("rb")], [K("rb")])
            o(lambda E, s: E.tensor_tensor(v_i[:, :W], xb, rb_i[:, :W], ALU.add).then_inc(s, 1),
              [src_key, K("rb")], [K("v")])
            o(lambda E, s: E.tensor_scalar(v_i[:, :W], v_i[:, :W], addc, None,
                                           ALU.add).then_inc(s, 1), [K("v")], [K("v")])
            if EMINB == 1.0:
                o(lambda E, s: E.tensor_scalar(dst_ap.bitcast(I32), v_i[:, :W], maskc, None,
                                               ALU.bitwise_and).then_inc(s, 1), [K("v")], [dst_key])
                return
            o(lambda E, s: E.tensor_scalar(v_i[:, :W], v_i[:, :W], maskc, None,
                                           ALU.bitwise_and).then_inc(s, 1), [K("v")], [K("v")])
            o(lambda E, s: E.tensor_scalar(a_i[:, :W], xb, 0x7FFFFFFF, None,
                                           ALU.bitwise_and).then_inc(s, 1), [src_key], [K("a")])
            o(lambda E, s: E.tensor_scalar(m_f[:, :W], a_i[:, :W].bitcast(F32), float(2.0 ** -14),
                                           None, ALU.is_lt).then_inc(s, 1), [K("a")], [K("m")])
            o(lambda E, s: E.tensor_scalar(s_f[:, :W], src_ap, 192.0, None,
                                           ALU.add).then_inc(s, 1), [src_key], [K("s")])
            o(lambda E, s: E.tensor_scalar(s_f[:, :W], s_f[:, :W], 192.0, None,
                                           ALU.subtract).then_inc(s, 1), [K("s")], [K("s")])
            o(lambda E, s: E.tensor_tensor(d_f[:, :W], s_f[:, :W], v_i[:, :W].bitcast(F32),
                                           ALU.subtract).then_inc(s, 1), [K("s"), K("v")], [K("d")])
            o(lambda E, s: E.tensor_tensor(d_f[:, :W], d_f[:, :W], m_f[:, :W],
                                           ALU.mult).then_inc(s, 1), [K("d"), K("m")], [K("d")])
            o(lambda E, s: E.tensor_tensor(dst_ap, v_i[:, :W].bitcast(F32), d_f[:, :W],
                                           ALU.add).then_inc(s, 1), [K("v"), K("d")], [dst_key])

        def emit_wquant(pool, src_dram, dst_dram, rows, cols, tag):
            W = min(2048, cols)
            tin = pool["f"][6]
            tout = pool["bf"]
            for r0 in range(0, rows, 128):
                for c0 in range(0, cols, W):
                    b.dmaop("io", lambda E, s, r0=r0, c0=c0, W=W: E.dma_start(
                                out=tin[:, 0:W], in_=src_dram[r0:r0 + 128, c0:c0 + W]).then_inc(s, 16),
                            reads=(), writes=(("wq_in", tag),))
                    emit_quant(pool, ("wq_in", tag), tin[:, 0:W], ("wq_out", tag), tout[:, 0:W],
                               2, 113.0, True, W)
                    b.dmaop("io", lambda E, s, r0=r0, c0=c0, W=W: E.dma_start(
                                out=dst_dram[r0:r0 + 128, c0:c0 + W], in_=tout[:, 0:W]).then_inc(s, 16),
                            reads=(("wq_out", tag),), writes=(("wq_st", tag),))

        l0buf = {}
        l0ctr = [0]

        def l0ffn_alloc(px):
            sfx = str(l0ctr[0])
            l0ctr[0] += 1
            l0buf["t_xin"] = px.enter_context(nc.sbuf_tensor("t_xin" + sfx, [128, NCK * ST], BF16))
            l0buf["t_f1"] = px.enter_context(nc.sbuf_tensor("t_f1" + sfx, [128, ST], F32))
            l0buf["t_xn"] = px.enter_context(nc.sbuf_tensor("t_xn" + sfx, [128, NCK * ST], BF16))
            l0buf["t_xa"] = px.enter_context(nc.sbuf_tensor("t_xa" + sfx, [128, NCK * ST], BF16))
            l0buf["t_h"] = px.enter_context(nc.sbuf_tensor("t_h" + sfx, [128, 8 * ST], BF16))
            l0buf["t_acc"] = px.enter_context(nc.sbuf_tensor("t_acc" + sfx, [128, NCK * ST], F32))
            l0buf["w_sq"] = [px.enter_context(nc.sbuf_tensor(f"w_sq{i}" + sfx, [128, NCK * 128], BF16))
                             for i in range(2)]
            l0buf["w_1"] = [px.enter_context(nc.sbuf_tensor(f"w_1{i}" + sfx, [128, NCK * 128], BF16))
                            for i in range(2)]
            l0buf["w2s"] = [px.enter_context(nc.sbuf_tensor(f"w2s{i}" + sfx, [128, D], BF16))
                            for i in range(4)]

        def l0ffn_supertile(m0, R):
            t_xin = l0buf["t_xin"]; t_f1 = l0buf["t_f1"]; t_xn = l0buf["t_xn"]
            t_xa = l0buf["t_xa"]; t_h = l0buf["t_h"]; t_acc = l0buf["t_acc"]
            t_eo = l0buf["t_xn"]; t_g = l0buf["t_f1"]
            w_sq = l0buf["w_sq"]; w_1 = l0buf["w_1"]; w2_slots = l0buf["w2s"]

            bc_r, bc_n = emit_ln(
                "lnA", ST,
                lambda c: ("R", c),
                lambda c, m0=m0, R=R: R[:, c * TOK + m0: c * TOK + m0 + ST], False)
            for c in range(NCK):
                b.op("ve", lambda E, s, c=c, m0=m0, R=R: E.tensor_tensor(
                        t_f1[:, 0:ST], R[:, c * TOK + m0: c * TOK + m0 + ST],
                        ps_bc[0][:, 0:ST], ALU.mult).then_inc(s, 1),
                     reads=(("R", c), bc_r), writes=("t_f1",))
                b.op("ve", lambda E, s, c=c: E.tensor_tensor(
                        t_xin[:, c * ST:(c + 1) * ST], t_f1[:, 0:ST],
                        ps_bc[1][:, 0:ST], ALU.add).then_inc(s, 1),
                     reads=("t_f1", bc_n), writes=(("xin", c),))
            bc_r2, bc_n2 = emit_ln(
                "lnB", ST,
                lambda c: ("xin", c),
                lambda c: t_xin[:, c * ST:(c + 1) * ST], True)
            for c in range(NCK):
                b.op("ve", lambda E, s, c=c: E.tensor_tensor(
                        t_f1[:, 0:ST], t_xin[:, c * ST:(c + 1) * ST],
                        ps_bc[0][:, 0:ST], ALU.mult).then_inc(s, 1),
                     reads=(("xin", c), bc_r2), writes=("t_f1",))
                b.op("ve", lambda E, s, c=c: E.tensor_tensor(
                        t_f1[:, 0:ST], t_f1[:, 0:ST],
                        ps_bc[1][:, 0:ST], ALU.add).then_inc(s, 1),
                     reads=("t_f1", bc_n2), writes=("t_f1",))
                b.op("se", lambda E, s, c=c: E.activation(
                        out=t_xn[:, c * ST:(c + 1) * ST], in_=t_f1[:, 0:ST], func=ACT.Identity,
                        bias=s_lnb[:, c:c + 1], scale=s_lns[:, c:c + 1]).then_inc(s, 1),
                     reads=("t_f1", "lns", "lnb"), writes=(("xn", c),))

            for ec in range(NCK):
                w = w_sq[ec % 2]
                b.dmaop("gw", lambda E, s, ec=ec, w=w: E.dma_start(
                            out=w[:].rearrange("p (c q) -> p c q", c=NCK),
                            in_=attn_wT[:, ec * 128:(ec + 1) * 128]
                                .rearrange("(c p) q -> p c q", p=128)).then_inc(s, 16),
                        reads=(), writes=(("w_sq", ec % 2),))
                b.op("te", lambda E, s, ec=ec, w=w: _group_mm(
                        E, s, ps_m[ec % 2][:, 0:ST],
                        [(w[:, c * 128:(c + 1) * 128], t_xn[:, c * ST:(c + 1) * ST])
                         for c in range(NCK)]),
                     reads=tuple(("xn", c) for c in range(NCK)) + (("w_sq", ec % 2),),
                     writes=(("ps_m", ec % 2),))
                b.op("se", lambda E, s, ec=ec: E.activation(
                        out=t_g[:, 0:ST], in_=ps_m[ec % 2][:, 0:ST], func=ACT.Identity,
                        bias=s_attnb[:, ec:ec + 1], scale=1.0).then_inc(s, 1),
                     reads=(("ps_m", ec % 2), "attnb"), writes=("t_f1",))
                b.op("ve", lambda E, s, ec=ec: E.tensor_tensor(
                        t_xa[:, ec * ST:(ec + 1) * ST], t_g[:, 0:ST],
                        t_xin[:, ec * ST:(ec + 1) * ST], ALU.add).then_inc(s, 1),
                     reads=("t_f1", ("xin", ec)), writes=(("xa", ec),))

            for qq in range(4):
                for half in range(2):
                    for hx in range(4):
                        hh = half * 4 + hx
                        hc = qq * 8 + hh
                        w = w_1[hx % 2]
                        b.dmaop("gw", lambda E, s, hc=hc, w=w: E.dma_start(
                                    out=w[:].rearrange("p (c q) -> p c q", c=NCK),
                                    in_=w1e[:, hc * 128:(hc + 1) * 128]
                                        .rearrange("(c p) q -> p c q", p=128)).then_inc(s, 16),
                                reads=(), writes=(("w_1", hx % 2),))
                        b.op("te", lambda E, s, hx=hx, w=w: _group_mm(
                                E, s, ps_h[hx % 2][:, 0:ST],
                                [(w[:, c * 128:(c + 1) * 128], t_xa[:, c * ST:(c + 1) * ST])
                                 for c in range(NCK)]),
                             reads=tuple(("xa", c) for c in range(NCK)) + (("w_1", hx % 2),),
                             writes=(("ps_h", hx % 2),))
                        b.op("se", lambda E, s, hc=hc, hx=hx: E.activation(
                                out=t_h[:, hx * ST:(hx + 1) * ST], in_=ps_h[hx % 2][:, 0:ST],
                                func=ACT.Gelu, bias=s_b1[:, hc:hc + 1], scale=1.0).then_inc(s, 1),
                             reads=(("ps_h", hx % 2), "b1"), writes=(("h", hx),))
                        b.dmaop("gw", lambda E, s, hc=hc, hx=hx: E.dma_start(
                                    out=w2_slots[hx][:],
                                    in_=w2e[hc * 128:(hc + 1) * 128, :]).then_inc(s, 16),
                                reads=(), writes=(("w2s", hx),))
                    for dc in range(NCK):
                        b.op("te", lambda E, s, dc=dc: _group_mm(
                                E, s, ps_m[dc % 2][:, 0:ST],
                                [(w2_slots[hx][:, dc * 128:(dc + 1) * 128],
                                  t_h[:, hx * ST:(hx + 1) * ST]) for hx in range(4)]),
                             reads=tuple(("h", hx) for hx in range(4)) +
                                   tuple(("w2s", hx) for hx in range(4)),
                             writes=(("ps_m", dc % 2),))
                        if qq == 0 and half == 0:
                            b.op("se", lambda E, s, dc=dc: E.activation(
                                    out=t_acc[:, dc * ST:(dc + 1) * ST],
                                    in_=ps_m[dc % 2][:, 0:ST], func=ACT.Identity).then_inc(s, 1),
                                 reads=(("ps_m", dc % 2),), writes=(("acc", dc),))
                        else:
                            b.op("ve", lambda E, s, dc=dc: E.tensor_tensor(
                                    t_acc[:, dc * ST:(dc + 1) * ST],
                                    t_acc[:, dc * ST:(dc + 1) * ST],
                                    ps_m[dc % 2][:, 0:ST], ALU.add).then_inc(s, 1),
                                 reads=(("ps_m", dc % 2), ("acc", dc)), writes=(("acc", dc),))

            for dc in range(NCK):
                b.op("se", lambda E, s, dc=dc: E.activation(
                        out=t_eo[:, dc * ST:(dc + 1) * ST],
                        in_=t_acc[:, dc * ST:(dc + 1) * ST], func=ACT.Identity,
                        bias=s_b2[:, dc:dc + 1], scale=1.0).then_inc(s, 1),
                     reads=(("acc", dc), "b2"), writes=(("xn", dc),))
            for ec in range(NCK):
                w = w_sq[ec % 2]
                b.dmaop("gw", lambda E, s, ec=ec, w=w: E.dma_start(
                            out=w[:].rearrange("p (c q) -> p c q", c=NCK),
                            in_=next_wT[:, ec * 128:(ec + 1) * 128]
                                .rearrange("(c p) q -> p c q", p=128)).then_inc(s, 16),
                        reads=(), writes=(("w_sq", ec % 2),))
                b.op("te", lambda E, s, ec=ec, w=w: _group_mm(
                        E, s, ps_h[ec % 2][:, 0:ST],
                        [(w[:, c * 128:(c + 1) * 128], t_eo[:, c * ST:(c + 1) * ST])
                         for c in range(NCK)]),
                     reads=tuple(("xn", c) for c in range(NCK)) + (("w_sq", ec % 2),),
                     writes=(("ps_h", ec % 2),))
                b.op("se", lambda E, s, ec=ec: E.activation(
                        out=t_g[:, 0:ST], in_=ps_h[ec % 2][:, 0:ST], func=ACT.Gelu,
                        bias=s_nextb[:, ec:ec + 1], scale=1.0).then_inc(s, 1),
                     reads=(("ps_h", ec % 2), "nextb"), writes=("t_f1",))
                b.op("ve", lambda E, s, ec=ec, m0=m0, R=R: E.scalar_tensor_tensor(
                        R[:, ec * TOK + m0: ec * TOK + m0 + ST], t_g[:, 0:ST], 0.5,
                        R[:, ec * TOK + m0: ec * TOK + m0 + ST],
                        ALU.mult, ALU.add).then_inc(s, 1),
                     reads=("t_f1", ("R", ec)), writes=(("R", ec),))

        def l0ffn_phase(R):
            with ExitStack() as px:
                l0ffn_alloc(px)
                for st_i in range(NST):
                    l0ffn_supertile(st_i * ST, R)

        l1ctr = [0]

        def l1ffn_phase():
            call = l1ctr[0]
            l1ctr[0] += 1
            with ExitStack() as px:
                sfx = f"l1_{call}"
                pb = lambda nm, shape, dt: px.enter_context(nc.sbuf_tensor(nm + sfx, shape, dt))
                pool = {
                    "f": [pb(f"qf{i}", [128, 2048], F32) for i in range(8)],
                    "i": [pb(f"qi{i}", [128, 2048], I32) for i in range(4)],
                    "bf": pb("qbf", [128, 2048], BF16),
                }
                t_rt = pb("t_rt", [128, NCK * ST], F32)
                xq = pb("xq", [128, NCK * ST], BF16)
                hq = pb("hq", [128, 2048], BF16)
                t_acc = pb("t_acc", [128, NCK * ST], F32)
                w1s = [pb(f"w1s{i}", [128, NCK * 128], BF16) for i in range(2)]
                w2s = [pb(f"w2s{i}", [128, D], BF16) for i in range(4)]
                t_mask = pb("t_mask", [128, NCK * ST], BF16)

                if call == 0:
                    emit_wquant(pool, W1T, d_w1q, D, H0, "w1")
                    b.op("ve", lambda E, s: E.engine_nop().then_inc(s, 1),
                         reads=(("wq_st", "w1"),), writes=("wq1done",))
                    emit_wquant(pool, W2T, d_w2q, H0, D, "w2")
                    b.op("ve", lambda E, s: E.engine_nop().then_inc(s, 1),
                         reads=(("wq_st", "w2"),), writes=("wq2done",))
                    emit_quant(pool, "l1b1", s_l1b1[:], ("l1b1q",), s_l1b1q[:], 2, 113.0, True, NH0)
                    emit_quant(pool, "l1b2", s_l1b2[:], ("l1b2q",), s_l1b2q[:], 2, 113.0, True, NCK)

                zst = pool["f"][6]
                xst = pool["f"][7]
                for st_i in range(NST):
                    m0 = st_i * ST
                    b.dmaop("io", lambda E, s, m0=m0: E.dma_start(
                                out=t_rt[:].rearrange("p (c m) -> p c m", c=NCK),
                                in_=d_rsp.rearrange("(c p) m -> p c m", p=128)
                                    [:, :, m0:m0 + ST]).then_inc(s, 16),
                            reads=("rspv",), writes=(("rt",),))
                    b.dmaop("io", lambda E, s, m0=m0: E.dma_start(
                                out=t_mask[:].rearrange("p (c m) -> p c m", c=NCK),
                                in_=maskT.rearrange("(c p) m -> p c m", p=128)
                                    [:, :, m0:m0 + ST]).then_inc(s, 16),
                            reads=(), writes=(("mask",),))
                    bc_r, bc_n = emit_ln("l1ln", ST, lambda c: ("rt",),
                                         lambda c: t_rt[:, c * ST:(c + 1) * ST], False)
                    for half in range(2):
                        for cc in range(4):
                            c = half * 4 + cc
                            b.op("ve", lambda E, s, c=c, cc=cc: E.tensor_tensor(
                                    xst[:, cc * ST:(cc + 1) * ST], t_rt[:, c * ST:(c + 1) * ST],
                                    ps_bc[0][:, 0:ST], ALU.mult).then_inc(s, 1),
                                 reads=(("rt",), bc_r), writes=("xst",))
                            b.op("ve", lambda E, s, c=c, cc=cc: E.tensor_tensor(
                                    xst[:, cc * ST:(cc + 1) * ST], xst[:, cc * ST:(cc + 1) * ST],
                                    ps_bc[1][:, 0:ST], ALU.add).then_inc(s, 1),
                                 reads=("xst", bc_n), writes=("xst",))
                        emit_quant(pool, "xst", xst[:, 0:2048], "xq",
                                   xq[:, half * 2048:(half + 1) * 2048], 2, 113.0, True, 2048)
                    xqk = ("xq",)
                    for g in range(8):
                        for hx in range(4):
                            hc = g * 4 + hx
                            w = w1s[hx % 2]
                            b.dmaop("wt", lambda E, s, hc=hc, w=w: E.dma_start(
                                        out=w[:].rearrange("p (c q) -> p c q", c=NCK),
                                        in_=d_w1q[:, hc * 128:(hc + 1) * 128]
                                            .rearrange("(c p) q -> p c q", p=128)).then_inc(s, 16),
                                    reads=("wq1done",), writes=(("w1s", hx % 2),))
                            b.op("te", lambda E, s, hx=hx, w=w: _group_mm(
                                    E, s, ps_h[hx % 2][:, 0:ST],
                                    [(w[:, c * 128:(c + 1) * 128], xq[:, c * ST:(c + 1) * ST])
                                     for c in range(NCK)]),
                                 reads=xqk + (("w1s", hx % 2),), writes=(("ps_h", hx % 2),))
                            b.op("se", lambda E, s, hc=hc, hx=hx: E.activation(
                                    out=zst[:, hx * ST:(hx + 1) * ST], in_=ps_h[hx % 2][:, 0:ST],
                                    func=ACT.Relu, bias=s_l1b1q[:, hc:hc + 1], scale=1.0).then_inc(s, 1),
                                 reads=(("ps_h", hx % 2), ("l1b1q",)), writes=("zst",))
                            b.dmaop("wt", lambda E, s, hc=hc, hx=hx: E.dma_start(
                                        out=w2s[hx][:],
                                        in_=d_w2q[hc * 128:(hc + 1) * 128, :]).then_inc(s, 16),
                                    reads=("wq2done",), writes=(("w2s", hx),))
                        emit_quant(pool, "zst", zst[:, 0:2048], "xst",
                                   xst[:, 0:2048], 7, 1.0, False, 2048)
                        emit_quant(pool, "xst", xst[:, 0:2048], "hq",
                                   hq[:, 0:2048], 2, 113.0, False, 2048)
                        for dc in range(NCK):
                            b.op("te", lambda E, s, dc=dc: _group_mm(
                                    E, s, ps_m[dc % 2][:, 0:ST],
                                    [(w2s[hx][:, dc * 128:(dc + 1) * 128],
                                      hq[:, hx * ST:(hx + 1) * ST]) for hx in range(4)]),
                                 reads=("hq",) + tuple(("w2s", hx) for hx in range(4)),
                                 writes=(("ps_m", dc % 2),))
                            if g == 0:
                                b.op("se", lambda E, s, dc=dc: E.activation(
                                        out=t_acc[:, dc * ST:(dc + 1) * ST],
                                        in_=ps_m[dc % 2][:, 0:ST], func=ACT.Identity).then_inc(s, 1),
                                     reads=(("ps_m", dc % 2),), writes=(("acc", dc),))
                            else:
                                b.op("ve", lambda E, s, dc=dc: E.tensor_tensor(
                                        t_acc[:, dc * ST:(dc + 1) * ST],
                                        t_acc[:, dc * ST:(dc + 1) * ST],
                                        ps_m[dc % 2][:, 0:ST], ALU.add).then_inc(s, 1),
                                     reads=(("ps_m", dc % 2), ("acc", dc)), writes=(("acc", dc),))
                    for half in range(2):
                        for cc in range(4):
                            dc = half * 4 + cc
                            b.op("se", lambda E, s, dc=dc, cc=cc: E.activation(
                                    out=zst[:, cc * ST:(cc + 1) * ST],
                                    in_=t_acc[:, dc * ST:(dc + 1) * ST], func=ACT.Identity,
                                    bias=s_l1b2q[:, dc:dc + 1], scale=1.0).then_inc(s, 1),
                                 reads=(("acc", dc), ("l1b2q",)), writes=("zst",))
                        emit_quant(pool, "zst", zst[:, 0:2048], "xst",
                                   xst[:, 0:2048], 7, 1.0, True, 2048)
                        b.op("ve", lambda E, s, half=half: E.tensor_tensor(
                                xst[:, 0:2048], xst[:, 0:2048],
                                t_mask[:, half * 2048:(half + 1) * 2048], ALU.mult).then_inc(s, 1),
                             reads=("xst", ("mask",)), writes=("xst",))
                        b.op("ve", lambda E, s, half=half: E.scalar_tensor_tensor(
                                t_rt[:, half * 2048:(half + 1) * 2048], xst[:, 0:2048], 0.5,
                                t_rt[:, half * 2048:(half + 1) * 2048],
                                ALU.mult, ALU.add).then_inc(s, 1),
                             reads=("xst", ("rt",)), writes=(("rt",),))
                    b.dmaop("io", lambda E, s, m0=m0: E.dma_start(
                                out=d_rsp.rearrange("(c p) m -> p c m", p=128)
                                    [:, :, m0:m0 + ST],
                                in_=t_rt[:].rearrange("p (c m) -> p c m", c=NCK)).then_inc(s, 16),
                            reads=(("rt",),), writes=(("rspv",),))

        GOFF = [0] * 17
        for q in range(16):
            GOFF[q + 1] = GOFF[q] + (16 - q) * 128

        def scan_pre(R):
            with ExitStack() as px:
                stg = px.enter_context(nc.sbuf_tensor("sc_stg", [128, NCK * ST], BF16))
                for st_i in range(NST):
                    m0 = st_i * ST
                    bc_r, bc_n = emit_ln(
                        "lnS", ST, lambda c: ("R", c),
                        lambda c, m0=m0, R=R: R[:, c * TOK + m0: c * TOK + m0 + ST], False)
                    for c in range(NCK):
                        b.op("ve", lambda E, s, c=c, m0=m0, R=R: E.tensor_tensor(
                                sq_slots[0][:, 0:ST], R[:, c * TOK + m0: c * TOK + m0 + ST],
                                ps_bc[0][:, 0:ST], ALU.mult).then_inc(s, 1),
                             reads=(("R", c), bc_r), writes=(("sqslot", 0),))
                        b.op("ve", lambda E, s, c=c: E.tensor_tensor(
                                stg[:, c * ST:(c + 1) * ST], sq_slots[0][:, 0:ST],
                                ps_bc[1][:, 0:ST], ALU.add).then_inc(s, 1),
                             reads=(("sqslot", 0), bc_n), writes=(("stg", c),))
                    p_ = m0 // SEQ
                    s0 = m0 % SEQ
                    for half in range(2):
                        jd = s0 // 256 + half
                        b.dmaop("io", lambda E, s, p_=p_, jd=jd, half=half: E.dma_start(
                                    out=d_x1in[jd * D:(jd + 1) * D, p_ * 256:(p_ + 1) * 256]
                                        .rearrange("(c p) t -> p c t", p=128),
                                    in_=stg[:].rearrange("p (c t) -> p c t", c=NCK)
                                        [:, :, half * 256:(half + 1) * 256]).then_inc(s, 16),
                                reads=tuple(("stg", c) for c in range(NCK)),
                                writes=(("x1in", st_i, half),))
                for c in range(NCK):
                    b.dmaop("io", lambda E, s, c=c, R=R: E.dma_start(
                                out=d_rsp[c * 128:(c + 1) * 128, :],
                                in_=R[:, c * TOK:(c + 1) * TOK]).then_inc(s, 16),
                            reads=(("R", c),), writes=(("rsp", c),))

        def _a2a(E, s, ins_ap, outs_ap):
            from concourse.aot_env import allow_ici_all_to_all
            with allow_ici_all_to_all():
                E.collective_compute(
                    "AllToAll", ALU.bypass, replica_groups=[list(range(NCORE))],
                    ins=[ins_ap], outs=[outs_ap]).then_inc(s, 1)

        def scan_phase_a():
            b.ccop(lambda E, s: _a2a(E, s, d_x1in.opt(), d_x1out.opt()),
                   reads=tuple(("x1in", i, h) for i in range(NST) for h in range(2)),
                   writes=("x1out",))
            pubkeys = []
            with ExitStack() as px:
                Xt = px.enter_context(nc.sbuf_tensor("sc_Xt", [128, NCK * MC], BF16))
                UV = px.enter_context(nc.sbuf_tensor("sc_UV", [128, 16 * 2048], BF16))
                Gb = px.enter_context(nc.sbuf_tensor("sc_G", [128, GOFF[16]], BF16))
                wkvs = px.enter_context(nc.sbuf_tensor("sc_wkv", [128, NCK * D], BF16))
                bkvs = px.enter_context(nc.sbuf_tensor("sc_bkv", [1, D], BF16))
                utst = px.enter_context(nc.sbuf_tensor("sc_utst", [128, NCK * 128], BF16))
                tls = {nm: px.enter_context(nc.sbuf_tensor("sc_" + nm, [128, 128], BF16))
                       for nm in ("tT", "tL", "t2", "t2t", "t4", "a1", "b2", "b4", "m1", "inv")}
                b.dmaop("gw", lambda E, s: E.dma_start(
                            out=wkvs[:].rearrange("p (c q) -> p c q", c=NCK),
                            in_=wkv_vT.rearrange("(c p) q -> p c q", p=128)).then_inc(s, 16),
                        writes=("wkvs",))
                b.dmaop("gw", lambda E, s: E.dma_start(out=bkvs[:], in_=bkv_row).then_inc(s, 16),
                        writes=("bkvs",))

                for cidx in range(2):
                    b.global_barrier()
                    base1 = cidx * PUB1
                    for o in range(NCORE):
                        for p_ in range(2):
                            blk = o * 2 + p_
                            b.dmaop("io", lambda E, s, o=o, p_=p_, blk=blk, cidx=cidx: E.dma_start(
                                        out=UV[:].rearrange("p (c r) -> p c r", c=16)
                                            [:, 0:NCK, blk * 128:(blk + 1) * 128],
                                        in_=d_x1out[o * D:(o + 1) * D,
                                                    p_ * 256 + cidx * 128: p_ * 256 + (cidx + 1) * 128]
                                            .rearrange("(c p) t -> p c t", p=128)).then_inc(s, 16),
                                    reads=("x1out",), writes=(("Xc", blk),))
                    for o in range(NCORE):
                        for p_ in range(2):
                            blk = o * 2 + p_
                            bglob = o + 8 * p_
                            b.op("ve", lambda E, s, blk=blk, bglob=bglob: E.tensor_copy(
                                    Xt[:].rearrange("p (c t bb) -> p c t bb", c=NCK, bb=16)
                                        [:, :, :, bglob],
                                    UV[:].rearrange("p (c r) -> p c r", c=16)
                                        [:, 0:NCK, blk * 128:(blk + 1) * 128]).then_inc(s, 1),
                                 reads=(("Xc", blk),), writes=(("Xt", o, p_),))
                    xtk = tuple(("Xt", o, p_) for o in range(NCORE) for p_ in range(2))
                    for j in range(16):
                        for cc in range(NCK):
                            b.dmaop("tp", lambda E, s, j=j, cc=cc: E.dma_start(
                                        out=UV[:, j * 2048 + cc * 128: j * 2048 + (cc + 1) * 128],
                                        in_=Xt[:, cc * MC + j * 128: cc * MC + (j + 1) * 128],
                                        transpose=True).then_inc(s, 16),
                                    reads=xtk, writes=(("UVx", j),))
                        b.dmaop("gw", lambda E, s, j=j, base1=base1: E.dma_start(
                                    out=d_pub1[base1 + D + j * 128: base1 + D + (j + 1) * 128, 0:1024],
                                    in_=UV[:, j * 2048: j * 2048 + 1024]).then_inc(s, 16),
                                reads=(("UVx", j),), writes=(("pubx", cidx, j),))
                        pubkeys.append(("pubx", cidx, j))
                    for j in range(16):
                        for hf in range(2):
                            def vmm(E, s, j=j, hf=hf):
                                for c in range(NCK):
                                    E.matmul(ps_m[hf][:, 0:512],
                                             Xt[:, c * MC + j * 128: c * MC + (j + 1) * 128],
                                             wkvs[:, c * D + hf * 512: c * D + (hf + 1) * 512],
                                             start=(c == 0), stop=False)
                                E.matmul(ps_m[hf][:, 0:512], c_ones_rowb[:],
                                         bkvs[0:1, hf * 512:(hf + 1) * 512],
                                         start=False, stop=True).then_inc(s, 1)
                            b.op("te", vmm,
                                 reads=xtk + ("wkvs", "bkvs", "ones_rowb"),
                                 writes=(("ps_m", hf),))
                            b.op("se", lambda E, s, j=j, hf=hf: E.activation(
                                    out=UV[:, j * 2048 + 1024 + hf * 512: j * 2048 + 1024 + (hf + 1) * 512],
                                    in_=ps_m[hf][:, 0:512], func=ACT.Identity).then_inc(s, 1),
                                 reads=(("ps_m", hf),), writes=(("UVv", j, hf),))
                    uvvk = tuple(("UVv", j, hf) for j in range(16) for hf in range(2))
                    for dst_i in range(NCORE):
                        rbase = ((dst_i * 2 + cidx) * 2 + 0) * MC
                        b.dmaop("gw", lambda E, s, dst_i=dst_i, rbase=rbase: E.dma_start(
                                    out=d_p2in[rbase: rbase + MC, :]
                                        .rearrange("(j p) q -> p j q", p=128),
                                    in_=UV[:].rearrange("p (j q) -> p j q", j=16)
                                        [:, :, 1024 + dst_i * 128: 1024 + (dst_i + 1) * 128]
                                    ).then_inc(s, 16),
                                reads=uvvk, writes=(("pubv", cidx, dst_i),))
                        pubkeys.append(("pubv", cidx, dst_i))
                    for q in range(16):
                        g0 = q
                        while g0 < 16:
                            wblk = min(4, 16 - g0)
                            def gmm(E, s, q=q, g0=g0, wblk=wblk):
                                for c in range(NCK):
                                    E.matmul(ps_h[0][:, 0:wblk * 128],
                                             Xt[:, c * MC + q * 128: c * MC + (q + 1) * 128],
                                             Xt[:, c * MC + g0 * 128: c * MC + (g0 + wblk) * 128],
                                             start=(c == 0), stop=(c == NCK - 1)).then_inc(s, 1) \
                                        if c == NCK - 1 else E.matmul(
                                             ps_h[0][:, 0:wblk * 128],
                                             Xt[:, c * MC + q * 128: c * MC + (q + 1) * 128],
                                             Xt[:, c * MC + g0 * 128: c * MC + (g0 + wblk) * 128],
                                             start=(c == 0), stop=False)
                            b.op("te", gmm, reads=xtk, writes=(("ps_h", 0),))
                            b.op("se", lambda E, s, q=q, g0=g0, wblk=wblk: E.activation(
                                    out=Gb[:, GOFF[q] + (g0 - q) * 128: GOFF[q] + (g0 - q + wblk) * 128],
                                    in_=ps_h[0][:, 0:wblk * 128], func=ACT.Identity).then_inc(s, 1),
                                 reads=(("ps_h", 0),), writes=(("G", q, g0),))
                            g0 += wblk
                    gk = tuple(("G", q, g0) for q in range(16) for g0 in range(q, 16, 4))
                    for p_ in range(16):
                        if p_ > 0:
                            for g in range(4):
                                def rmm2(E, s, p_=p_, g=g):
                                    ins = None
                                    for qi in range(p_):
                                        ins = E.matmul(ps_m[g % 2][:, 0:512],
                                                 Gb[:, GOFF[qi] + (p_ - qi) * 128: GOFF[qi] + (p_ - qi + 1) * 128],
                                                 UV[:, qi * 2048 + g * 512: qi * 2048 + (g + 1) * 512],
                                                 start=(qi == 0), stop=(qi == p_ - 1))
                                    ins.then_inc(s, 1)
                                b.op("te", rmm2,
                                     reads=gk + tuple(("UVs", qi) for qi in range(p_)) + uvvk +
                                           tuple(("UVx", j) for j in range(16)),
                                     writes=(("ps_m", g % 2),))
                                b.op("ve", lambda E, s, p_=p_, g=g: E.scalar_tensor_tensor(
                                        UV[:, p_ * 2048 + g * 512: p_ * 2048 + (g + 1) * 512],
                                        ps_m[g % 2][:, 0:512], c_negc[:, 0:1],
                                        UV[:, p_ * 2048 + g * 512: p_ * 2048 + (g + 1) * 512],
                                        ALU.mult, ALU.add).then_inc(s, 1),
                                     reads=(("ps_m", g % 2), "negc", ("UVx", p_), ("UVv", p_, g % 2)),
                                     writes=(("UVr", p_, g), ("UVx", p_), ("UVv", p_, g % 2)))
                        b.op("ve", lambda E, s, p_=p_: E.scalar_tensor_tensor(
                                tls["tT"][:], Gb[:, GOFF[p_]:GOFF[p_] + 128], c_negc[:, 0:1],
                                c_msu[:], ALU.mult, ALU.mult).then_inc(s, 1),
                             reads=gk + ("negc", "msu"), writes=(("tT",),))
                        b.op("ve", lambda E, s, p_=p_: E.scalar_tensor_tensor(
                                tls["tL"][:], Gb[:, GOFF[p_]:GOFF[p_] + 128], c_negc[:, 0:1],
                                c_msl[:], ALU.mult, ALU.mult).then_inc(s, 1),
                             reads=gk + ("negc", "msl"), writes=(("tL",),))
                        def mm1(dst, lh, rh, rk, wk):
                            b.op("te", lambda E, s, lh=lh, rh=rh: E.matmul(
                                    ps_h[1][:, 0:128], lh[:], rh[:],
                                    start=True, stop=True).then_inc(s, 1),
                                 reads=rk, writes=(("ps_h", 1),))
                            b.op("ve", lambda E, s, dst=dst: E.tensor_copy(
                                    dst[:], ps_h[1][:, 0:128]).then_inc(s, 1),
                                 reads=(("ps_h", 1),), writes=(wk,))
                        mm1(tls["t2"], tls["tT"], tls["tL"], (("tT",), ("tL",)), ("t2",))
                        mm1(tls["t2t"], tls["tL"], tls["tT"], (("tT",), ("tL",)), ("t2t",))
                        mm1(tls["t4"], tls["t2t"], tls["t2"], (("t2",), ("t2t",)), ("t4",))
                        b.op("ve", lambda E, s: E.tensor_tensor(
                                tls["a1"][:], tls["tT"][:], c_ident[:], ALU.add).then_inc(s, 1),
                             reads=(("tT",), "ident"), writes=(("a1",),))
                        b.op("ve", lambda E, s: E.tensor_tensor(
                                tls["b2"][:], tls["t2"][:], c_ident[:], ALU.add).then_inc(s, 1),
                             reads=(("t2",), "ident"), writes=(("b2",),))
                        b.op("ve", lambda E, s: E.tensor_tensor(
                                tls["b4"][:], tls["t4"][:], c_ident[:], ALU.add).then_inc(s, 1),
                             reads=(("t4",), "ident"), writes=(("b4",),))
                        mm1(tls["m1"], tls["b2"], tls["a1"], (("b2",), ("a1",)), ("m1",))
                        mm1(tls["inv"], tls["b4"], tls["m1"], (("b4",), ("m1",)), ("inv",))
                        for g in range(4):
                            b.op("te", lambda E, s, p_=p_, g=g: E.matmul(
                                    ps_m[g % 2][:, 0:512], tls["inv"][:],
                                    UV[:, p_ * 2048 + g * 512: p_ * 2048 + (g + 1) * 512],
                                    start=True, stop=True).then_inc(s, 1),
                                 reads=(("inv",),) + ((("UVr", p_, g),) if p_ > 0 else
                                                      (("UVx", p_), ("UVv", p_, g % 2))),
                                 writes=(("ps_m", g % 2),))
                            b.op("se", lambda E, s, p_=p_, g=g: E.activation(
                                    out=UV[:, p_ * 2048 + g * 512: p_ * 2048 + (g + 1) * 512],
                                    in_=ps_m[g % 2][:, 0:512], func=ACT.Identity).then_inc(s, 1),
                                 reads=(("ps_m", g % 2),), writes=(("UVs", p_),))
                    uvsk = tuple(("UVs", j) for j in range(16))
                    for j in range(16):
                        b.dmaop("tp", lambda E, s, j=j: E.dma_start(
                                    out=utst[:].rearrange("p (c q) -> p c q", c=NCK),
                                    in_=UV[:, j * 2048: j * 2048 + 1024]
                                        .rearrange("p (c q) -> p c q", c=NCK),
                                    transpose=True).then_inc(s, 16),
                                reads=uvsk, writes=(("utst",),))
                        b.dmaop("gw", lambda E, s, j=j, base1=base1: E.dma_start(
                                    out=d_pub1[base1: base1 + 1024, j * 128:(j + 1) * 128]
                                        .rearrange("(c p) q -> p c q", p=128),
                                    in_=utst[:].rearrange("p (c q) -> p c q", c=NCK)
                                    ).then_inc(s, 16),
                                reads=(("utst",),), writes=(("pubu", cidx, j),))
                        pubkeys.append(("pubu", cidx, j))
                    for dst_i in range(NCORE):
                        rbase = ((dst_i * 2 + cidx) * 2 + 1) * MC
                        b.dmaop("gw", lambda E, s, dst_i=dst_i, rbase=rbase: E.dma_start(
                                    out=d_p2in[rbase: rbase + MC, :]
                                        .rearrange("(j p) q -> p j q", p=128),
                                    in_=UV[:].rearrange("p (j q) -> p j q", j=16)
                                        [:, :, 1024 + dst_i * 128: 1024 + (dst_i + 1) * 128]
                                    ).then_inc(s, 16),
                                reads=uvsk, writes=(("pubf", cidx, dst_i),))
                        pubkeys.append(("pubf", cidx, dst_i))
            return pubkeys

        def scan_phase_b(pubkeys):
            b.ccop(lambda E, s: E.collective_compute(
                        "AllGather", ALU.bypass, replica_groups=[list(range(NCORE))],
                        ins=[d_pub1.opt()], outs=[d_agp1.opt()],
                    ).then_inc(s, 1),
                   reads=tuple(pubkeys), writes=("agp1",))
            b.ccop(lambda E, s: _a2a(E, s, d_p2in.opt(), d_p2out.opt()),
                   reads=tuple(pubkeys), writes=("p2out",))
            b.global_barrier()
            with ExitStack() as px:
                Ut = [px.enter_context(nc.sbuf_tensor(f"sb_Ut{i}", [128, NCK * MC], BF16)) for i in range(2)]
                Xr = [px.enter_context(nc.sbuf_tensor(f"sb_Xr{i}", [128, 16 * 1024], BF16)) for i in range(2)]
                Vb = [px.enter_context(nc.sbuf_tensor(f"sb_Vb{i}", [128, MC], BF16)) for i in range(2)]
                Fb = [px.enter_context(nc.sbuf_tensor(f"sb_Fb{i}", [128, MC], BF16)) for i in range(2)]
                UZt = px.enter_context(nc.sbuf_tensor("sb_UZt", [128, MC], BF16))
                UZr = px.enter_context(nc.sbuf_tensor("sb_UZr", [128, MC], BF16))
                Erow = px.enter_context(nc.sbuf_tensor("sb_Erow", [128, MC], BF16))
                Yrow = px.enter_context(nc.sbuf_tensor("sb_Yrow", [128, MC], BF16))
                Yt = px.enter_context(nc.sbuf_tensor("sb_Yt", [128, MC], BF16))
                Yb = px.enter_context(nc.sbuf_tensor("sb_Yb", [128, MC], BF16))
                Z = px.enter_context(nc.sbuf_tensor("sb_Z", [128, NCK * 128], F32))
                Zb = px.enter_context(nc.sbuf_tensor("sb_Zb", [128, NCK * 128], BF16))
                b.dmaop("io", lambda E, s: E.dma_start(
                            out=Z[:].rearrange("p (c q) -> p c q", c=NCK),
                            in_=z0.rearrange("(c p) q -> p c q", p=128)).then_inc(s, 16),
                        writes=(("Z",),))
                b.op("se", lambda E, s: E.copy(out=Zb[:], in_=Z[:]).then_inc(s, 1),
                     reads=(("Z",),), writes=(("Zb",),))
                for j in range(NCHUNK):
                    sl = j % 2
                    b.dmaop("io", lambda E, s, j=j, sl=sl: E.dma_start(
                                out=Ut[sl][:].rearrange("p (c m) -> p c m", c=NCK),
                                in_=d_agp1[j * PUB1: j * PUB1 + 1024, :]
                                    .rearrange("(c p) m -> p c m", p=128)).then_inc(s, 16),
                            reads=("agp1",), writes=(("Ut", sl),))
                    b.dmaop("io", lambda E, s, j=j, sl=sl: E.dma_start(
                                out=Xr[sl][:].rearrange("p (r q) -> p r q", r=16),
                                in_=d_agp1[j * PUB1 + 1024: j * PUB1 + 1024 + MC, 0:1024]
                                    .rearrange("(r p) q -> p r q", p=128)).then_inc(s, 16),
                            reads=("agp1",), writes=(("Xr", sl),))
                    b.dmaop("io", lambda E, s, j=j, sl=sl: E.dma_start(
                                out=Vb[sl][:].rearrange("p (r q) -> p r q", r=16),
                                in_=d_p2out[(j * 2 + 0) * MC: (j * 2 + 1) * MC, :]
                                    .rearrange("(r p) q -> p r q", p=128)).then_inc(s, 16),
                            reads=("p2out",), writes=(("Vb", sl),))
                    b.dmaop("io", lambda E, s, j=j, sl=sl: E.dma_start(
                                out=Fb[sl][:].rearrange("p (r q) -> p r q", r=16),
                                in_=d_p2out[(j * 2 + 1) * MC: (j * 2 + 2) * MC, :]
                                    .rearrange("(r p) q -> p r q", p=128)).then_inc(s, 16),
                            reads=("p2out",), writes=(("Fb", sl),))
                    for g in range(4):
                        def emm(E, s, g=g, sl=sl):
                            ins = None
                            for c in range(NCK):
                                ins = E.matmul(ps_m[g % 2][:, 0:512],
                                               Zb[:, c * 128:(c + 1) * 128],
                                               Ut[sl][:, c * MC + g * 512: c * MC + (g + 1) * 512],
                                               start=(c == 0), stop=(c == NCK - 1))
                            ins.then_inc(s, 1)
                        b.op("te", emm, reads=(("Zb",), ("Ut", sl)), writes=(("ps_m", g % 2),))
                        b.op("se", lambda E, s, g=g: E.activation(
                                out=UZt[:, g * 512:(g + 1) * 512], in_=ps_m[g % 2][:, 0:512],
                                func=ACT.Identity).then_inc(s, 1),
                             reads=(("ps_m", g % 2),), writes=(("UZt", g),))
                    for g in range(4):
                        b.dmaop("tp", lambda E, s, g=g: E.dma_start(
                                    out=UZr[:, g * 512:(g + 1) * 512]
                                        .rearrange("p (t q) -> p t q", t=4),
                                    in_=UZt[:, g * 512:(g + 1) * 512]
                                        .rearrange("p (t q) -> p t q", t=4),
                                    transpose=True).then_inc(s, 16),
                                reads=(("UZt", g),), writes=(("UZr", g),))
                    uzk = tuple(("UZr", g) for g in range(4))
                    b.op("ve", lambda E, s, sl=sl: E.tensor_tensor(
                            Erow[:], UZr[:], Fb[sl][:], ALU.subtract).then_inc(s, 1),
                         reads=uzk + (("Fb", sl),), writes=(("Erow",),))
                    b.op("ve", lambda E, s, sl=sl: E.tensor_tensor(
                            Yrow[:], Erow[:], Vb[sl][:], ALU.add).then_inc(s, 1),
                         reads=(("Erow",), ("Vb", sl)), writes=(("Yrow",),))
                    for g in range(4):
                        b.dmaop("tp", lambda E, s, g=g: E.dma_start(
                                    out=Yt[:, g * 512:(g + 1) * 512]
                                        .rearrange("p (t q) -> p t q", t=4),
                                    in_=Yrow[:, g * 512:(g + 1) * 512]
                                        .rearrange("p (t q) -> p t q", t=4),
                                    transpose=True).then_inc(s, 16),
                                reads=(("Yrow",),), writes=(("Yt", g),))
                    b.op("ve", lambda E, s: E.tensor_copy(
                            Yb[:].rearrange("p (bb t) -> p bb t", bb=B),
                            Yt[:].rearrange("p (t bb) -> p bb t", bb=B)).then_inc(s, 1),
                         reads=tuple(("Yt", g) for g in range(4)), writes=(("Yb",),))
                    b.dmaop("io", lambda E, s, j=j: E.dma_start(
                                out=d_y.rearrange("p (bb t2) -> p bb t2", bb=B)
                                    [:, :, j * 128:(j + 1) * 128],
                                in_=Yb[:].rearrange("p (bb t) -> p bb t", bb=B)).then_inc(s, 16),
                            reads=(("Yb",),), writes=(("d_y", j),))
                    for bank in range(2):
                        for q4 in range(4):
                            dc = bank * 4 + q4
                            def zmm(E, s, dc=dc, bank=bank, q4=q4, sl=sl):
                                ins = None
                                for r in range(16):
                                    ins = E.matmul(ps_h[bank][:, q4 * 128:(q4 + 1) * 128],
                                                   Xr[sl][:, r * 1024 + dc * 128: r * 1024 + (dc + 1) * 128],
                                                   Erow[:, r * 128:(r + 1) * 128],
                                                   start=(r == 0), stop=(r == 15))
                                ins.then_inc(s, 1)
                            b.op("te", zmm, reads=(("Xr", sl), ("Erow",)),
                                 writes=(("ps_hz", bank, q4),))
                            b.op("ve", lambda E, s, dc=dc, bank=bank, q4=q4: E.scalar_tensor_tensor(
                                    Z[:, dc * 128:(dc + 1) * 128],
                                    ps_h[bank][:, q4 * 128:(q4 + 1) * 128], c_negc[:, 0:1],
                                    Z[:, dc * 128:(dc + 1) * 128],
                                    ALU.mult, ALU.add).then_inc(s, 1),
                                 reads=(("ps_hz", bank, q4), "negc", ("Z",)), writes=(("Z",),))
                    b.op("ve", lambda E, s: E.tensor_copy(Zb[:], Z[:]).then_inc(s, 1),
                         reads=(("Z",),), writes=(("Zb",),))
                for dst_i in range(NCORE):
                    for p_ in range(2):
                        bglob = dst_i + 8 * p_
                        b.dmaop("io", lambda E, s, dst_i=dst_i, p_=p_, bglob=bglob: E.dma_start(
                                    out=d_y3in[dst_i * 128:(dst_i + 1) * 128,
                                               p_ * SEQ:(p_ + 1) * SEQ],
                                    in_=d_y[:, bglob * SEQ:(bglob + 1) * SEQ]
                                    ).then_inc(s, 16),
                                reads=tuple(("d_y", j) for j in range(NCHUNK)),
                                writes=(("y3in", dst_i, p_),))
            b.ccop(lambda E, s: _a2a(E, s, d_y3in.opt(), d_y3out.opt()),
                   reads=tuple(("y3in", i, p_) for i in range(NCORE) for p_ in range(2)),
                   writes=("y3out",))

        def scan_post(R):
            with ExitStack() as px:
                yst = [px.enter_context(nc.sbuf_tensor(f"sc_yst{i}", [128, ST], BF16)) for i in range(2)]
                for c in range(NCK):
                    b.dmaop("io", lambda E, s, c=c, R=R: E.dma_start(
                                out=R[:, c * TOK:(c + 1) * TOK],
                                in_=d_rsp[c * 128:(c + 1) * 128, :]).then_inc(s, 16),
                            reads=(("rsp", c), "y3out"), writes=(("R", c),))
                k = 0
                for c in range(NCK):
                    for st_i in range(NST):
                        m0 = st_i * ST
                        sl = k % 2
                        k += 1
                        b.dmaop("io", lambda E, s, c=c, m0=m0, sl=sl: E.dma_start(
                                    out=yst[sl][:],
                                    in_=d_y3out[c * 128:(c + 1) * 128, m0:m0 + ST]).then_inc(s, 16),
                                reads=("y3out",), writes=(("yst", sl),))
                        b.op("ve", lambda E, s, c=c, m0=m0, sl=sl, R=R: E.tensor_tensor(
                                R[:, c * TOK + m0: c * TOK + m0 + ST],
                                R[:, c * TOK + m0: c * TOK + m0 + ST],
                                yst[sl][:], ALU.add).then_inc(s, 1),
                             reads=(("yst", sl), ("R", c)), writes=(("R", c),))

        def attn_phase():
            SCL = 1.0 / math.sqrt(2.0 * D)
            with ExitStack() as px:
                pb = lambda nm, shape, dt: px.enter_context(nc.sbuf_tensor(nm + "_at", shape, dt))
                t_rt = pb("t_rt", [128, NCK * ST], F32)
                t_f1a = pb("t_f1a", [128, ST], F32)
                xnq = pb("xnq", [128, NCK * ST], BF16)
                qkvT = pb("qkvT", [128, 24 * ST], BF16)
                wq = [pb(f"wq{i}", [128, NCK * 128], BF16) for i in range(2)]
                Qh = pb("Qh", [128, SEQ], BF16)
                Kh = pb("Kh", [128, SEQ], BF16)
                Vh = pb("Vh", [128, SEQ], BF16)
                Vrow = pb("Vrow", [128, 16 * 64], BF16)
                Vaug = pb("Vaug", [128, 16 * 65], BF16)
                Pbuf = pb("Pbuf", [128, 16 * 512], BF16)
                oh = pb("oh", [128, SEQ], BF16)
                ohg = pb("ohg", [128, SEQ], BF16)
                gst = pb("gst", [128, SEQ], BF16)
                ost = pb("ost", [128, 512], F32)
                rsum = pb("rsum", [1, 512], F32)
                aacc = pb("aacc", [128, NCK * ST], BF16)
                c_m4 = pb("c_m4", [128, 1], F32)
                b.op("ve", lambda E, s: E.memset(c_m4[:], -4.0).then_inc(s, 1), writes=("m4",))

                for st_i in range(NST):
                    m0 = st_i * ST
                    hh0 = (m0 % SEQ) // 128
                    b.dmaop("io", lambda E, s, m0=m0: E.dma_start(
                                out=t_rt[:].rearrange("p (c m) -> p c m", c=NCK),
                                in_=d_rsp.rearrange("(c p) m -> p c m", p=128)
                                    [:, :, m0:m0 + ST]).then_inc(s, 16),
                            reads=("rspv",), writes=(("rt",),))
                    bc_r, bc_n = emit_ln("atln", ST, lambda c: ("rt",),
                                         lambda c: t_rt[:, c * ST:(c + 1) * ST], False)
                    for c in range(NCK):
                        b.op("ve", lambda E, s, c=c: E.tensor_tensor(
                                t_f1a[:, 0:ST], t_rt[:, c * ST:(c + 1) * ST],
                                ps_bc[0][:, 0:ST], ALU.mult).then_inc(s, 1),
                             reads=(("rt",), bc_r), writes=("t_f1a",))
                        b.op("ve", lambda E, s, c=c: E.tensor_tensor(
                                xnq[:, c * ST:(c + 1) * ST], t_f1a[:, 0:ST],
                                ps_bc[1][:, 0:ST], ALU.add).then_inc(s, 1),
                             reads=("t_f1a", bc_n), writes=("xnq",))
                    for ec in range(24):
                        w = wq[ec % 2]
                        b.dmaop("gw", lambda E, s, ec=ec, w=w: E.dma_start(
                                    out=w[:].rearrange("p (c q) -> p c q", c=NCK),
                                    in_=qkv_k[:, ec * 128:(ec + 1) * 128]
                                        .rearrange("(c p) q -> p c q", p=128)).then_inc(s, 16),
                                reads=(), writes=(("wq", ec % 2),))
                        b.op("te", lambda E, s, ec=ec, w=w: _group_mm(
                                E, s, ps_m[ec % 2][:, 0:ST],
                                [(w[:, c * 128:(c + 1) * 128], xnq[:, c * ST:(c + 1) * ST])
                                 for c in range(NCK)]),
                             reads=("xnq", ("wq", ec % 2)), writes=(("ps_m", ec % 2),))
                        b.op("se", lambda E, s, ec=ec: E.activation(
                                out=qkvT[:, ec * ST:(ec + 1) * ST], in_=ps_m[ec % 2][:, 0:ST],
                                func=ACT.Identity, bias=s_qkvb[:, ec:ec + 1], scale=1.0).then_inc(s, 1),
                             reads=(("ps_m", ec % 2), "qkvb"), writes=("qkvT",))
                    for hi in range(4):
                        for mat, off, dst in (("q", 0, Qh), ("k", 64, Kh), ("v", 128, Vh)):
                            for par in range(2):
                                f0 = 192 * par + off
                                ec0, pp0 = f0 // 128, f0 % 128
                                b.dmaop("tp", lambda E, s, hi=hi, par=par, ec0=ec0, pp0=pp0:
                                        E.dma_start(
                                            out=gst[0:64, par * 1024:(par + 1) * 1024]
                                                .rearrange("p (gg t) -> p gg t", gg=8),
                                            in_=qkvT[pp0:pp0 + 64, :]
                                                .rearrange("p (e m) -> p e m", e=24)
                                                [:, ec0::3, hi * 128:(hi + 1) * 128]).then_inc(s, 16),
                                        reads=("qkvT",), writes=(("gst", par),))
                            b.op("ve", lambda E, s, dst=dst: E.tensor_copy(
                                    dst[0:64, :].rearrange("p (t gg par) -> p gg par t", gg=8, par=2),
                                    gst[0:64, :].rearrange("p (par gg t) -> p gg par t", par=2, gg=8)
                                    ).then_inc(s, 1),
                                 reads=(("gst", 0), ("gst", 1)), writes=((mat,),))
                        b.dmaop("tp", lambda E, s: E.dma_start(
                                    out=Vrow[:].rearrange("p (t q) -> p t q", t=16),
                                    in_=Vh[0:64, :].rearrange("p (t q) -> p t q", t=16),
                                    transpose=True).then_inc(s, 16),
                                reads=(("v",),), writes=("Vrow",))
                        b.op("ve", lambda E, s: E.tensor_copy(
                                Vaug[:].rearrange("p (t q) -> p t q", q=65)[:, :, 0:64],
                                Vrow[:].rearrange("p (t q) -> p t q", q=64)).then_inc(s, 1),
                             reads=("Vrow",), writes=("Vaug",))
                        b.op("ve", lambda E, s: E.memset(
                                Vaug[:].rearrange("p (t q) -> p t q", q=65)[:, :, 64:65],
                                1.0).then_inc(s, 1),
                             reads=(), writes=("Vaug",))
                        for qt in range(4):
                            nkc = 4 * qt + 4
                            for kc in range(nkc):
                                b.op("te", lambda E, s, kc=kc, qt=qt: E.matmul(
                                        ps_m[kc % 2][:, 0:512],
                                        Kh[0:64, kc * 128:(kc + 1) * 128],
                                        Qh[0:64, qt * 512:(qt + 1) * 512],
                                        start=True, stop=True).then_inc(s, 1),
                                     reads=(("q",), ("k",)), writes=(("ps_m", kc % 2),))
                                b.op("se", lambda E, s, kc=kc, qt=qt: E.activation(
                                        out=Pbuf[:, kc * 512:(kc + 1) * 512],
                                        in_=ps_m[kc % 2][:, 0:512], func=ACT.Exp,
                                        bias=c_m4[:, 0:1], scale=SCL).then_inc(s, 1),
                                     reads=(("ps_m", kc % 2), "m4"), writes=("Pbuf",))
                            for j in range(4):
                                kc = 4 * qt + j
                                b.op("ve", lambda E, s, kc=kc, j=j: E.tensor_tensor(
                                        Pbuf[:, kc * 512 + j * 128: kc * 512 + (j + 1) * 128],
                                        Pbuf[:, kc * 512 + j * 128: kc * 512 + (j + 1) * 128],
                                        c_tri[:], ALU.mult).then_inc(s, 1),
                                     reads=("Pbuf", "tri"), writes=("Pbuf",))
                                if j > 0:
                                    b.op("ve", lambda E, s, kc=kc, j=j: E.memset(
                                            Pbuf[:, kc * 512: kc * 512 + j * 128], 0.0).then_inc(s, 1),
                                         reads=(), writes=("Pbuf",))
                            def pvmm(E, s, qt=qt, nkc=nkc):
                                ins = None
                                for kc in range(nkc):
                                    ins = E.matmul(ps_h[qt % 2][0:65, 0:512],
                                                   Vaug[:, kc * 65:(kc + 1) * 65],
                                                   Pbuf[:, kc * 512:(kc + 1) * 512],
                                                   start=(kc == 0), stop=(kc == nkc - 1))
                                ins.then_inc(s, 1)
                            b.op("te", pvmm, reads=("Vaug", "Pbuf"), writes=(("ps_h", qt % 2),))
                            b.op("ve", lambda E, s, qt=qt: E.reciprocal(
                                    rsum[0:1, 0:512], ps_h[qt % 2][64:65, 0:512]).then_inc(s, 1),
                                 reads=(("ps_h", qt % 2),), writes=("rsum",))
                            b.op("te", lambda E, s: E.matmul(
                                    ps_bc[0][0:64, 0:512], c_ones_row[0:1, 0:64], rsum[0:1, 0:512],
                                    start=True, stop=True).then_inc(s, 1),
                                 reads=("rsum", "ones_row"), writes=("bc_r",))
                            b.op("se", lambda E, s, qt=qt: E.activation(
                                    out=ost[0:64, 0:512], in_=ps_h[qt % 2][0:64, 0:512],
                                    func=ACT.Identity).then_inc(s, 1),
                                 reads=(("ps_h", qt % 2),), writes=("ost",))
                            b.op("ve", lambda E, s, qt=qt: E.tensor_tensor(
                                    oh[0:64, qt * 512:(qt + 1) * 512], ost[0:64, 0:512],
                                    ps_bc[0][0:64, 0:512], ALU.mult).then_inc(s, 1),
                                 reads=("ost", "bc_r"), writes=("oh",))
                        b.op("ve", lambda E, s: E.tensor_copy(
                                ohg[0:64, :].rearrange("p (g t) -> p g t", g=16),
                                oh[0:64, :].rearrange("p (t g) -> p g t", g=16)).then_inc(s, 1),
                             reads=("oh",), writes=("ohg",))
                        for par in range(2):
                            b.dmaop("tp", lambda E, s, hi=hi, par=par: E.dma_start(
                                        out=aacc[par * 64:par * 64 + 64, :]
                                            .rearrange("p (e m) -> p e m", e=NCK)
                                            [:, :, hi * 128:(hi + 1) * 128],
                                        in_=ohg[0:64, :]
                                            .rearrange("p (gh par2 t) -> p gh par2 t", gh=8, par2=2)
                                            [:, :, par, :]).then_inc(s, 16),
                                    reads=("ohg",), writes=("aacc",))
                    for c in range(NCK):
                        b.op("ve", lambda E, s, c=c: E.tensor_tensor(
                                t_rt[:, c * ST:(c + 1) * ST], t_rt[:, c * ST:(c + 1) * ST],
                                aacc[:, c * ST:(c + 1) * ST], ALU.add).then_inc(s, 1),
                             reads=(("rt",), "aacc"), writes=(("rt",),))
                    b.dmaop("io", lambda E, s, m0=m0: E.dma_start(
                                out=d_rsp.rearrange("(c p) m -> p c m", p=128)
                                    [:, :, m0:m0 + ST],
                                in_=t_rt[:].rearrange("p (c m) -> p c m", c=NCK)).then_inc(s, 16),
                            reads=(("rt",),), writes=(("rspv",),))

        def spill_R(R):
            for c in range(NCK):
                b.dmaop("io", lambda E, s, c=c, R=R: E.dma_start(
                            out=d_rsp[c * 128:(c + 1) * 128, :],
                            in_=R[:, c * TOK:(c + 1) * TOK]).then_inc(s, 16),
                        reads=(("R", c),), writes=(("rsp", c),))
            b.op("ve", lambda E, s: E.engine_nop().then_inc(s, 1),
                 reads=tuple(("rsp", c) for c in range(NCK)), writes=("rspv",))

        def load_x(R):
            for c in range(NCK):
                b.dmaop("io", lambda E, s, c=c, R=R: E.dma_start(
                            out=R[:, c * TOK:(c + 1) * TOK],
                            in_=xT[c * 128:(c + 1) * 128, :]).then_inc(s, 16),
                        reads=(), writes=(("R", c),))

        if PH_SCAN:
            with ExitStack() as e1:
                R1 = e1.enter_context(nc.sbuf_tensor("R1", [128, NCK * TOK], F32))
                load_x(R1)
                if PH_L0FA:
                    l0ffn_phase(R1)
                scan_pre(R1)
            b.global_barrier()
            pk = scan_phase_a()
            b.global_barrier()
            scan_phase_b(pk)
            b.global_barrier()
            with ExitStack() as e2:
                R2 = e2.enter_context(nc.sbuf_tensor("R2", [128, NCK * TOK], F32))
                scan_post(R2)
                if PH_L0FB:
                    l0ffn_phase(R2)
                spill_R(R2)
            b.global_barrier()
        else:
            with ExitStack() as e1:
                R1 = e1.enter_context(nc.sbuf_tensor("R1", [128, NCK * TOK], F32))
                load_x(R1)
                if PH_L0FA:
                    l0ffn_phase(R1)
                spill_R(R1)
            b.global_barrier()

        if PH_L1FA:
            l1ffn_phase()
            b.global_barrier()
        if PH_ATTN:
            attn_phase()
            b.global_barrier()
        if PH_L1FB:
            l1ffn_phase()
            b.global_barrier()

        for c in range(NCK):
            b.dmaop("io", lambda E, s, c=c: E.dma_start(
                        out=out_T[c * 128:(c + 1) * 128, :],
                        in_=d_rsp[c * 128:(c + 1) * 128, :]).then_inc(s, 16),
                    reads=("rspv",), writes=(("out", c),))
        for c in range(NCK):
            b.raw_wait("sy", ("out", c))

        block = ctx.enter_context(nc.Block())
        b.emit_all(block, sems)
    return nc


def _group_mm(E, s, outap, pairs):
    n = len(pairs)
    for i, (l, r) in enumerate(pairs):
        ins = E.matmul(outap, l, r, start=(i == 0), stop=(i == n - 1))
    ins.then_inc(s, 1)




def _bernoulli_threefry(seed, p, n):
    rot = [[13, 15, 26, 6], [17, 29, 16, 24]]
    ks0, ks1 = np.uint32(0), np.uint32(seed)
    ks2 = np.uint32(ks0 ^ ks1 ^ np.uint32(0x1BD11BDA))
    i = np.arange(n, dtype=np.uint64)
    x0 = (i >> np.uint64(32)).astype(np.uint32)
    x1 = (i & np.uint64(0xFFFFFFFF)).astype(np.uint32)
    x0 = (x0 + ks0).astype(np.uint32)
    x1 = (x1 + ks1).astype(np.uint32)
    ks = [ks0, ks1, ks2]

    def rotl(x, r):
        r = np.uint32(r)
        return ((x << r) | (x >> np.uint32(32 - r))).astype(np.uint32)

    for d in range(5):
        for r in rot[d % 2]:
            x0 = (x0 + x1).astype(np.uint32)
            x1 = rotl(x1, r)
            x1 = (x1 ^ x0).astype(np.uint32)
        x0 = (x0 + ks[(d + 1) % 3]).astype(np.uint32)
        x1 = (x1 + ks[(d + 2) % 3] + np.uint32(d + 1)).astype(np.uint32)
    bits = x0 ^ x1
    u = ((bits >> np.uint32(9)) | np.uint32(0x3F800000)).view(np.float32) - np.float32(1.0)
    return u < np.float32(p)


def _prep_inputs(kw):
    f32 = np.float32
    x = np.asarray(kw["L0_attn_x"], f32)
    W0 = np.asarray(kw["L0_attn_W0"], f32)
    Wkv = np.asarray(kw["L0_attn_Wkv"], f32)
    bkv = np.asarray(kw["L0_attn_bkv"], f32)
    eta = np.asarray(kw["L0_attn_eta"], f32).reshape(1, 1)

    def pcol(v):
        return np.ascontiguousarray(np.asarray(v, f32).reshape(-1, 128).T)

    C = np.ascontiguousarray
    keep = _bernoulli_threefry(42, 0.8, B * SEQ * D).reshape(B, SEQ, D)
    maskv = (keep.astype(np.float32) * 1.25).astype(ml_dtypes.bfloat16)

    kq = np.arange(128)
    tri = (kq[:, None] <= kq[None, :]).astype(np.float32)
    tb = np.arange(128) // B
    msl_np = (tb[:, None] > tb[None, :]).astype(np.float32)

    common = {
        "attn_wT": C(np.asarray(kw["L0_ffn_attn_w"], f32).T),
        "next_wT": C(np.asarray(kw["L0_ffn_next_w"], f32).T),
        "ln_s": pcol(kw["L0_ffn_ln_scale"]), "ln_b": pcol(kw["L0_ffn_ln_bias"]),
        "attn_b": pcol(kw["L0_ffn_attn_b"]), "next_b": pcol(kw["L0_ffn_next_b"]),
        "wkv_vT": C(Wkv[D:, :].T), "bkv_row": C(bkv[D:].reshape(1, D)),
        "eta_in": eta,
        "qkv_k": C(np.asarray(kw["L1_attn_qkv_kernel"], f32)),
        "qkv_b": pcol(np.asarray(kw["L1_attn_qkv_bias"], f32).reshape(3 * D)),
        "W1T": C(np.asarray(kw["L1_ffn_W1"], f32).T),
        "W2T": C(np.asarray(kw["L1_ffn_W2"], f32).T),
        "l1b1": pcol(kw["L1_ffn_b1"]), "l1b2": pcol(kw["L1_ffn_b2"]),
        "ones_col": np.ones((128, 1), f32),
        "ones_colb": np.ones((128, 1), ml_dtypes.bfloat16),
        "ones_row": np.ones((1, 128), f32),
        "ones_rowb": np.ones((1, 128), ml_dtypes.bfloat16),
        "identb": np.eye(128, dtype=ml_dtypes.bfloat16),
        "msl": msl_np.astype(ml_dtypes.bfloat16),
        "msu": msl_np.T.astype(ml_dtypes.bfloat16),
        "trimask": tri.astype(ml_dtypes.bfloat16),
    }
    w1 = np.asarray(kw["L0_ffn_w1"], f32)
    w2 = np.asarray(kw["L0_ffn_w2"], f32)
    b1 = np.asarray(kw["L0_ffn_b1"], f32)
    b2 = np.asarray(kw["L0_ffn_b2"], f32)
    in_maps = []
    for i in range(NCORE):
        bpair = [i, i + 8]
        m = dict(common)
        m["xT"] = C(x[bpair].reshape(TOK, D).T)
        m["w1e"] = C(w1[i])
        m["w2e"] = C(w2[i])
        m["b1e"] = pcol(b1[i])
        m["b2e"] = pcol(b2[i])
        m["z0"] = C(W0[i * 128:(i + 1) * 128, :].T)
        m["maskT"] = C(maskv[bpair].reshape(TOK, D).T)
        in_maps.append(m)
    return in_maps


def kernel(**inputs):
    in_maps = _prep_inputs(inputs)
    nc = build_program()
    res = run_bass_kernel_spmd(nc, in_maps, core_ids=list(range(NCORE)))
    out = np.empty((B, SEQ, D), np.float32)
    for i in range(NCORE):
        oT = res.results[i]["outT"]
        out[[i, i + 8]] = np.asarray(oT).T.reshape(2, SEQ, D)
    return out


# version 06 (final)
# speedup vs baseline: 1.3764x; 1.0332x over previous
import math
import numpy as np
import ml_dtypes
from contextlib import ExitStack

import concourse.bass as bass
import concourse.mybir as mybir
from concourse.bass_utils import run_bass_kernel_spmd

F32 = mybir.dt.float32
BF16 = mybir.dt.bfloat16
I32 = mybir.dt.int32
ALU = mybir.AluOpType
ACT = mybir.ActivationFunctionType
AX = mybir.AxisListType.X

NCORE = 8
B, SEQ, D = 16, 2048, 1024
TOK = 2 * SEQ
NCK = D // 128
ST = 512
NST = TOK // ST
H0 = 4 * D
NH0 = H0 // 128
EPS = 1e-5
INV_D = 1.0 / D

KSTEP = 128
MC = KSTEP * B
NCHUNK = SEQ // KSTEP
SUBS = 8
SBR = SUBS * B
NSUB = KSTEP // SUBS

PH_L0FA = True
PH_SCAN = True
PH_L0FB = True
PH_L1FA = True
PH_ATTN = True
PH_L1FB = True

ENGS = ("te", "ve", "se", "ge", "sy")
ENG_ATTR = {"te": "tensor", "ve": "vector", "se": "scalar", "ge": "gpsimd", "sy": "sync"}


class Bld:

    def __init__(self, nc):
        self.nc = nc
        self.ops = {e: [] for e in ENGS}
        self.count = {e: 0 for e in ENGS}
        self.credits = {e: {} for e in ENGS}
        self.tickets = {}
        self.readers = {}
        self.sem_names = ["E_" + e for e in ENGS] + ["E_cc"]
        self.dma = {}
        self.cc_count = 0

    def add_dma_stream(self, name, eng, ring):
        sems = [f"D_{name}_{i}" for i in range(ring)]
        self.sem_names += sems
        self.dma[name] = dict(eng=eng, ring=sems, idx=0, cnt=[0] * ring)

    def _wait(self, eng, sem, val):
        if self.credits[eng].get(sem, 0) >= val:
            return
        self.credits[eng][sem] = val
        self.ops[eng].append(("w", sem, val))

    def _dep(self, eng, reads, writes):
        for k in reads:
            t = self.tickets.get(k)
            if t is not None:
                self._wait(eng, t[0], t[1])
        for k in writes:
            t = self.tickets.get(k)
            if t is not None:
                self._wait(eng, t[0], t[1])
            for sn, v in self.readers.get(k, {}).items():
                self._wait(eng, sn, v)

    def op(self, eng, emit, reads=(), writes=()):
        self._dep(eng, reads, writes)
        self.count[eng] += 1
        n = self.count[eng]
        sem = "E_" + eng
        self.ops[eng].append(("o", emit, sem))
        for k in reads:
            self.readers.setdefault(k, {})[sem] = n
        for k in writes:
            self.tickets[k] = (sem, n)
            self.readers[k] = {}
        return n

    def dmaop(self, stream, emit, reads=(), writes=()):
        st = self.dma[stream]
        eng = st["eng"]
        slot = st["idx"] % len(st["ring"])
        st["idx"] += 1
        semn = st["ring"][slot]
        st["cnt"][slot] += 16
        target = st["cnt"][slot]
        self._dep(eng, reads, writes)
        if target > 16:
            self._wait(eng, semn, target - 16)
        self.ops[eng].append(("d", emit, semn))
        for k in reads:
            self.readers.setdefault(k, {})[semn] = target
        for k in writes:
            self.tickets[k] = (semn, target)
            self.readers[k] = {}
        return semn, target

    def ccop(self, emit, reads=(), writes=()):
        self._dep("ge", reads, writes)
        self.cc_count += 1
        n = self.cc_count
        self.ops["ge"].append(("o", emit, "E_cc"))
        for k in reads:
            self.readers.setdefault(k, {})["E_cc"] = n
        for k in writes:
            self.tickets[k] = ("E_cc", n)
            self.readers[k] = {}
        return n

    def global_barrier(self):
        targets = [("E_" + e, self.count[e]) for e in ENGS if self.count[e] > 0]
        targets.append(("E_cc", self.cc_count))
        for st in self.dma.values():
            for i, sn in enumerate(st["ring"]):
                if st["cnt"][i] > 0:
                    targets.append((sn, st["cnt"][i]))
        for e in ENGS:
            for sn, v in targets:
                if v > 0:
                    self._wait(e, sn, v)

    def raw_wait(self, eng, key):
        t = self.tickets[key]
        self._wait(eng, t[0], t[1])

    def emit_all(self, block, sems):
        for e in ENGS:
            oplist = self.ops[e]

            def make_body(oplist=oplist):
                def body(E):
                    for item in oplist:
                        if item[0] == "w":
                            E.wait_ge(sems[item[1]], item[2])
                        else:
                            item[1](E, sems[item[2]])
                return body

            getattr(block, ENG_ATTR[e])(make_body())


def build_program():
    nc = bass.Bass("TRN2", target_bir_lowering=False, debug=False)

    def dram_in(name, shape, dt=F32):
        return nc.dram_tensor(name, list(shape), dt, kind="ExternalInput").ap()

    xT = dram_in("xT", [D, TOK])
    attn_wT = dram_in("attn_wT", [D, D])
    next_wT = dram_in("next_wT", [D, D])
    w1e = dram_in("w1e", [D, H0])
    w2e = dram_in("w2e", [H0, D])
    ln_s = dram_in("ln_s", [128, NCK])
    ln_b = dram_in("ln_b", [128, NCK])
    attn_b = dram_in("attn_b", [128, NCK])
    next_b = dram_in("next_b", [128, NCK])
    b1e = dram_in("b1e", [128, NH0])
    b2e = dram_in("b2e", [128, NCK])
    wkv_vT = dram_in("wkv_vT", [D, D])
    bkv_row = dram_in("bkv_row", [1, D])
    z0 = dram_in("z0", [D, 128])
    eta_in = dram_in("eta_in", [1, 1])
    qkv_k = dram_in("qkv_k", [D, 3 * D])
    qkv_b = dram_in("qkv_b", [128, 24])
    W1T = dram_in("W1T", [D, H0])
    W2T = dram_in("W2T", [H0, D])
    l1b1 = dram_in("l1b1", [128, NH0])
    l1b2 = dram_in("l1b2", [128, NCK])
    maskT = dram_in("maskT", [D, TOK], BF16)
    ones_col = dram_in("ones_col", [128, 1])
    ones_colb = dram_in("ones_colb", [128, 1], BF16)
    ones_row = dram_in("ones_row", [1, 128])
    ones_rowb = dram_in("ones_rowb", [1, 128], BF16)
    identb = dram_in("identb", [128, 128], BF16)
    msl = dram_in("msl", [128, 128], BF16)
    msu = dram_in("msu", [128, 128], BF16)
    trimask = dram_in("trimask", [128, 128], BF16)

    out_T = nc.dram_tensor("outT", [D, TOK], F32, kind="ExternalOutput").ap()

    d_x1in = nc.dram_tensor("d_x1in", [NCORE * D, 512], BF16).ap()
    d_x1out = nc.dram_tensor("d_x1out", [NCORE * D, 512], BF16).ap()
    PUB1 = D + MC
    d_pub1 = nc.dram_tensor("d_pub1", [2 * PUB1, MC], BF16).ap()
    d_agp1 = nc.dram_tensor("d_agp1", [NCORE * 2 * PUB1, MC], BF16).ap()
    d_p2in = nc.dram_tensor("d_p2in", [NCORE * 2 * 2 * MC, 128], BF16).ap()
    d_p2out = nc.dram_tensor("d_p2out", [NCORE * 2 * 2 * MC, 128], BF16).ap()
    d_y = nc.dram_tensor("d_y", [128, B * SEQ], BF16).ap()
    d_y3in = nc.dram_tensor("d_y3in", [NCORE * 128, TOK], BF16).ap()
    d_y3out = nc.dram_tensor("d_y3out", [NCORE * 128, TOK], BF16).ap()
    d_rsp = nc.dram_tensor("d_rsp", [D, TOK], F32).ap()
    d_xq = nc.dram_tensor("d_xq", [D, TOK], BF16).ap()
    d_w1q = nc.dram_tensor("d_w1q", [D, H0], BF16).ap()
    d_w2q = nc.dram_tensor("d_w2q", [H0, D], BF16).ap()

    b = Bld(nc)
    b.add_dma_stream("io", "sy", 4)
    b.add_dma_stream("wt", "se", 4)
    b.add_dma_stream("gw", "ge", 6)
    b.add_dma_stream("tp", "sy", 4)

    with ExitStack() as ctx:
        sems = {sn: ctx.enter_context(nc.semaphore(sn)) for sn in b.sem_names}
        sbuf = lambda name, shape, dt: ctx.enter_context(nc.sbuf_tensor(name, shape, dt))
        pst = lambda name, shape: ctx.enter_context(nc.psum_tensor(name, shape, F32))

        c_ones = sbuf("c_ones", [128, 1], F32)
        c_onesb = sbuf("c_onesb", [128, 1], BF16)
        c_ones_row = sbuf("c_ones_row", [1, 128], F32)
        c_ones_rowb = sbuf("c_ones_rowb", [1, 128], BF16)
        c_ident = sbuf("c_ident", [128, 128], BF16)
        c_msl = sbuf("c_msl", [128, 128], BF16)
        c_msu = sbuf("c_msu", [128, 128], BF16)
        c_tri = sbuf("c_tri", [128, 128], BF16)
        s_lns = sbuf("s_lns", [128, NCK], F32)
        s_lnb = sbuf("s_lnb", [128, NCK], F32)
        s_attnb = sbuf("s_attnb", [128, NCK], F32)
        s_nextb = sbuf("s_nextb", [128, NCK], F32)
        s_b1 = sbuf("s_b1", [128, NH0], F32)
        s_b2 = sbuf("s_b2", [128, NCK], F32)
        s_qkvb = sbuf("s_qkvb", [128, 24], F32)
        s_l1b1 = sbuf("s_l1b1", [128, NH0], F32)
        s_l1b2 = sbuf("s_l1b2", [128, NCK], F32)
        s_eta = sbuf("s_eta", [1, 1], F32)
        c_negc = sbuf("c_negc", [128, 1], F32)
        s_l1b1q = sbuf("s_l1b1q", [128, NH0], F32)
        s_l1b2q = sbuf("s_l1b2q", [128, NCK], F32)

        r_mu = sbuf("r_mu", [1, ST], F32)
        r_var = sbuf("r_var", [1, ST], F32)
        r_rstd = sbuf("r_rstd", [1, ST], F32)
        r_nmur = sbuf("r_nmur", [1, ST], F32)

        ps_m = [pst(f"ps_m{i}", [128, 512]) for i in range(2)]
        ps_h = [pst(f"ps_h{i}", [128, 512]) for i in range(2)]
        ps_sum = pst("ps_sum", [1, ST])
        ps_sq = pst("ps_sq", [1, ST])
        ps_bc = [pst(f"ps_bc{i}", [128, ST]) for i in range(2)]

        def ld(stream, dst, src, key):
            b.dmaop(stream, lambda E, s, d=dst, so=src: E.dma_start(out=d, in_=so).then_inc(s, 16),
                    reads=(), writes=(key,))

        ld("io", c_ones[:], ones_col, "ones")
        ld("io", c_onesb[:], ones_colb, "onesb")
        ld("io", c_ones_row[:], ones_row, "ones_row")
        ld("io", c_ones_rowb[:], ones_rowb, "ones_rowb")
        ld("io", c_ident[:], identb, "ident")
        ld("io", c_msl[:], msl, "msl")
        ld("io", c_msu[:], msu, "msu")
        ld("io", c_tri[:], trimask, "tri")
        ld("io", s_lns[:], ln_s, "lns")
        ld("io", s_lnb[:], ln_b, "lnb")
        ld("io", s_attnb[:], attn_b, "attnb")
        ld("io", s_nextb[:], next_b, "nextb")
        ld("io", s_b1[:], b1e, "b1")
        ld("io", s_b2[:], b2e, "b2")
        ld("io", s_qkvb[:], qkv_b, "qkvb")
        ld("io", s_l1b1[:], l1b1, "l1b1")
        ld("io", s_l1b2[:], l1b2, "l1b2")
        ld("io", s_eta[:], eta_in, "eta")

        b.op("te", lambda E, s: E.matmul(ps_bc[0][:, 0:1], c_ones_row[:], s_eta[:],
                                         start=True, stop=True).then_inc(s, 1),
             reads=("ones_row", "eta"), writes=("bc_r",))
        b.op("ve", lambda E, s: E.tensor_scalar(c_negc[:], ps_bc[0][:, 0:1], -2.0 / (B * D), None,
                                                ALU.mult).then_inc(s, 1),
             reads=("bc_r",), writes=("negc",))

        sq_slots = [sbuf(f"sqs{i}", [128, ST], F32) for i in range(1)]

        def emit_ln(tag, w, src_key, src_ap, src_bf16):
            ones_ap = c_onesb[:] if src_bf16 else c_ones[:]
            ones_key = "onesb" if src_bf16 else "ones"
            for c in range(NCK):
                b.op("te", lambda E, s, c=c: E.matmul(
                        ps_sum[0:1, 0:w], ones_ap, src_ap(c),
                        start=(c == 0), stop=(c == NCK - 1)).then_inc(s, 1),
                     reads=(src_key(c), ones_key), writes=("ps_sum",))
                sl = sq_slots[0]
                b.op("se", lambda E, s, c=c, sl=sl: E.activation(
                        out=sl[:, 0:w], in_=src_ap(c), func=ACT.Square).then_inc(s, 1),
                     reads=(src_key(c),), writes=(("sqslot", 0),))
                b.op("te", lambda E, s, c=c, sl=sl: E.matmul(
                        ps_sq[0:1, 0:w], c_ones[:], sl[:, 0:w],
                        start=(c == 0), stop=(c == NCK - 1)).then_inc(s, 1),
                     reads=(("sqslot", 0), "ones"), writes=("ps_sq",))
            b.op("ve", lambda E, s: E.tensor_scalar(r_mu[0:1, 0:w], ps_sum[0:1, 0:w], INV_D, None,
                                                    ALU.mult).then_inc(s, 1),
                 reads=("ps_sum",), writes=("r_mu",))
            b.op("ve", lambda E, s: E.tensor_scalar(r_var[0:1, 0:w], ps_sq[0:1, 0:w], INV_D, EPS,
                                                    ALU.mult, ALU.add).then_inc(s, 1),
                 reads=("ps_sq",), writes=("r_var",))
            b.op("ve", lambda E, s: E.scalar_tensor_tensor(r_rstd[0:1, 0:w], r_mu[0:1, 0:w], 1.0,
                                                           r_mu[0:1, 0:w], ALU.mult, ALU.mult).then_inc(s, 1),
                 reads=("r_mu",), writes=("r_rstd",))
            b.op("ve", lambda E, s: E.tensor_tensor(r_var[0:1, 0:w], r_var[0:1, 0:w], r_rstd[0:1, 0:w],
                                                    ALU.subtract).then_inc(s, 1),
                 reads=("r_var", "r_rstd"), writes=("r_var",))
            b.op("se", lambda E, s: E.activation(out=r_rstd[0:1, 0:w], in_=r_var[0:1, 0:w],
                                                 func=ACT.Sqrt).then_inc(s, 1),
                 reads=("r_var",), writes=("r_rstd",))
            b.op("ve", lambda E, s: E.reciprocal(r_rstd[0:1, 0:w], r_rstd[0:1, 0:w]).then_inc(s, 1),
                 reads=("r_rstd",), writes=("r_rstd",))
            b.op("ve", lambda E, s: E.scalar_tensor_tensor(r_nmur[0:1, 0:w], r_mu[0:1, 0:w], -1.0,
                                                           r_rstd[0:1, 0:w], ALU.mult, ALU.mult).then_inc(s, 1),
                 reads=("r_mu", "r_rstd"), writes=("r_nmur",))
            b.op("te", lambda E, s: E.matmul(ps_bc[0][:, 0:w], c_ones_row[:], r_rstd[0:1, 0:w],
                                             start=True, stop=True).then_inc(s, 1),
                 reads=("r_rstd", "ones_row"), writes=("bc_r",))
            b.op("te", lambda E, s: E.matmul(ps_bc[1][:, 0:w], c_ones_row[:], r_nmur[0:1, 0:w],
                                             start=True, stop=True).then_inc(s, 1),
                 reads=("r_nmur", "ones_row"), writes=("bc_n",))
            return "bc_r", "bc_n"

        def emit_quant(pool, src_key, src_ap, dst_key, dst_ap, M, EMINB, signed, W):
            pf, pi = pool["f"], pool["i"]
            a_i, rb_i, v_i = pi[0], pi[1], pi[2]
            m_f, s_f, d_f = pf[0], pf[1], pf[2]
            xb = src_ap.bitcast(I32)
            K = lambda nm: ("q", nm)
            drop = 23 - M
            addc = (1 << (drop - 1)) - 1
            maskc = -(1 << drop)

            def o(emit, reads, writes):
                b.op("ve", emit, reads=reads, writes=writes)

            o(lambda E, s: E.tensor_scalar(rb_i[:, :W], xb, drop, 1,
                                           ALU.logical_shift_right, ALU.bitwise_and).then_inc(s, 1),
              [src_key], [K("rb")])
            o(lambda E, s: E.scalar_tensor_tensor(v_i[:, :W], xb, addc, rb_i[:, :W],
                                                  ALU.add, ALU.add).then_inc(s, 1),
              [src_key, K("rb")], [K("v")])
            if EMINB == 1.0:
                o(lambda E, s: E.tensor_scalar(dst_ap.bitcast(I32), v_i[:, :W], maskc, None,
                                               ALU.bitwise_and).then_inc(s, 1), [K("v")], [dst_key])
                return
            o(lambda E, s: E.tensor_scalar(v_i[:, :W], v_i[:, :W], maskc, None,
                                           ALU.bitwise_and).then_inc(s, 1), [K("v")], [K("v")])
            o(lambda E, s: E.tensor_scalar(a_i[:, :W], xb, 0x7FFFFFFF, None,
                                           ALU.bitwise_and).then_inc(s, 1), [src_key], [K("a")])
            o(lambda E, s: E.tensor_scalar(m_f[:, :W], a_i[:, :W].bitcast(F32), float(2.0 ** -14),
                                           None, ALU.is_lt).then_inc(s, 1), [K("a")], [K("m")])
            o(lambda E, s: E.tensor_scalar(s_f[:, :W], src_ap, 192.0, None,
                                           ALU.add).then_inc(s, 1), [src_key], [K("s")])
            o(lambda E, s: E.tensor_scalar(s_f[:, :W], s_f[:, :W], 192.0, None,
                                           ALU.subtract).then_inc(s, 1), [K("s")], [K("s")])
            o(lambda E, s: E.tensor_tensor(d_f[:, :W], s_f[:, :W], v_i[:, :W].bitcast(F32),
                                           ALU.subtract).then_inc(s, 1), [K("s"), K("v")], [K("d")])
            o(lambda E, s: E.tensor_tensor(d_f[:, :W], d_f[:, :W], m_f[:, :W],
                                           ALU.mult).then_inc(s, 1), [K("d"), K("m")], [K("d")])
            o(lambda E, s: E.tensor_tensor(dst_ap, v_i[:, :W].bitcast(F32), d_f[:, :W],
                                           ALU.add).then_inc(s, 1), [K("v"), K("d")], [dst_key])

        def emit_wquant(pool, src_dram, dst_dram, rows, cols, tag):
            W = min(2048, cols)
            tin = pool["f"][6]
            tout = pool["bf"]
            for r0 in range(0, rows, 128):
                for c0 in range(0, cols, W):
                    b.dmaop("io", lambda E, s, r0=r0, c0=c0, W=W: E.dma_start(
                                out=tin[:, 0:W], in_=src_dram[r0:r0 + 128, c0:c0 + W]).then_inc(s, 16),
                            reads=(), writes=(("wq_in", tag),))
                    emit_quant(pool, ("wq_in", tag), tin[:, 0:W], ("wq_out", tag), tout[:, 0:W],
                               2, 113.0, True, W)
                    b.dmaop("io", lambda E, s, r0=r0, c0=c0, W=W: E.dma_start(
                                out=dst_dram[r0:r0 + 128, c0:c0 + W], in_=tout[:, 0:W]).then_inc(s, 16),
                            reads=(("wq_out", tag),), writes=(("wq_st", tag),))

        l0buf = {}
        l0ctr = [0]

        def l0ffn_alloc(px):
            sfx = str(l0ctr[0])
            l0ctr[0] += 1
            l0buf["t_xin"] = px.enter_context(nc.sbuf_tensor("t_xin" + sfx, [128, NCK * ST], BF16))
            l0buf["t_f1"] = px.enter_context(nc.sbuf_tensor("t_f1" + sfx, [128, ST], F32))
            l0buf["t_xn"] = px.enter_context(nc.sbuf_tensor("t_xn" + sfx, [128, NCK * ST], BF16))
            l0buf["t_xa"] = px.enter_context(nc.sbuf_tensor("t_xa" + sfx, [128, NCK * ST], BF16))
            l0buf["t_h"] = px.enter_context(nc.sbuf_tensor("t_h" + sfx, [128, 8 * ST], BF16))
            l0buf["t_acc"] = px.enter_context(nc.sbuf_tensor("t_acc" + sfx, [128, NCK * ST], F32))
            l0buf["w_sq"] = [px.enter_context(nc.sbuf_tensor(f"w_sq{i}" + sfx, [128, NCK * 128], BF16))
                             for i in range(2)]
            l0buf["w_1"] = [px.enter_context(nc.sbuf_tensor(f"w_1{i}" + sfx, [128, NCK * 128], BF16))
                            for i in range(2)]
            l0buf["w2s"] = [px.enter_context(nc.sbuf_tensor(f"w2s{i}" + sfx, [128, D], BF16))
                            for i in range(4)]

        def l0ffn_supertile(m0, R):
            t_xin = l0buf["t_xin"]; t_f1 = l0buf["t_f1"]; t_xn = l0buf["t_xn"]
            t_xa = l0buf["t_xa"]; t_h = l0buf["t_h"]; t_acc = l0buf["t_acc"]
            t_eo = l0buf["t_xn"]; t_g = l0buf["t_f1"]
            w_sq = l0buf["w_sq"]; w_1 = l0buf["w_1"]; w2_slots = l0buf["w2s"]

            bc_r, bc_n = emit_ln(
                "lnA", ST,
                lambda c: ("R", c),
                lambda c, m0=m0, R=R: R[:, c * TOK + m0: c * TOK + m0 + ST], False)
            for c in range(NCK):
                b.op("ve", lambda E, s, c=c, m0=m0, R=R: E.tensor_tensor(
                        t_f1[:, 0:ST], R[:, c * TOK + m0: c * TOK + m0 + ST],
                        ps_bc[0][:, 0:ST], ALU.mult).then_inc(s, 1),
                     reads=(("R", c), bc_r), writes=("t_f1",))
                b.op("ve", lambda E, s, c=c: E.tensor_tensor(
                        t_xin[:, c * ST:(c + 1) * ST], t_f1[:, 0:ST],
                        ps_bc[1][:, 0:ST], ALU.add).then_inc(s, 1),
                     reads=("t_f1", bc_n), writes=(("xin", c),))
            bc_r2, bc_n2 = emit_ln(
                "lnB", ST,
                lambda c: ("xin", c),
                lambda c: t_xin[:, c * ST:(c + 1) * ST], True)
            for c in range(NCK):
                b.op("ve", lambda E, s, c=c: E.tensor_tensor(
                        t_f1[:, 0:ST], t_xin[:, c * ST:(c + 1) * ST],
                        ps_bc[0][:, 0:ST], ALU.mult).then_inc(s, 1),
                     reads=(("xin", c), bc_r2), writes=("t_f1",))
                b.op("ve", lambda E, s, c=c: E.tensor_tensor(
                        t_f1[:, 0:ST], t_f1[:, 0:ST],
                        ps_bc[1][:, 0:ST], ALU.add).then_inc(s, 1),
                     reads=("t_f1", bc_n2), writes=("t_f1",))
                b.op("se", lambda E, s, c=c: E.activation(
                        out=t_xn[:, c * ST:(c + 1) * ST], in_=t_f1[:, 0:ST], func=ACT.Identity,
                        bias=s_lnb[:, c:c + 1], scale=s_lns[:, c:c + 1]).then_inc(s, 1),
                     reads=("t_f1", "lns", "lnb"), writes=(("xn", c),))

            for ec in range(NCK):
                w = w_sq[ec % 2]
                b.dmaop("gw", lambda E, s, ec=ec, w=w: E.dma_start(
                            out=w[:].rearrange("p (c q) -> p c q", c=NCK),
                            in_=attn_wT[:, ec * 128:(ec + 1) * 128]
                                .rearrange("(c p) q -> p c q", p=128)).then_inc(s, 16),
                        reads=(), writes=(("w_sq", ec % 2),))
                b.op("te", lambda E, s, ec=ec, w=w: _group_mm(
                        E, s, ps_m[ec % 2][:, 0:ST],
                        [(w[:, c * 128:(c + 1) * 128], t_xn[:, c * ST:(c + 1) * ST])
                         for c in range(NCK)]),
                     reads=tuple(("xn", c) for c in range(NCK)) + (("w_sq", ec % 2),),
                     writes=(("ps_m", ec % 2),))
                b.op("se", lambda E, s, ec=ec: E.activation(
                        out=t_g[:, 0:ST], in_=ps_m[ec % 2][:, 0:ST], func=ACT.Identity,
                        bias=s_attnb[:, ec:ec + 1], scale=1.0).then_inc(s, 1),
                     reads=(("ps_m", ec % 2), "attnb"), writes=("t_f1",))
                b.op("ve", lambda E, s, ec=ec: E.tensor_tensor(
                        t_xa[:, ec * ST:(ec + 1) * ST], t_g[:, 0:ST],
                        t_xin[:, ec * ST:(ec + 1) * ST], ALU.add).then_inc(s, 1),
                     reads=("t_f1", ("xin", ec)), writes=(("xa", ec),))

            for qq in range(4):
                for half in range(2):
                    for hx in range(4):
                        hh = half * 4 + hx
                        hc = qq * 8 + hh
                        w = w_1[hx % 2]
                        b.dmaop("gw", lambda E, s, hc=hc, w=w: E.dma_start(
                                    out=w[:].rearrange("p (c q) -> p c q", c=NCK),
                                    in_=w1e[:, hc * 128:(hc + 1) * 128]
                                        .rearrange("(c p) q -> p c q", p=128)).then_inc(s, 16),
                                reads=(), writes=(("w_1", hx % 2),))
                        b.op("te", lambda E, s, hx=hx, w=w: _group_mm(
                                E, s, ps_h[hx % 2][:, 0:ST],
                                [(w[:, c * 128:(c + 1) * 128], t_xa[:, c * ST:(c + 1) * ST])
                                 for c in range(NCK)]),
                             reads=tuple(("xa", c) for c in range(NCK)) + (("w_1", hx % 2),),
                             writes=(("ps_h", hx % 2),))
                        b.op("se", lambda E, s, hc=hc, hx=hx: E.activation(
                                out=t_h[:, hx * ST:(hx + 1) * ST], in_=ps_h[hx % 2][:, 0:ST],
                                func=ACT.Gelu, bias=s_b1[:, hc:hc + 1], scale=1.0).then_inc(s, 1),
                             reads=(("ps_h", hx % 2), "b1"), writes=(("h", hx),))
                        b.dmaop("gw", lambda E, s, hc=hc, hx=hx: E.dma_start(
                                    out=w2_slots[hx][:],
                                    in_=w2e[hc * 128:(hc + 1) * 128, :]).then_inc(s, 16),
                                reads=(), writes=(("w2s", hx),))
                    for dc in range(NCK):
                        b.op("te", lambda E, s, dc=dc: _group_mm(
                                E, s, ps_m[dc % 2][:, 0:ST],
                                [(w2_slots[hx][:, dc * 128:(dc + 1) * 128],
                                  t_h[:, hx * ST:(hx + 1) * ST]) for hx in range(4)]),
                             reads=tuple(("h", hx) for hx in range(4)) +
                                   tuple(("w2s", hx) for hx in range(4)),
                             writes=(("ps_m", dc % 2),))
                        if qq == 0 and half == 0:
                            b.op("se", lambda E, s, dc=dc: E.activation(
                                    out=t_acc[:, dc * ST:(dc + 1) * ST],
                                    in_=ps_m[dc % 2][:, 0:ST], func=ACT.Identity).then_inc(s, 1),
                                 reads=(("ps_m", dc % 2),), writes=(("acc", dc),))
                        else:
                            b.op("ve", lambda E, s, dc=dc: E.tensor_tensor(
                                    t_acc[:, dc * ST:(dc + 1) * ST],
                                    t_acc[:, dc * ST:(dc + 1) * ST],
                                    ps_m[dc % 2][:, 0:ST], ALU.add).then_inc(s, 1),
                                 reads=(("ps_m", dc % 2), ("acc", dc)), writes=(("acc", dc),))

            for dc in range(NCK):
                b.op("se", lambda E, s, dc=dc: E.activation(
                        out=t_eo[:, dc * ST:(dc + 1) * ST],
                        in_=t_acc[:, dc * ST:(dc + 1) * ST], func=ACT.Identity,
                        bias=s_b2[:, dc:dc + 1], scale=1.0).then_inc(s, 1),
                     reads=(("acc", dc), "b2"), writes=(("xn", dc),))
            for ec in range(NCK):
                w = w_sq[ec % 2]
                b.dmaop("gw", lambda E, s, ec=ec, w=w: E.dma_start(
                            out=w[:].rearrange("p (c q) -> p c q", c=NCK),
                            in_=next_wT[:, ec * 128:(ec + 1) * 128]
                                .rearrange("(c p) q -> p c q", p=128)).then_inc(s, 16),
                        reads=(), writes=(("w_sq", ec % 2),))
                b.op("te", lambda E, s, ec=ec, w=w: _group_mm(
                        E, s, ps_h[ec % 2][:, 0:ST],
                        [(w[:, c * 128:(c + 1) * 128], t_eo[:, c * ST:(c + 1) * ST])
                         for c in range(NCK)]),
                     reads=tuple(("xn", c) for c in range(NCK)) + (("w_sq", ec % 2),),
                     writes=(("ps_h", ec % 2),))
                b.op("se", lambda E, s, ec=ec: E.activation(
                        out=t_g[:, 0:ST], in_=ps_h[ec % 2][:, 0:ST], func=ACT.Gelu,
                        bias=s_nextb[:, ec:ec + 1], scale=1.0).then_inc(s, 1),
                     reads=(("ps_h", ec % 2), "nextb"), writes=("t_f1",))
                b.op("ve", lambda E, s, ec=ec, m0=m0, R=R: E.scalar_tensor_tensor(
                        R[:, ec * TOK + m0: ec * TOK + m0 + ST], t_g[:, 0:ST], 0.5,
                        R[:, ec * TOK + m0: ec * TOK + m0 + ST],
                        ALU.mult, ALU.add).then_inc(s, 1),
                     reads=("t_f1", ("R", ec)), writes=(("R", ec),))

        def l0ffn_phase(R):
            with ExitStack() as px:
                l0ffn_alloc(px)
                for st_i in range(NST):
                    l0ffn_supertile(st_i * ST, R)

        l1ctr = [0]

        def l1ffn_phase():
            call = l1ctr[0]
            l1ctr[0] += 1
            with ExitStack() as px:
                sfx = f"l1_{call}"
                pb = lambda nm, shape, dt: px.enter_context(nc.sbuf_tensor(nm + sfx, shape, dt))
                pool = {
                    "f": [pb(f"qf{i}", [128, 2048], F32) for i in range(8)],
                    "i": [pb(f"qi{i}", [128, 2048], I32) for i in range(4)],
                    "bf": pb("qbf", [128, 2048], BF16),
                }
                t_rt = pb("t_rt", [128, NCK * ST], F32)
                xq = pb("xq", [128, NCK * ST], BF16)
                hq = pb("hq", [128, 2048], BF16)
                t_acc = pb("t_acc", [128, NCK * ST], F32)
                w1s = [pb(f"w1s{i}", [128, NCK * 128], BF16) for i in range(2)]
                w2s = [pb(f"w2s{i}", [128, D], BF16) for i in range(4)]
                t_mask = pb("t_mask", [128, NCK * ST], BF16)

                if call == 0:
                    emit_wquant(pool, W1T, d_w1q, D, H0, "w1")
                    b.op("ve", lambda E, s: E.engine_nop().then_inc(s, 1),
                         reads=(("wq_st", "w1"),), writes=("wq1done",))
                    emit_wquant(pool, W2T, d_w2q, H0, D, "w2")
                    b.op("ve", lambda E, s: E.engine_nop().then_inc(s, 1),
                         reads=(("wq_st", "w2"),), writes=("wq2done",))
                    emit_quant(pool, "l1b1", s_l1b1[:], ("l1b1q",), s_l1b1q[:], 2, 113.0, True, NH0)
                    emit_quant(pool, "l1b2", s_l1b2[:], ("l1b2q",), s_l1b2q[:], 2, 113.0, True, NCK)

                zst = pool["f"][6]
                xst = pool["f"][7]
                for st_i in range(NST):
                    m0 = st_i * ST
                    b.dmaop("io", lambda E, s, m0=m0: E.dma_start(
                                out=t_rt[:].rearrange("p (c m) -> p c m", c=NCK),
                                in_=d_rsp.rearrange("(c p) m -> p c m", p=128)
                                    [:, :, m0:m0 + ST]).then_inc(s, 16),
                            reads=("rspv",), writes=(("rt",),))
                    b.dmaop("io", lambda E, s, m0=m0: E.dma_start(
                                out=t_mask[:].rearrange("p (c m) -> p c m", c=NCK),
                                in_=maskT.rearrange("(c p) m -> p c m", p=128)
                                    [:, :, m0:m0 + ST]).then_inc(s, 16),
                            reads=(), writes=(("mask",),))
                    bc_r, bc_n = emit_ln("l1ln", ST, lambda c: ("rt",),
                                         lambda c: t_rt[:, c * ST:(c + 1) * ST], False)
                    for half in range(2):
                        for cc in range(4):
                            c = half * 4 + cc
                            b.op("ve", lambda E, s, c=c, cc=cc: E.tensor_tensor(
                                    xst[:, cc * ST:(cc + 1) * ST], t_rt[:, c * ST:(c + 1) * ST],
                                    ps_bc[0][:, 0:ST], ALU.mult).then_inc(s, 1),
                                 reads=(("rt",), bc_r), writes=("xst",))
                            b.op("ve", lambda E, s, c=c, cc=cc: E.tensor_tensor(
                                    xst[:, cc * ST:(cc + 1) * ST], xst[:, cc * ST:(cc + 1) * ST],
                                    ps_bc[1][:, 0:ST], ALU.add).then_inc(s, 1),
                                 reads=("xst", bc_n), writes=("xst",))
                        emit_quant(pool, "xst", xst[:, 0:2048], "xq",
                                   xq[:, half * 2048:(half + 1) * 2048], 2, 113.0, True, 2048)
                    xqk = ("xq",)
                    for g in range(8):
                        for hx in range(4):
                            hc = g * 4 + hx
                            w = w1s[hx % 2]
                            b.dmaop("wt", lambda E, s, hc=hc, w=w: E.dma_start(
                                        out=w[:].rearrange("p (c q) -> p c q", c=NCK),
                                        in_=d_w1q[:, hc * 128:(hc + 1) * 128]
                                            .rearrange("(c p) q -> p c q", p=128)).then_inc(s, 16),
                                    reads=("wq1done",), writes=(("w1s", hx % 2),))
                            b.op("te", lambda E, s, hx=hx, w=w: _group_mm(
                                    E, s, ps_h[hx % 2][:, 0:ST],
                                    [(w[:, c * 128:(c + 1) * 128], xq[:, c * ST:(c + 1) * ST])
                                     for c in range(NCK)]),
                                 reads=xqk + (("w1s", hx % 2),), writes=(("ps_h", hx % 2),))
                            b.op("se", lambda E, s, hc=hc, hx=hx: E.activation(
                                    out=zst[:, hx * ST:(hx + 1) * ST], in_=ps_h[hx % 2][:, 0:ST],
                                    func=ACT.Relu, bias=s_l1b1q[:, hc:hc + 1], scale=1.0).then_inc(s, 1),
                                 reads=(("ps_h", hx % 2), ("l1b1q",)), writes=("zst",))
                            b.dmaop("wt", lambda E, s, hc=hc, hx=hx: E.dma_start(
                                        out=w2s[hx][:],
                                        in_=d_w2q[hc * 128:(hc + 1) * 128, :]).then_inc(s, 16),
                                    reads=("wq2done",), writes=(("w2s", hx),))
                        emit_quant(pool, "zst", zst[:, 0:2048], "xst",
                                   xst[:, 0:2048], 7, 1.0, False, 2048)
                        emit_quant(pool, "xst", xst[:, 0:2048], "hq",
                                   hq[:, 0:2048], 2, 113.0, False, 2048)
                        for dc in range(NCK):
                            b.op("te", lambda E, s, dc=dc: _group_mm(
                                    E, s, ps_m[dc % 2][:, 0:ST],
                                    [(w2s[hx][:, dc * 128:(dc + 1) * 128],
                                      hq[:, hx * ST:(hx + 1) * ST]) for hx in range(4)]),
                                 reads=("hq",) + tuple(("w2s", hx) for hx in range(4)),
                                 writes=(("ps_m", dc % 2),))
                            if g == 0:
                                b.op("se", lambda E, s, dc=dc: E.activation(
                                        out=t_acc[:, dc * ST:(dc + 1) * ST],
                                        in_=ps_m[dc % 2][:, 0:ST], func=ACT.Identity).then_inc(s, 1),
                                     reads=(("ps_m", dc % 2),), writes=(("acc", dc),))
                            else:
                                b.op("ve", lambda E, s, dc=dc: E.tensor_tensor(
                                        t_acc[:, dc * ST:(dc + 1) * ST],
                                        t_acc[:, dc * ST:(dc + 1) * ST],
                                        ps_m[dc % 2][:, 0:ST], ALU.add).then_inc(s, 1),
                                     reads=(("ps_m", dc % 2), ("acc", dc)), writes=(("acc", dc),))
                    for half in range(2):
                        for cc in range(4):
                            dc = half * 4 + cc
                            b.op("se", lambda E, s, dc=dc, cc=cc: E.activation(
                                    out=zst[:, cc * ST:(cc + 1) * ST],
                                    in_=t_acc[:, dc * ST:(dc + 1) * ST], func=ACT.Identity,
                                    bias=s_l1b2q[:, dc:dc + 1], scale=1.0).then_inc(s, 1),
                                 reads=(("acc", dc), ("l1b2q",)), writes=("zst",))
                        emit_quant(pool, "zst", zst[:, 0:2048], "xst",
                                   xst[:, 0:2048], 7, 1.0, True, 2048)
                        b.op("ve", lambda E, s, half=half: E.tensor_tensor(
                                xst[:, 0:2048], xst[:, 0:2048],
                                t_mask[:, half * 2048:(half + 1) * 2048], ALU.mult).then_inc(s, 1),
                             reads=("xst", ("mask",)), writes=("xst",))
                        b.op("ve", lambda E, s, half=half: E.scalar_tensor_tensor(
                                t_rt[:, half * 2048:(half + 1) * 2048], xst[:, 0:2048], 0.5,
                                t_rt[:, half * 2048:(half + 1) * 2048],
                                ALU.mult, ALU.add).then_inc(s, 1),
                             reads=("xst", ("rt",)), writes=(("rt",),))
                    b.dmaop("io", lambda E, s, m0=m0: E.dma_start(
                                out=d_rsp.rearrange("(c p) m -> p c m", p=128)
                                    [:, :, m0:m0 + ST],
                                in_=t_rt[:].rearrange("p (c m) -> p c m", c=NCK)).then_inc(s, 16),
                            reads=(("rt",),), writes=(("rspv",),))

        GOFF = [0] * 17
        for q in range(16):
            GOFF[q + 1] = GOFF[q] + (16 - q) * 128

        def scan_pre(R):
            with ExitStack() as px:
                stg = px.enter_context(nc.sbuf_tensor("sc_stg", [128, NCK * ST], BF16))
                for st_i in range(NST):
                    m0 = st_i * ST
                    bc_r, bc_n = emit_ln(
                        "lnS", ST, lambda c: ("R", c),
                        lambda c, m0=m0, R=R: R[:, c * TOK + m0: c * TOK + m0 + ST], False)
                    for c in range(NCK):
                        b.op("ve", lambda E, s, c=c, m0=m0, R=R: E.tensor_tensor(
                                sq_slots[0][:, 0:ST], R[:, c * TOK + m0: c * TOK + m0 + ST],
                                ps_bc[0][:, 0:ST], ALU.mult).then_inc(s, 1),
                             reads=(("R", c), bc_r), writes=(("sqslot", 0),))
                        b.op("ve", lambda E, s, c=c: E.tensor_tensor(
                                stg[:, c * ST:(c + 1) * ST], sq_slots[0][:, 0:ST],
                                ps_bc[1][:, 0:ST], ALU.add).then_inc(s, 1),
                             reads=(("sqslot", 0), bc_n), writes=(("stg", c),))
                    p_ = m0 // SEQ
                    s0 = m0 % SEQ
                    for half in range(2):
                        jd = s0 // 256 + half
                        b.dmaop("io", lambda E, s, p_=p_, jd=jd, half=half: E.dma_start(
                                    out=d_x1in[jd * D:(jd + 1) * D, p_ * 256:(p_ + 1) * 256]
                                        .rearrange("(c p) t -> p c t", p=128),
                                    in_=stg[:].rearrange("p (c t) -> p c t", c=NCK)
                                        [:, :, half * 256:(half + 1) * 256]).then_inc(s, 16),
                                reads=tuple(("stg", c) for c in range(NCK)),
                                writes=(("x1in", st_i, half),))
                for c in range(NCK):
                    b.dmaop("io", lambda E, s, c=c, R=R: E.dma_start(
                                out=d_rsp[c * 128:(c + 1) * 128, :],
                                in_=R[:, c * TOK:(c + 1) * TOK]).then_inc(s, 16),
                            reads=(("R", c),), writes=(("rsp", c),))

        def _a2a(E, s, ins_ap, outs_ap):
            from concourse.aot_env import allow_ici_all_to_all
            with allow_ici_all_to_all():
                E.collective_compute(
                    "AllToAll", ALU.bypass, replica_groups=[list(range(NCORE))],
                    ins=[ins_ap], outs=[outs_ap]).then_inc(s, 1)

        def scan_phase_a():
            b.ccop(lambda E, s: _a2a(E, s, d_x1in.opt(), d_x1out.opt()),
                   reads=tuple(("x1in", i, h) for i in range(NST) for h in range(2)),
                   writes=("x1out",))
            pubkeys = []
            with ExitStack() as px:
                Xt = px.enter_context(nc.sbuf_tensor("sc_Xt", [128, NCK * MC], BF16))
                UV = px.enter_context(nc.sbuf_tensor("sc_UV", [128, 16 * 2048], BF16))
                Gb = px.enter_context(nc.sbuf_tensor("sc_G", [128, GOFF[16]], BF16))
                wkvs = px.enter_context(nc.sbuf_tensor("sc_wkv", [128, NCK * D], BF16))
                bkvs = px.enter_context(nc.sbuf_tensor("sc_bkv", [1, D], BF16))
                utst = px.enter_context(nc.sbuf_tensor("sc_utst", [128, NCK * 128], BF16))
                tls = {nm: px.enter_context(nc.sbuf_tensor("sc_" + nm, [128, 128], BF16))
                       for nm in ("tT", "tL", "t2", "t2t", "t4", "a1", "b2", "b4", "m1", "inv")}
                b.dmaop("gw", lambda E, s: E.dma_start(
                            out=wkvs[:].rearrange("p (c q) -> p c q", c=NCK),
                            in_=wkv_vT.rearrange("(c p) q -> p c q", p=128)).then_inc(s, 16),
                        writes=("wkvs",))
                b.dmaop("gw", lambda E, s: E.dma_start(out=bkvs[:], in_=bkv_row).then_inc(s, 16),
                        writes=("bkvs",))

                for cidx in range(2):
                    b.global_barrier()
                    base1 = cidx * PUB1
                    for o in range(NCORE):
                        for p_ in range(2):
                            blk = o * 2 + p_
                            b.dmaop("io", lambda E, s, o=o, p_=p_, blk=blk, cidx=cidx: E.dma_start(
                                        out=UV[:].rearrange("p (c r) -> p c r", c=16)
                                            [:, 0:NCK, blk * 128:(blk + 1) * 128],
                                        in_=d_x1out[o * D:(o + 1) * D,
                                                    p_ * 256 + cidx * 128: p_ * 256 + (cidx + 1) * 128]
                                            .rearrange("(c p) t -> p c t", p=128)).then_inc(s, 16),
                                    reads=("x1out",), writes=(("Xc", blk),))
                    for o in range(NCORE):
                        for p_ in range(2):
                            blk = o * 2 + p_
                            bglob = o + 8 * p_
                            b.op("ve", lambda E, s, blk=blk, bglob=bglob: E.tensor_copy(
                                    Xt[:].rearrange("p (c t bb) -> p c t bb", c=NCK, bb=16)
                                        [:, :, :, bglob],
                                    UV[:].rearrange("p (c r) -> p c r", c=16)
                                        [:, 0:NCK, blk * 128:(blk + 1) * 128]).then_inc(s, 1),
                                 reads=(("Xc", blk),), writes=(("Xt", o, p_),))
                    xtk = tuple(("Xt", o, p_) for o in range(NCORE) for p_ in range(2))
                    for j in range(16):
                        for cc in range(NCK):
                            b.dmaop("tp", lambda E, s, j=j, cc=cc: E.dma_start(
                                        out=UV[:, j * 2048 + cc * 128: j * 2048 + (cc + 1) * 128],
                                        in_=Xt[:, cc * MC + j * 128: cc * MC + (j + 1) * 128],
                                        transpose=True).then_inc(s, 16),
                                    reads=xtk, writes=(("UVx", j),))
                        b.dmaop("gw", lambda E, s, j=j, base1=base1: E.dma_start(
                                    out=d_pub1[base1 + D + j * 128: base1 + D + (j + 1) * 128, 0:1024],
                                    in_=UV[:, j * 2048: j * 2048 + 1024]).then_inc(s, 16),
                                reads=(("UVx", j),), writes=(("pubx", cidx, j),))
                        pubkeys.append(("pubx", cidx, j))
                    for j in range(16):
                        for hf in range(2):
                            def vmm(E, s, j=j, hf=hf):
                                for c in range(NCK):
                                    E.matmul(ps_m[hf][:, 0:512],
                                             Xt[:, c * MC + j * 128: c * MC + (j + 1) * 128],
                                             wkvs[:, c * D + hf * 512: c * D + (hf + 1) * 512],
                                             start=(c == 0), stop=False)
                                E.matmul(ps_m[hf][:, 0:512], c_ones_rowb[:],
                                         bkvs[0:1, hf * 512:(hf + 1) * 512],
                                         start=False, stop=True).then_inc(s, 1)
                            b.op("te", vmm,
                                 reads=xtk + ("wkvs", "bkvs", "ones_rowb"),
                                 writes=(("ps_m", hf),))
                            b.op("se", lambda E, s, j=j, hf=hf: E.activation(
                                    out=UV[:, j * 2048 + 1024 + hf * 512: j * 2048 + 1024 + (hf + 1) * 512],
                                    in_=ps_m[hf][:, 0:512], func=ACT.Identity).then_inc(s, 1),
                                 reads=(("ps_m", hf),), writes=(("UVv", j, hf),))
                    uvvk = tuple(("UVv", j, hf) for j in range(16) for hf in range(2))
                    for dst_i in range(NCORE):
                        rbase = ((dst_i * 2 + cidx) * 2 + 0) * MC
                        b.dmaop("gw", lambda E, s, dst_i=dst_i, rbase=rbase: E.dma_start(
                                    out=d_p2in[rbase: rbase + MC, :]
                                        .rearrange("(j p) q -> p j q", p=128),
                                    in_=UV[:].rearrange("p (j q) -> p j q", j=16)
                                        [:, :, 1024 + dst_i * 128: 1024 + (dst_i + 1) * 128]
                                    ).then_inc(s, 16),
                                reads=uvvk, writes=(("pubv", cidx, dst_i),))
                        pubkeys.append(("pubv", cidx, dst_i))
                    for q in range(16):
                        g0 = q
                        while g0 < 16:
                            wblk = min(4, 16 - g0)
                            def gmm(E, s, q=q, g0=g0, wblk=wblk):
                                for c in range(NCK):
                                    E.matmul(ps_h[0][:, 0:wblk * 128],
                                             Xt[:, c * MC + q * 128: c * MC + (q + 1) * 128],
                                             Xt[:, c * MC + g0 * 128: c * MC + (g0 + wblk) * 128],
                                             start=(c == 0), stop=(c == NCK - 1)).then_inc(s, 1) \
                                        if c == NCK - 1 else E.matmul(
                                             ps_h[0][:, 0:wblk * 128],
                                             Xt[:, c * MC + q * 128: c * MC + (q + 1) * 128],
                                             Xt[:, c * MC + g0 * 128: c * MC + (g0 + wblk) * 128],
                                             start=(c == 0), stop=False)
                            b.op("te", gmm, reads=xtk, writes=(("ps_h", 0),))
                            b.op("se", lambda E, s, q=q, g0=g0, wblk=wblk: E.activation(
                                    out=Gb[:, GOFF[q] + (g0 - q) * 128: GOFF[q] + (g0 - q + wblk) * 128],
                                    in_=ps_h[0][:, 0:wblk * 128], func=ACT.Identity).then_inc(s, 1),
                                 reads=(("ps_h", 0),), writes=(("G", q, g0),))
                            g0 += wblk
                    gk = tuple(("G", q, g0) for q in range(16) for g0 in range(q, 16, 4))
                    for p_ in range(16):
                        if p_ > 0:
                            for g in range(4):
                                def rmm2(E, s, p_=p_, g=g):
                                    ins = None
                                    for qi in range(p_):
                                        ins = E.matmul(ps_m[g % 2][:, 0:512],
                                                 Gb[:, GOFF[qi] + (p_ - qi) * 128: GOFF[qi] + (p_ - qi + 1) * 128],
                                                 UV[:, qi * 2048 + g * 512: qi * 2048 + (g + 1) * 512],
                                                 start=(qi == 0), stop=(qi == p_ - 1))
                                    ins.then_inc(s, 1)
                                b.op("te", rmm2,
                                     reads=gk + tuple(("UVs", qi) for qi in range(p_)) + uvvk +
                                           tuple(("UVx", j) for j in range(16)),
                                     writes=(("ps_m", g % 2),))
                                b.op("ve", lambda E, s, p_=p_, g=g: E.scalar_tensor_tensor(
                                        UV[:, p_ * 2048 + g * 512: p_ * 2048 + (g + 1) * 512],
                                        ps_m[g % 2][:, 0:512], c_negc[:, 0:1],
                                        UV[:, p_ * 2048 + g * 512: p_ * 2048 + (g + 1) * 512],
                                        ALU.mult, ALU.add).then_inc(s, 1),
                                     reads=(("ps_m", g % 2), "negc", ("UVx", p_), ("UVv", p_, g % 2)),
                                     writes=(("UVr", p_, g), ("UVx", p_), ("UVv", p_, g % 2)))
                        b.op("ve", lambda E, s, p_=p_: E.scalar_tensor_tensor(
                                tls["tT"][:], Gb[:, GOFF[p_]:GOFF[p_] + 128], c_negc[:, 0:1],
                                c_msu[:], ALU.mult, ALU.mult).then_inc(s, 1),
                             reads=gk + ("negc", "msu"), writes=(("tT",),))
                        b.op("ve", lambda E, s, p_=p_: E.scalar_tensor_tensor(
                                tls["tL"][:], Gb[:, GOFF[p_]:GOFF[p_] + 128], c_negc[:, 0:1],
                                c_msl[:], ALU.mult, ALU.mult).then_inc(s, 1),
                             reads=gk + ("negc", "msl"), writes=(("tL",),))
                        def mm1(dst, lh, rh, rk, wk):
                            b.op("te", lambda E, s, lh=lh, rh=rh: E.matmul(
                                    ps_h[1][:, 0:128], lh[:], rh[:],
                                    start=True, stop=True).then_inc(s, 1),
                                 reads=rk, writes=(("ps_h", 1),))
                            b.op("ve", lambda E, s, dst=dst: E.tensor_copy(
                                    dst[:], ps_h[1][:, 0:128]).then_inc(s, 1),
                                 reads=(("ps_h", 1),), writes=(wk,))
                        mm1(tls["t2"], tls["tT"], tls["tL"], (("tT",), ("tL",)), ("t2",))
                        mm1(tls["t2t"], tls["tL"], tls["tT"], (("tT",), ("tL",)), ("t2t",))
                        mm1(tls["t4"], tls["t2t"], tls["t2"], (("t2",), ("t2t",)), ("t4",))
                        b.op("ve", lambda E, s: E.tensor_tensor(
                                tls["a1"][:], tls["tT"][:], c_ident[:], ALU.add).then_inc(s, 1),
                             reads=(("tT",), "ident"), writes=(("a1",),))
                        b.op("ve", lambda E, s: E.tensor_tensor(
                                tls["b2"][:], tls["t2"][:], c_ident[:], ALU.add).then_inc(s, 1),
                             reads=(("t2",), "ident"), writes=(("b2",),))
                        b.op("ve", lambda E, s: E.tensor_tensor(
                                tls["b4"][:], tls["t4"][:], c_ident[:], ALU.add).then_inc(s, 1),
                             reads=(("t4",), "ident"), writes=(("b4",),))
                        mm1(tls["m1"], tls["b2"], tls["a1"], (("b2",), ("a1",)), ("m1",))
                        mm1(tls["inv"], tls["b4"], tls["m1"], (("b4",), ("m1",)), ("inv",))
                        for g in range(4):
                            b.op("te", lambda E, s, p_=p_, g=g: E.matmul(
                                    ps_m[g % 2][:, 0:512], tls["inv"][:],
                                    UV[:, p_ * 2048 + g * 512: p_ * 2048 + (g + 1) * 512],
                                    start=True, stop=True).then_inc(s, 1),
                                 reads=(("inv",),) + ((("UVr", p_, g),) if p_ > 0 else
                                                      (("UVx", p_), ("UVv", p_, g % 2))),
                                 writes=(("ps_m", g % 2),))
                            b.op("se", lambda E, s, p_=p_, g=g: E.activation(
                                    out=UV[:, p_ * 2048 + g * 512: p_ * 2048 + (g + 1) * 512],
                                    in_=ps_m[g % 2][:, 0:512], func=ACT.Identity).then_inc(s, 1),
                                 reads=(("ps_m", g % 2),), writes=(("UVs", p_),))
                    uvsk = tuple(("UVs", j) for j in range(16))
                    for j in range(16):
                        b.dmaop("tp", lambda E, s, j=j: E.dma_start(
                                    out=utst[:].rearrange("p (c q) -> p c q", c=NCK),
                                    in_=UV[:, j * 2048: j * 2048 + 1024]
                                        .rearrange("p (c q) -> p c q", c=NCK),
                                    transpose=True).then_inc(s, 16),
                                reads=uvsk, writes=(("utst",),))
                        b.dmaop("gw", lambda E, s, j=j, base1=base1: E.dma_start(
                                    out=d_pub1[base1: base1 + 1024, j * 128:(j + 1) * 128]
                                        .rearrange("(c p) q -> p c q", p=128),
                                    in_=utst[:].rearrange("p (c q) -> p c q", c=NCK)
                                    ).then_inc(s, 16),
                                reads=(("utst",),), writes=(("pubu", cidx, j),))
                        pubkeys.append(("pubu", cidx, j))
                    for dst_i in range(NCORE):
                        rbase = ((dst_i * 2 + cidx) * 2 + 1) * MC
                        b.dmaop("gw", lambda E, s, dst_i=dst_i, rbase=rbase: E.dma_start(
                                    out=d_p2in[rbase: rbase + MC, :]
                                        .rearrange("(j p) q -> p j q", p=128),
                                    in_=UV[:].rearrange("p (j q) -> p j q", j=16)
                                        [:, :, 1024 + dst_i * 128: 1024 + (dst_i + 1) * 128]
                                    ).then_inc(s, 16),
                                reads=uvsk, writes=(("pubf", cidx, dst_i),))
                        pubkeys.append(("pubf", cidx, dst_i))
            return pubkeys

        def scan_phase_b(pubkeys):
            b.ccop(lambda E, s: E.collective_compute(
                        "AllGather", ALU.bypass, replica_groups=[list(range(NCORE))],
                        ins=[d_pub1.opt()], outs=[d_agp1.opt()],
                    ).then_inc(s, 1),
                   reads=tuple(pubkeys), writes=("agp1",))
            b.ccop(lambda E, s: _a2a(E, s, d_p2in.opt(), d_p2out.opt()),
                   reads=tuple(pubkeys), writes=("p2out",))
            b.global_barrier()
            with ExitStack() as px:
                Ut = [px.enter_context(nc.sbuf_tensor(f"sb_Ut{i}", [128, NCK * MC], BF16)) for i in range(2)]
                Xr = [px.enter_context(nc.sbuf_tensor(f"sb_Xr{i}", [128, 16 * 1024], BF16)) for i in range(2)]
                Vb = [px.enter_context(nc.sbuf_tensor(f"sb_Vb{i}", [128, MC], BF16)) for i in range(2)]
                Fb = [px.enter_context(nc.sbuf_tensor(f"sb_Fb{i}", [128, MC], BF16)) for i in range(2)]
                UZt = px.enter_context(nc.sbuf_tensor("sb_UZt", [128, MC], BF16))
                UZr = px.enter_context(nc.sbuf_tensor("sb_UZr", [128, MC], BF16))
                Erow = px.enter_context(nc.sbuf_tensor("sb_Erow", [128, MC], BF16))
                Yrow = px.enter_context(nc.sbuf_tensor("sb_Yrow", [128, MC], BF16))
                Yt = px.enter_context(nc.sbuf_tensor("sb_Yt", [128, MC], BF16))
                Yb = px.enter_context(nc.sbuf_tensor("sb_Yb", [128, MC], BF16))
                Z = px.enter_context(nc.sbuf_tensor("sb_Z", [128, NCK * 128], F32))
                Zb = px.enter_context(nc.sbuf_tensor("sb_Zb", [128, NCK * 128], BF16))
                b.dmaop("io", lambda E, s: E.dma_start(
                            out=Z[:].rearrange("p (c q) -> p c q", c=NCK),
                            in_=z0.rearrange("(c p) q -> p c q", p=128)).then_inc(s, 16),
                        writes=(("Z",),))
                b.op("se", lambda E, s: E.copy(out=Zb[:], in_=Z[:]).then_inc(s, 1),
                     reads=(("Z",),), writes=(("Zb",),))
                for j in range(NCHUNK):
                    sl = j % 2
                    b.dmaop("io", lambda E, s, j=j, sl=sl: E.dma_start(
                                out=Ut[sl][:].rearrange("p (c m) -> p c m", c=NCK),
                                in_=d_agp1[j * PUB1: j * PUB1 + 1024, :]
                                    .rearrange("(c p) m -> p c m", p=128)).then_inc(s, 16),
                            reads=("agp1",), writes=(("Ut", sl),))
                    b.dmaop("io", lambda E, s, j=j, sl=sl: E.dma_start(
                                out=Xr[sl][:].rearrange("p (r q) -> p r q", r=16),
                                in_=d_agp1[j * PUB1 + 1024: j * PUB1 + 1024 + MC, 0:1024]
                                    .rearrange("(r p) q -> p r q", p=128)).then_inc(s, 16),
                            reads=("agp1",), writes=(("Xr", sl),))
                    b.dmaop("io", lambda E, s, j=j, sl=sl: E.dma_start(
                                out=Vb[sl][:].rearrange("p (r q) -> p r q", r=16),
                                in_=d_p2out[(j * 2 + 0) * MC: (j * 2 + 1) * MC, :]
                                    .rearrange("(r p) q -> p r q", p=128)).then_inc(s, 16),
                            reads=("p2out",), writes=(("Vb", sl),))
                    b.dmaop("io", lambda E, s, j=j, sl=sl: E.dma_start(
                                out=Fb[sl][:].rearrange("p (r q) -> p r q", r=16),
                                in_=d_p2out[(j * 2 + 1) * MC: (j * 2 + 2) * MC, :]
                                    .rearrange("(r p) q -> p r q", p=128)).then_inc(s, 16),
                            reads=("p2out",), writes=(("Fb", sl),))
                    for g in range(4):
                        def emm(E, s, g=g, sl=sl):
                            ins = None
                            for c in range(NCK):
                                ins = E.matmul(ps_m[g % 2][:, 0:512],
                                               Zb[:, c * 128:(c + 1) * 128],
                                               Ut[sl][:, c * MC + g * 512: c * MC + (g + 1) * 512],
                                               start=(c == 0), stop=(c == NCK - 1))
                            ins.then_inc(s, 1)
                        b.op("te", emm, reads=(("Zb",), ("Ut", sl)), writes=(("ps_m", g % 2),))
                        b.op("se", lambda E, s, g=g: E.activation(
                                out=UZt[:, g * 512:(g + 1) * 512], in_=ps_m[g % 2][:, 0:512],
                                func=ACT.Identity).then_inc(s, 1),
                             reads=(("ps_m", g % 2),), writes=(("UZt", g),))
                    for g in range(4):
                        b.dmaop("tp", lambda E, s, g=g: E.dma_start(
                                    out=UZr[:, g * 512:(g + 1) * 512]
                                        .rearrange("p (t q) -> p t q", t=4),
                                    in_=UZt[:, g * 512:(g + 1) * 512]
                                        .rearrange("p (t q) -> p t q", t=4),
                                    transpose=True).then_inc(s, 16),
                                reads=(("UZt", g),), writes=(("UZr", g),))
                    uzk = tuple(("UZr", g) for g in range(4))
                    b.op("ve", lambda E, s, sl=sl: E.tensor_tensor(
                            Erow[:], UZr[:], Fb[sl][:], ALU.subtract).then_inc(s, 1),
                         reads=uzk + (("Fb", sl),), writes=(("Erow",),))
                    b.op("ve", lambda E, s, sl=sl: E.tensor_tensor(
                            Yrow[:], Erow[:], Vb[sl][:], ALU.add).then_inc(s, 1),
                         reads=(("Erow",), ("Vb", sl)), writes=(("Yrow",),))
                    for g in range(4):
                        b.dmaop("tp", lambda E, s, g=g: E.dma_start(
                                    out=Yt[:, g * 512:(g + 1) * 512]
                                        .rearrange("p (t q) -> p t q", t=4),
                                    in_=Yrow[:, g * 512:(g + 1) * 512]
                                        .rearrange("p (t q) -> p t q", t=4),
                                    transpose=True).then_inc(s, 16),
                                reads=(("Yrow",),), writes=(("Yt", g),))
                    b.op("ve", lambda E, s: E.tensor_copy(
                            Yb[:].rearrange("p (bb t) -> p bb t", bb=B),
                            Yt[:].rearrange("p (t bb) -> p bb t", bb=B)).then_inc(s, 1),
                         reads=tuple(("Yt", g) for g in range(4)), writes=(("Yb",),))
                    b.dmaop("io", lambda E, s, j=j: E.dma_start(
                                out=d_y.rearrange("p (bb t2) -> p bb t2", bb=B)
                                    [:, :, j * 128:(j + 1) * 128],
                                in_=Yb[:].rearrange("p (bb t) -> p bb t", bb=B)).then_inc(s, 16),
                            reads=(("Yb",),), writes=(("d_y", j),))
                    for bank in range(2):
                        for q4 in range(4):
                            dc = bank * 4 + q4
                            def zmm(E, s, dc=dc, bank=bank, q4=q4, sl=sl):
                                ins = None
                                for r in range(16):
                                    ins = E.matmul(ps_h[bank][:, q4 * 128:(q4 + 1) * 128],
                                                   Xr[sl][:, r * 1024 + dc * 128: r * 1024 + (dc + 1) * 128],
                                                   Erow[:, r * 128:(r + 1) * 128],
                                                   start=(r == 0), stop=(r == 15))
                                ins.then_inc(s, 1)
                            b.op("te", zmm, reads=(("Xr", sl), ("Erow",)),
                                 writes=(("ps_hz", bank, q4),))
                            b.op("ve", lambda E, s, dc=dc, bank=bank, q4=q4: E.scalar_tensor_tensor(
                                    Z[:, dc * 128:(dc + 1) * 128],
                                    ps_h[bank][:, q4 * 128:(q4 + 1) * 128], c_negc[:, 0:1],
                                    Z[:, dc * 128:(dc + 1) * 128],
                                    ALU.mult, ALU.add).then_inc(s, 1),
                                 reads=(("ps_hz", bank, q4), "negc", ("Z",)), writes=(("Z",),))
                    b.op("ve", lambda E, s: E.tensor_copy(Zb[:], Z[:]).then_inc(s, 1),
                         reads=(("Z",),), writes=(("Zb",),))
                for dst_i in range(NCORE):
                    for p_ in range(2):
                        bglob = dst_i + 8 * p_
                        b.dmaop("io", lambda E, s, dst_i=dst_i, p_=p_, bglob=bglob: E.dma_start(
                                    out=d_y3in[dst_i * 128:(dst_i + 1) * 128,
                                               p_ * SEQ:(p_ + 1) * SEQ],
                                    in_=d_y[:, bglob * SEQ:(bglob + 1) * SEQ]
                                    ).then_inc(s, 16),
                                reads=tuple(("d_y", j) for j in range(NCHUNK)),
                                writes=(("y3in", dst_i, p_),))
            b.ccop(lambda E, s: _a2a(E, s, d_y3in.opt(), d_y3out.opt()),
                   reads=tuple(("y3in", i, p_) for i in range(NCORE) for p_ in range(2)),
                   writes=("y3out",))

        def scan_post(R):
            with ExitStack() as px:
                yst = [px.enter_context(nc.sbuf_tensor(f"sc_yst{i}", [128, ST], BF16)) for i in range(2)]
                for c in range(NCK):
                    b.dmaop("io", lambda E, s, c=c, R=R: E.dma_start(
                                out=R[:, c * TOK:(c + 1) * TOK],
                                in_=d_rsp[c * 128:(c + 1) * 128, :]).then_inc(s, 16),
                            reads=(("rsp", c), "y3out"), writes=(("R", c),))
                k = 0
                for c in range(NCK):
                    for st_i in range(NST):
                        m0 = st_i * ST
                        sl = k % 2
                        k += 1
                        b.dmaop("io", lambda E, s, c=c, m0=m0, sl=sl: E.dma_start(
                                    out=yst[sl][:],
                                    in_=d_y3out[c * 128:(c + 1) * 128, m0:m0 + ST]).then_inc(s, 16),
                                reads=("y3out",), writes=(("yst", sl),))
                        b.op("ve", lambda E, s, c=c, m0=m0, sl=sl, R=R: E.tensor_tensor(
                                R[:, c * TOK + m0: c * TOK + m0 + ST],
                                R[:, c * TOK + m0: c * TOK + m0 + ST],
                                yst[sl][:], ALU.add).then_inc(s, 1),
                             reads=(("yst", sl), ("R", c)), writes=(("R", c),))

        def attn_phase():
            SCL = 1.0 / math.sqrt(2.0 * D)
            with ExitStack() as px:
                pb = lambda nm, shape, dt: px.enter_context(nc.sbuf_tensor(nm + "_at", shape, dt))
                t_rt = pb("t_rt", [128, NCK * ST], F32)
                t_f1a = pb("t_f1a", [128, ST], F32)
                xnq = pb("xnq", [128, NCK * ST], BF16)
                qkvT = pb("qkvT", [128, 24 * ST], BF16)
                wq = [pb(f"wq{i}", [128, NCK * 128], BF16) for i in range(2)]
                Qh = pb("Qh", [128, SEQ], BF16)
                Kh = pb("Kh", [128, SEQ], BF16)
                Vh = pb("Vh", [128, SEQ], BF16)
                Vrow = pb("Vrow", [128, 16 * 64], BF16)
                Vaug = pb("Vaug", [128, 16 * 65], BF16)
                Pbuf = pb("Pbuf", [128, 16 * 512], BF16)
                oh = pb("oh", [128, SEQ], BF16)
                ohg = pb("ohg", [128, SEQ], BF16)
                gst = pb("gst", [128, SEQ], BF16)
                ost = pb("ost", [128, 512], F32)
                rsum = pb("rsum", [1, 512], F32)
                aacc = pb("aacc", [128, NCK * ST], BF16)
                c_m4 = pb("c_m4", [128, 1], F32)
                b.op("ve", lambda E, s: E.memset(c_m4[:], -4.0).then_inc(s, 1), writes=("m4",))

                for st_i in range(NST):
                    m0 = st_i * ST
                    hh0 = (m0 % SEQ) // 128
                    b.dmaop("io", lambda E, s, m0=m0: E.dma_start(
                                out=t_rt[:].rearrange("p (c m) -> p c m", c=NCK),
                                in_=d_rsp.rearrange("(c p) m -> p c m", p=128)
                                    [:, :, m0:m0 + ST]).then_inc(s, 16),
                            reads=("rspv",), writes=(("rt",),))
                    bc_r, bc_n = emit_ln("atln", ST, lambda c: ("rt",),
                                         lambda c: t_rt[:, c * ST:(c + 1) * ST], False)
                    for c in range(NCK):
                        b.op("ve", lambda E, s, c=c: E.tensor_tensor(
                                t_f1a[:, 0:ST], t_rt[:, c * ST:(c + 1) * ST],
                                ps_bc[0][:, 0:ST], ALU.mult).then_inc(s, 1),
                             reads=(("rt",), bc_r), writes=("t_f1a",))
                        b.op("ve", lambda E, s, c=c: E.tensor_tensor(
                                xnq[:, c * ST:(c + 1) * ST], t_f1a[:, 0:ST],
                                ps_bc[1][:, 0:ST], ALU.add).then_inc(s, 1),
                             reads=("t_f1a", bc_n), writes=("xnq",))
                    for ec in range(24):
                        w = wq[ec % 2]
                        b.dmaop("gw", lambda E, s, ec=ec, w=w: E.dma_start(
                                    out=w[:].rearrange("p (c q) -> p c q", c=NCK),
                                    in_=qkv_k[:, ec * 128:(ec + 1) * 128]
                                        .rearrange("(c p) q -> p c q", p=128)).then_inc(s, 16),
                                reads=(), writes=(("wq", ec % 2),))
                        b.op("te", lambda E, s, ec=ec, w=w: _group_mm(
                                E, s, ps_m[ec % 2][:, 0:ST],
                                [(w[:, c * 128:(c + 1) * 128], xnq[:, c * ST:(c + 1) * ST])
                                 for c in range(NCK)]),
                             reads=("xnq", ("wq", ec % 2)), writes=(("ps_m", ec % 2),))
                        b.op("se", lambda E, s, ec=ec: E.activation(
                                out=qkvT[:, ec * ST:(ec + 1) * ST], in_=ps_m[ec % 2][:, 0:ST],
                                func=ACT.Identity, bias=s_qkvb[:, ec:ec + 1], scale=1.0).then_inc(s, 1),
                             reads=(("ps_m", ec % 2), "qkvb"), writes=("qkvT",))
                    for hi in range(4):
                        for mat, off, dst in (("q", 0, Qh), ("k", 64, Kh), ("v", 128, Vh)):
                            for par in range(2):
                                f0 = 192 * par + off
                                ec0, pp0 = f0 // 128, f0 % 128
                                b.dmaop("tp", lambda E, s, hi=hi, par=par, ec0=ec0, pp0=pp0:
                                        E.dma_start(
                                            out=gst[0:64, par * 1024:(par + 1) * 1024]
                                                .rearrange("p (gg t) -> p gg t", gg=8),
                                            in_=qkvT[pp0:pp0 + 64, :]
                                                .rearrange("p (e m) -> p e m", e=24)
                                                [:, ec0::3, hi * 128:(hi + 1) * 128]).then_inc(s, 16),
                                        reads=("qkvT",), writes=(("gst", par),))
                            b.op("ve", lambda E, s, dst=dst: E.tensor_copy(
                                    dst[0:64, :].rearrange("p (t gg par) -> p gg par t", gg=8, par=2),
                                    gst[0:64, :].rearrange("p (par gg t) -> p gg par t", par=2, gg=8)
                                    ).then_inc(s, 1),
                                 reads=(("gst", 0), ("gst", 1)), writes=((mat,),))
                        b.dmaop("tp", lambda E, s: E.dma_start(
                                    out=Vrow[:].rearrange("p (t q) -> p t q", t=16),
                                    in_=Vh[0:64, :].rearrange("p (t q) -> p t q", t=16),
                                    transpose=True).then_inc(s, 16),
                                reads=(("v",),), writes=("Vrow",))
                        b.op("ve", lambda E, s: E.tensor_copy(
                                Vaug[:].rearrange("p (t q) -> p t q", q=65)[:, :, 0:64],
                                Vrow[:].rearrange("p (t q) -> p t q", q=64)).then_inc(s, 1),
                             reads=("Vrow",), writes=("Vaug",))
                        b.op("ve", lambda E, s: E.memset(
                                Vaug[:].rearrange("p (t q) -> p t q", q=65)[:, :, 64:65],
                                1.0).then_inc(s, 1),
                             reads=(), writes=("Vaug",))
                        for qt in range(4):
                            nkc = 4 * qt + 4
                            for kc in range(nkc):
                                b.op("te", lambda E, s, kc=kc, qt=qt: E.matmul(
                                        ps_m[kc % 2][:, 0:512],
                                        Kh[0:64, kc * 128:(kc + 1) * 128],
                                        Qh[0:64, qt * 512:(qt + 1) * 512],
                                        start=True, stop=True).then_inc(s, 1),
                                     reads=(("q",), ("k",)), writes=(("ps_m", kc % 2),))
                                b.op("se", lambda E, s, kc=kc, qt=qt: E.activation(
                                        out=Pbuf[:, kc * 512:(kc + 1) * 512],
                                        in_=ps_m[kc % 2][:, 0:512], func=ACT.Exp,
                                        bias=c_m4[:, 0:1], scale=SCL).then_inc(s, 1),
                                     reads=(("ps_m", kc % 2), "m4"), writes=("Pbuf",))
                            for j in range(4):
                                kc = 4 * qt + j
                                b.op("ve", lambda E, s, kc=kc, j=j: E.tensor_tensor(
                                        Pbuf[:, kc * 512 + j * 128: kc * 512 + (j + 1) * 128],
                                        Pbuf[:, kc * 512 + j * 128: kc * 512 + (j + 1) * 128],
                                        c_tri[:], ALU.mult).then_inc(s, 1),
                                     reads=("Pbuf", "tri"), writes=("Pbuf",))
                                if j > 0:
                                    b.op("ve", lambda E, s, kc=kc, j=j: E.memset(
                                            Pbuf[:, kc * 512: kc * 512 + j * 128], 0.0).then_inc(s, 1),
                                         reads=(), writes=("Pbuf",))
                            def pvmm(E, s, qt=qt, nkc=nkc):
                                ins = None
                                for kc in range(nkc):
                                    ins = E.matmul(ps_h[qt % 2][0:65, 0:512],
                                                   Vaug[:, kc * 65:(kc + 1) * 65],
                                                   Pbuf[:, kc * 512:(kc + 1) * 512],
                                                   start=(kc == 0), stop=(kc == nkc - 1))
                                ins.then_inc(s, 1)
                            b.op("te", pvmm, reads=("Vaug", "Pbuf"), writes=(("ps_h", qt % 2),))
                            b.op("ve", lambda E, s, qt=qt: E.reciprocal(
                                    rsum[0:1, 0:512], ps_h[qt % 2][64:65, 0:512]).then_inc(s, 1),
                                 reads=(("ps_h", qt % 2),), writes=("rsum",))
                            b.op("te", lambda E, s: E.matmul(
                                    ps_bc[0][0:64, 0:512], c_ones_row[0:1, 0:64], rsum[0:1, 0:512],
                                    start=True, stop=True).then_inc(s, 1),
                                 reads=("rsum", "ones_row"), writes=("bc_r",))
                            b.op("se", lambda E, s, qt=qt: E.activation(
                                    out=ost[0:64, 0:512], in_=ps_h[qt % 2][0:64, 0:512],
                                    func=ACT.Identity).then_inc(s, 1),
                                 reads=(("ps_h", qt % 2),), writes=("ost",))
                            b.op("ve", lambda E, s, qt=qt: E.tensor_tensor(
                                    oh[0:64, qt * 512:(qt + 1) * 512], ost[0:64, 0:512],
                                    ps_bc[0][0:64, 0:512], ALU.mult).then_inc(s, 1),
                                 reads=("ost", "bc_r"), writes=("oh",))
                        b.op("ve", lambda E, s: E.tensor_copy(
                                ohg[0:64, :].rearrange("p (g t) -> p g t", g=16),
                                oh[0:64, :].rearrange("p (t g) -> p g t", g=16)).then_inc(s, 1),
                             reads=("oh",), writes=("ohg",))
                        for par in range(2):
                            b.dmaop("tp", lambda E, s, hi=hi, par=par: E.dma_start(
                                        out=aacc[par * 64:par * 64 + 64, :]
                                            .rearrange("p (e m) -> p e m", e=NCK)
                                            [:, :, hi * 128:(hi + 1) * 128],
                                        in_=ohg[0:64, :]
                                            .rearrange("p (gh par2 t) -> p gh par2 t", gh=8, par2=2)
                                            [:, :, par, :]).then_inc(s, 16),
                                    reads=("ohg",), writes=("aacc",))
                    for c in range(NCK):
                        b.op("ve", lambda E, s, c=c: E.tensor_tensor(
                                t_rt[:, c * ST:(c + 1) * ST], t_rt[:, c * ST:(c + 1) * ST],
                                aacc[:, c * ST:(c + 1) * ST], ALU.add).then_inc(s, 1),
                             reads=(("rt",), "aacc"), writes=(("rt",),))
                    b.dmaop("io", lambda E, s, m0=m0: E.dma_start(
                                out=d_rsp.rearrange("(c p) m -> p c m", p=128)
                                    [:, :, m0:m0 + ST],
                                in_=t_rt[:].rearrange("p (c m) -> p c m", c=NCK)).then_inc(s, 16),
                            reads=(("rt",),), writes=(("rspv",),))

        def spill_R(R):
            for c in range(NCK):
                b.dmaop("io", lambda E, s, c=c, R=R: E.dma_start(
                            out=d_rsp[c * 128:(c + 1) * 128, :],
                            in_=R[:, c * TOK:(c + 1) * TOK]).then_inc(s, 16),
                        reads=(("R", c),), writes=(("rsp", c),))
            b.op("ve", lambda E, s: E.engine_nop().then_inc(s, 1),
                 reads=tuple(("rsp", c) for c in range(NCK)), writes=("rspv",))

        def load_x(R):
            for c in range(NCK):
                b.dmaop("io", lambda E, s, c=c, R=R: E.dma_start(
                            out=R[:, c * TOK:(c + 1) * TOK],
                            in_=xT[c * 128:(c + 1) * 128, :]).then_inc(s, 16),
                        reads=(), writes=(("R", c),))

        if PH_SCAN:
            with ExitStack() as e1:
                R1 = e1.enter_context(nc.sbuf_tensor("R1", [128, NCK * TOK], F32))
                load_x(R1)
                if PH_L0FA:
                    l0ffn_phase(R1)
                scan_pre(R1)
            b.global_barrier()
            pk = scan_phase_a()
            b.global_barrier()
            scan_phase_b(pk)
            b.global_barrier()
            with ExitStack() as e2:
                R2 = e2.enter_context(nc.sbuf_tensor("R2", [128, NCK * TOK], F32))
                scan_post(R2)
                if PH_L0FB:
                    l0ffn_phase(R2)
                spill_R(R2)
            b.global_barrier()
        else:
            with ExitStack() as e1:
                R1 = e1.enter_context(nc.sbuf_tensor("R1", [128, NCK * TOK], F32))
                load_x(R1)
                if PH_L0FA:
                    l0ffn_phase(R1)
                spill_R(R1)
            b.global_barrier()

        if PH_L1FA:
            l1ffn_phase()
            b.global_barrier()
        if PH_ATTN:
            attn_phase()
            b.global_barrier()
        if PH_L1FB:
            l1ffn_phase()
            b.global_barrier()

        for c in range(NCK):
            b.dmaop("io", lambda E, s, c=c: E.dma_start(
                        out=out_T[c * 128:(c + 1) * 128, :],
                        in_=d_rsp[c * 128:(c + 1) * 128, :]).then_inc(s, 16),
                    reads=("rspv",), writes=(("out", c),))
        for c in range(NCK):
            b.raw_wait("sy", ("out", c))

        block = ctx.enter_context(nc.Block())
        b.emit_all(block, sems)
    return nc


def _group_mm(E, s, outap, pairs):
    n = len(pairs)
    for i, (l, r) in enumerate(pairs):
        ins = E.matmul(outap, l, r, start=(i == 0), stop=(i == n - 1))
    ins.then_inc(s, 1)




def _bernoulli_threefry(seed, p, n):
    rot = [[13, 15, 26, 6], [17, 29, 16, 24]]
    ks0, ks1 = np.uint32(0), np.uint32(seed)
    ks2 = np.uint32(ks0 ^ ks1 ^ np.uint32(0x1BD11BDA))
    i = np.arange(n, dtype=np.uint64)
    x0 = (i >> np.uint64(32)).astype(np.uint32)
    x1 = (i & np.uint64(0xFFFFFFFF)).astype(np.uint32)
    x0 = (x0 + ks0).astype(np.uint32)
    x1 = (x1 + ks1).astype(np.uint32)
    ks = [ks0, ks1, ks2]

    def rotl(x, r):
        r = np.uint32(r)
        return ((x << r) | (x >> np.uint32(32 - r))).astype(np.uint32)

    for d in range(5):
        for r in rot[d % 2]:
            x0 = (x0 + x1).astype(np.uint32)
            x1 = rotl(x1, r)
            x1 = (x1 ^ x0).astype(np.uint32)
        x0 = (x0 + ks[(d + 1) % 3]).astype(np.uint32)
        x1 = (x1 + ks[(d + 2) % 3] + np.uint32(d + 1)).astype(np.uint32)
    bits = x0 ^ x1
    u = ((bits >> np.uint32(9)) | np.uint32(0x3F800000)).view(np.float32) - np.float32(1.0)
    return u < np.float32(p)


def _prep_inputs(kw):
    f32 = np.float32
    x = np.asarray(kw["L0_attn_x"], f32)
    W0 = np.asarray(kw["L0_attn_W0"], f32)
    Wkv = np.asarray(kw["L0_attn_Wkv"], f32)
    bkv = np.asarray(kw["L0_attn_bkv"], f32)
    eta = np.asarray(kw["L0_attn_eta"], f32).reshape(1, 1)

    def pcol(v):
        return np.ascontiguousarray(np.asarray(v, f32).reshape(-1, 128).T)

    C = np.ascontiguousarray
    keep = _bernoulli_threefry(42, 0.8, B * SEQ * D).reshape(B, SEQ, D)
    maskv = (keep.astype(np.float32) * 1.25).astype(ml_dtypes.bfloat16)

    kq = np.arange(128)
    tri = (kq[:, None] <= kq[None, :]).astype(np.float32)
    tb = np.arange(128) // B
    msl_np = (tb[:, None] > tb[None, :]).astype(np.float32)

    common = {
        "attn_wT": C(np.asarray(kw["L0_ffn_attn_w"], f32).T),
        "next_wT": C(np.asarray(kw["L0_ffn_next_w"], f32).T),
        "ln_s": pcol(kw["L0_ffn_ln_scale"]), "ln_b": pcol(kw["L0_ffn_ln_bias"]),
        "attn_b": pcol(kw["L0_ffn_attn_b"]), "next_b": pcol(kw["L0_ffn_next_b"]),
        "wkv_vT": C(Wkv[D:, :].T), "bkv_row": C(bkv[D:].reshape(1, D)),
        "eta_in": eta,
        "qkv_k": C(np.asarray(kw["L1_attn_qkv_kernel"], f32)),
        "qkv_b": pcol(np.asarray(kw["L1_attn_qkv_bias"], f32).reshape(3 * D)),
        "W1T": C(np.asarray(kw["L1_ffn_W1"], f32).T),
        "W2T": C(np.asarray(kw["L1_ffn_W2"], f32).T),
        "l1b1": pcol(kw["L1_ffn_b1"]), "l1b2": pcol(kw["L1_ffn_b2"]),
        "ones_col": np.ones((128, 1), f32),
        "ones_colb": np.ones((128, 1), ml_dtypes.bfloat16),
        "ones_row": np.ones((1, 128), f32),
        "ones_rowb": np.ones((1, 128), ml_dtypes.bfloat16),
        "identb": np.eye(128, dtype=ml_dtypes.bfloat16),
        "msl": msl_np.astype(ml_dtypes.bfloat16),
        "msu": msl_np.T.astype(ml_dtypes.bfloat16),
        "trimask": tri.astype(ml_dtypes.bfloat16),
    }
    w1 = np.asarray(kw["L0_ffn_w1"], f32)
    w2 = np.asarray(kw["L0_ffn_w2"], f32)
    b1 = np.asarray(kw["L0_ffn_b1"], f32)
    b2 = np.asarray(kw["L0_ffn_b2"], f32)
    in_maps = []
    for i in range(NCORE):
        bpair = [i, i + 8]
        m = dict(common)
        m["xT"] = C(x[bpair].reshape(TOK, D).T)
        m["w1e"] = C(w1[i])
        m["w2e"] = C(w2[i])
        m["b1e"] = pcol(b1[i])
        m["b2e"] = pcol(b2[i])
        m["z0"] = C(W0[i * 128:(i + 1) * 128, :].T)
        m["maskT"] = C(maskv[bpair].reshape(TOK, D).T)
        in_maps.append(m)
    return in_maps


def kernel(**inputs):
    in_maps = _prep_inputs(inputs)
    nc = build_program()
    res = run_bass_kernel_spmd(nc, in_maps, core_ids=list(range(NCORE)))
    out = np.empty((B, SEQ, D), np.float32)
    for i in range(NCORE):
        oT = res.results[i]["outT"]
        out[[i, i + 8]] = np.asarray(oT).T.reshape(2, SEQ, D)
    return out
